# version 01
# baseline (speedup 1.0000x reference)
import numpy as np
import ml_dtypes
from contextlib import ExitStack
import concourse.bass as bass
import concourse.mybir as mybir
from concourse.bass_utils import run_bass_kernel_spmd

F32 = mybir.dt.float32
BF16 = mybir.dt.bfloat16
AF = mybir.ActivationFunctionType
ALU = mybir.AluOpType
AX = mybir.AxisListType
I32 = mybir.dt.int32

EPS = 1e-5
T = 4096
D = 2048
TOK = 1024
NCH = 32
QL = 128


def had(n):
    h = np.array([[1.0]], np.float32)
    while h.shape[0] < n:
        h = np.block([[h, h], [h, -h]])
    return h


class Ctr:
    def __init__(self, ctx, nc, name):
        self.sem = ctx.enter_context(nc.semaphore(name))
        self.n = 0

    def inc(self, k=1):
        self.n += k
        return self.n


class Build:

    def __init__(self, dbg=()):
        self.dbg = set(dbg)
        self.nc = bass.Bass("TRN2", target_bir_lowering=False)
        self.ops = {k: [] for k in "gvaty"}

    def em(self, eng, fn):
        self.ops[eng].append(fn)

    def dma(self, eng, out, in_, ctr=None, **kw):
        c = ctr if ctr is not None else {"g": self.dG, "y": self.dS, "a": self.dA}[eng]
        tgt = c.inc(16)
        self.em(eng, lambda e, o=out, i=in_, k=dict(kw): e.dma_start(out=o, in_=i, **k).then_inc(c.sem, 16))
        return tgt

    def wait(self, eng, ctr, tgt):
        if tgt > 0:
            self.em(eng, lambda e, t=tgt: e.wait_ge(ctr.sem, t))

    def op(self, eng, fn):
        c = {"v": self.sV, "a": self.sA, "t": self.sP, "g": self.sG}[eng]
        tgt = c.inc(1)
        self.em(eng, lambda e, f=fn: f(e).then_inc(c.sem, 1))
        return tgt

    def dop(self, eng, fn):
        c = {"v": self.sV, "a": self.sA, "t": self.sP, "g": self.sG}[eng]
        self.wait(eng, c, c.n)
        return self.op(eng, fn)

    def barrier(self, name=""):
        snap = [(c, c.n) for c in self.ctrs]
        for eng in "gvaty":
            for c, n in snap:
                self.wait(eng, c, n)

    def dbg_copy(self, name, src_ap, shape, dtype):
        if name not in self.dbg:
            return
        t = self.nc.dram_tensor("dbg_" + name, shape, dtype, kind="ExternalOutput").ap()
        self.dma("y", t, src_ap, ctr=self.STO)

    def declare(self):
        nc = self.nc
        dr = lambda n, s, d, k: nc.dram_tensor(n, s, d, kind=k).ap()
        I, O = "ExternalInput", "ExternalOutput"
        self.xb = dr("xb", [T, D], F32, I)
        self.x_own = dr("x_own", [TOK, D], F32, I)
        self.w_in_s = dr("w_in_s", [D, 2320], F32, I)
        self.conv_w_s = dr("conv_w_s", [128, 10, 4], F32, I)
        self.conv_b_s = dr("conv_b_s", [128, 10], F32, I)
        self.dtb = dr("dtb", [16, 1], F32, I)
        self.alog = dr("alog", [16, 1], F32, I)
        self.dbigI = dr("dbigI", [128, 8], F32, I)
        self.ln1wI = dr("ln1wI", [128, 16], F32, I)
        self.ln2wI = dr("ln2wI", [128, 16], F32, I)
        self.gnwI = dr("gnwI", [128, 8], F32, I)
        self.wout_s = dr("wout_s", [1024, D], F32, I)
        self.gate_w = dr("gate_w", [D, 8192], F32, I)
        self.up_w = dr("up_w", [D, 8192], F32, I)
        self.down_w = dr("down_w", [8192, D], F32, I)
        self.wup = dr("wup", [8192, D], F32, I)
        self.wgate = dr("wgate", [8192, D], F32, I)
        self.wdown = dr("wdown", [D, 8192], F32, I)
        self.su_ugI = dr("su_ugI", [128, 16], F32, I)
        self.su_dpI = dr("su_dpI", [128, 64], F32, I)
        self.su1I = dr("su1I", [128, 64], F32, I)
        self.su2I = dr("su2I", [128, 64], F32, I)
        self.sdbigI = dr("sdbigI", [128, D], F32, I)
        self.h128I = dr("h128I", [128, 128], F32, I)
        self.bd16I = dr("bd16I", [128, 128], F32, I)
        self.bd64I = dr("bd64I", [128, 128], F32, I)
        self.identI = dr("identI", [128, 128], F32, I)
        self.utI = dr("utI", [128, 128], F32, I)
        self.negmaskI = dr("negmaskI", [128, 128], F32, I)
        self.icol32I = dr("icol32I", [32, 32], F32, I)
        self.hsel128I = dr("hsel128I", [16, 2048], F32, I)
        self.outT = dr("out", [TOK, D], F32, O)

        dint = lambda n, s, d: nc.dram_tensor(n, s, d, kind="Internal")
        self.xrscr = dint("xrscr", [T, D], BF16)
        self.statdram = dint("statdram", [32, 128, 16], F32)
        self.dtdram = dint("dtdram", [16, T], F32)
        self.zscr = dint("zscr", [128, 8, T], BF16)
        self.cumdram = dint("cumdram", [16, T], F32)
        self.gamdram = dint("gamdram", [16, T], F32)
        self.rs_in = dint("rs_in", [T, D], F32)
        self.rs_out = dint("rs_out", [TOK, D], F32)
        self.ar_in = dint("ar_in", [1, T], F32)
        self.ar_out = dint("ar_out", [1, T], F32)
        self.wupT = dint("wupT", [D, 8192], BF16)
        self.wgateT = dint("wgateT", [D, 8192], BF16)
        self.wdownT = dint("wdownT", [8192, D], BF16)
        self.wupbf = dint("wupbf", [8192, D], BF16)
        self.wgatebf = dint("wgatebf", [8192, D], BF16)
        self.wdownbf = dint("wdownbf", [D, 8192], BF16)
        self.h2nscr = dint("h2nscr", [TOK, D], BF16)
        self.xnownscr = dint("xnownscr", [TOK, D], BF16)
        self.s1scr = dint("s1scr", [D, TOK], BF16)
        self.s2scr = dint("s2scr", [D, TOK], BF16)
        self.s3scr = dint("s3scr", [8192, TOK], BF16)
        self.s4scr = dint("s4scr", [8192, TOK], BF16)

    def alloc(self, ctx):
        nc = self.nc
        E = ctx.enter_context
        sb = lambda n, s, d: E(nc.sbuf_tensor(n, s, d))
        self.STAT = sb("STAT", [128, 16], F32)
        self.EPSC = sb("EPSC", [128, 1], F32)
        self.ONE1C = sb("ONE1C", [128, 1], F32)
        self.ARE = sb("ARE", [128, 38912], BF16)
        self.BIG = sb("BIG", [128, 32768], BF16)
        self.MID = sb("MID", [128, 16384], BF16)
        self.XRS = sb("XRS", [128, 16, 512], BF16)
        self.Bt = sb("Bt", [128, T], BF16)
        self.Ct = sb("Ct", [128, T], BF16)
        self.H128 = sb("H128", [128, 128], BF16)
        self.BD16 = sb("BD16", [128, 128], BF16)
        self.BD64 = sb("BD64", [128, 128], BF16)
        self.IDB = sb("IDB", [128, 128], BF16)
        self.UT = sb("UT", [128, 128], F32)
        self.NEG = sb("NEG", [128, 128], F32)
        self.ICOL = sb("ICOL", [32, 32], F32)
        self.ONES32 = sb("ONES32", [32, 128], F32)
        self.ONESC = sb("ONESC", [128, 1], F32)
        self.ONESR = sb("ONESR", [1, 128], F32)
        self.CONVB = sb("CONVB", [128, 10], F32)
        self.LN1W = sb("LN1W", [128, 16], F32)
        self.LN2W = sb("LN2W", [128, 16], F32)
        self.DBIG = sb("DBIG", [128, 8], F32)
        self.DTB = sb("DTB", [16, 1], F32)
        self.ACOL = sb("ACOL", [16, 1], F32)
        self.ZCOL = sb("ZCOL", [16, 1], F32)
        self.SUU = sb("SUU", [128, 16], F32)
        self.AT = sb("AT", [32, 16], F32)
        self.ALPHAS = sb("ALPHAS", [128, 16], F32)

        A = self.ARE
        self.WIN16 = A[:, 0:16 * 2320].rearrange("p (k m) -> p k m", k=16)
        self.DT16 = A[0:16, 0:8192].bitcast(F32)
        self.LNDT = A[0:16, 8192:16384].bitcast(F32)
        self.CUM = A[0:16, 16384:24576].bitcast(F32)
        self.NB = A[0:16, 24576:32768].bitcast(F32)
        self.E2 = A[:, 0:2048]
        self.MT = A[:, 2048:4096]
        self.WTl = A[:, 4096:6144]
        self.GBC = A[:, 6144:8192]
        self.XTM = A[:, 8192:10240].rearrange("p (b n) -> p b n", b=2)
        self.BTM = A[:, 10240:10496].rearrange("p (b n) -> p b n", b=2)
        self.SZ = A[:, 10496:14592].rearrange("p (b j n) -> p b j n", b=2, j=8)
        self.CBTM = A[:, 14592:14720]
        self.XDD = A[:, 14720:15744]
        self.D2 = A[:, 16384:20480].bitcast(F32)
        self.O1 = A[:, 20480:22528].bitcast(F32)
        self.Zst = A[:, 22528:24576].bitcast(F32)
        self.RHS48 = A[0:48, 33280:37376].bitcast(F32)
        self.LHS48 = A[0:48, 37376:37632].bitcast(F32)
        XF = self.XRS.rearrange("p a b -> p (a b)")
        self.GNW = XF[:, 0:16].bitcast(F32)
        self.SUD = XF[:, 16:144].bitcast(F32)
        self.SU1 = XF[:, 144:272].bitcast(F32)
        self.SU2 = XF[:, 272:400].bitcast(F32)
        self.EVST = A[:, 0:8192].bitcast(F32).rearrange("p (b n) -> p b n", b=2)
        self.RGBC = A[:, 8192:12288]
        self.SQS = A[:, 12288:20480].bitcast(F32)
        self.SDB = A[:, 20480:24576].bitcast(F32)
        self.ROWA = A[0:1, 24576:32768].bitcast(F32)
        self.ROWB = A[0:1, 32768:36864].bitcast(F32)
        self.H2NS = A[:, 36864:38912].rearrange("p (b n) -> p b n", b=2)
        self.X2A = A[:, 8192:24576].rearrange("p (k m) -> p k m", k=64)
        self.WKT8 = A[:, 24576:32768]
        self.WKD = A[:, 32768:36864].rearrange("p (b n) -> p b n", b=2)
        self.XH3 = A[:, 0:4096].rearrange("p (k m) -> p k m", k=16)
        self.WUPT = A[:, 8192:24576].rearrange("p (k m) -> p k m", k=16)
        self.XU = A[:, 24576:38912 - 2048].rearrange("p (k m) -> p k m", k=48)

        self.XS = self.BIG.rearrange("p (j t) -> p j t", j=8)
        self.OUT = self.BIG[:, 0:32768].bitcast(F32).rearrange("p (j t) -> p j t", j=8)
        self.XTI = self.MID[:, 0:8192].bitcast(F32).rearrange("p (b n) -> p b n", b=2)
        self.XRO = self.MID[:, 8192:16384].rearrange("p (b n) -> p b n", b=4)
        self.CONVD = self.MID[:, 0:5120].rearrange("p (k m) -> p k m", k=40)
        self.DTST = self.MID[0:16, 5120:6144].bitcast(F32)
        self.CS = self.MID[:, 6144:11344].rearrange("p (a b) -> p a b", a=10)
        self.SST = self.MID[:, 11344:13392].rearrange("p (b n) -> p b n", b=4)
        self.SQ1 = self.XRS.rearrange("p a b -> p (a b)")[:, 0:4096].bitcast(F32)
        self.CWPARK = self.Bt[:, 0:80].bitcast(F32).rearrange("p (a b) -> p a b", a=10)
        self.XNOWN = self.MID[:, 0:16384].rearrange("p (j t) -> p j t", j=8)

        ps = lambda n, s, d: E(nc.psum_tensor(n, s, d))
        self.pA = ps("pA", [128, 2048], F32)
        self.pY = ps("pY", [128, 1024], F32)
        self.pXT = ps("pXT", [128, 1024], BF16)
        self.pS = ps("pS", [128, 512], F32)
        self.pB = self.pS[:, 0:128]
        self.pBT = self.pS[:, 128:192].bitcast(BF16)
        self.pAL = self.pS[:, 192:208]

        self.CSTy = Ctr(ctx, nc, "CSTy")
        self.CSTg = Ctr(ctx, nc, "CSTg")
        self.WLD = Ctr(ctx, nc, "WLD")
        self.XL = [Ctr(ctx, nc, f"XL{i}") for i in range(2)]
        self.XST = [Ctr(ctx, nc, f"XST{i}") for i in range(4)]
        self.RHT = Ctr(ctx, nc, "RHT")
        self.ZSTc = [Ctr(ctx, nc, f"ZST{i}") for i in range(4)]
        self.DTT = Ctr(ctx, nc, "DTT")
        self.STO = Ctr(ctx, nc, "STO")
        self.PRP = Ctr(ctx, nc, "PRP")
        self.FIN = Ctr(ctx, nc, "FIN")
        self.WCST = Ctr(ctx, nc, "WCST")
        self.WTA = Ctr(ctx, nc, "WTA")
        self.WTB = Ctr(ctx, nc, "WTB")
        self.PR2 = Ctr(ctx, nc, "PR2")
        self.RSQ = [Ctr(ctx, nc, f"RSQ{i}") for i in range(2)]
        self.WOBc = [Ctr(ctx, nc, f"WOB{i}") for i in range(2)]
        self.XOL = [Ctr(ctx, nc, f"XOL{i}") for i in range(2)]
        self.RSL = [Ctr(ctx, nc, f"RSL{i}") for i in range(2)]
        self.H2T = [Ctr(ctx, nc, f"H2T{i}") for i in range(2)]
        self.H2L = Ctr(ctx, nc, "H2L")
        self.WBL = [Ctr(ctx, nc, f"WBL{i}") for i in range(2)]
        self.DRL = [Ctr(ctx, nc, "DRL0")]
        self.DRL2 = Ctr(ctx, nc, "DRL2")
        self.SH1 = [Ctr(ctx, nc, f"SH1{i}") for i in range(2)]
        self.SH2 = [Ctr(ctx, nc, f"SH2{i}") for i in range(2)]
        self.GBL = [Ctr(ctx, nc, f"GBL{i}") for i in range(2)]
        self.SZL = [Ctr(ctx, nc, f"SZL{i}") for i in range(2)]
        self.dG = Ctr(ctx, nc, "dG")
        self.dS = Ctr(ctx, nc, "dS")
        self.dA = Ctr(ctx, nc, "dA")
        self.cc = Ctr(ctx, nc, "cc")
        self.sV = Ctr(ctx, nc, "sV")
        self.sA = Ctr(ctx, nc, "sA")
        self.sP = Ctr(ctx, nc, "sP")
        self.sG = Ctr(ctx, nc, "sG")
        self.ctrs = ([self.dG, self.dS, self.dA, self.cc, self.sV, self.sA, self.sP, self.sG,
                      self.CSTy, self.CSTg, self.WLD, self.RHT, self.DTT, self.STO,
                      self.PRP, self.FIN, self.WCST, self.WTA, self.WTB, self.PR2, self.H2L]
                     + self.XL + self.XST + self.ZSTc + self.GBL + self.SZL
                     + self.RSQ + self.WOBc + self.XOL + self.RSL + self.H2T
                     + self.WBL + self.DRL + [self.DRL2] + self.SH1 + self.SH2)

    def p0(self):
        for dst, src in [(self.H128, self.h128I), (self.BD16, self.bd16I),
                         (self.BD64, self.bd64I), (self.IDB, self.identI)]:
            self.dma("g", dst[:, :], src, ctr=self.CSTg)
        for dst, src in [(self.UT, self.utI),
                         (self.NEG, self.negmaskI), (self.ICOL, self.icol32I),
                         (self.CONVB, self.conv_b_s), (self.LN1W, self.ln1wI),
                         (self.LN2W, self.ln2wI),
                         (self.DBIG, self.dbigI), (self.DTB, self.dtb),
                         (self.SUU, self.su_ugI),
                         (self.ACOL, self.alog)]:
            self.dma("y", dst[:, :], src, ctr=self.CSTy)
        cwt = self.dma("y", self.CWPARK[:, :, :], self.conv_w_s, ctr=self.CSTy)
        gld, sld = self.CSTg.n, self.CSTy.n
        for eng in "va":
            self.wait(eng, self.CSTg, gld)
            self.wait(eng, self.CSTy, sld)
        self.op("v", lambda e: e.memset(self.ONES32[:, :], 1.0))
        self.op("v", lambda e: e.memset(self.ONESC[:, :], 1.0))
        self.op("v", lambda e: e.memset(self.ONESR[:, :], 1.0))
        self.op("v", lambda e: e.memset(self.ZCOL[:, :], 0.0))
        self.op("v", lambda e: e.memset(self.EPSC[:, :], EPS))
        self.op("v", lambda e: e.memset(self.ONE1C[:, :], 1.0))
        self.op("v", lambda e: e.tensor_scalar_mul(self.SUU[:, :], self.SUU[:, :], 1.0 / np.sqrt(2048.0)))
        ae = self.op("a", lambda e: e.activation(self.ACOL[:, :], self.ACOL[:, :], AF.Exp))
        self.wait("v", self.sA, ae)
        self.op("v", lambda e: e.tensor_scalar_mul(self.ACOL[:, :], self.ACOL[:, :], -1.0))
        self.ev_p0_v = self.sV.n
        self.ev_p0_a = self.sA.n

    def p1(self):
        xb = self.xb.rearrange("(n p) d -> n p d", p=128)
        xrw = self.xrscr.ap().rearrange("(n p) d -> n p d", p=128)
        S = self.STAT
        self.xr_tile_done = {}
        xro_free = {}
        for eng in "va":
            self.wait(eng, self.sV, self.ev_p0_v)
        for i in range(getattr(self, 'p1_tiles', 32)):
            b = i % 2
            ob = i % 4
            if i >= 2:
                self.wait("y", self.sV, self.xti_used[i - 2])
            ld = self.dma("y", self.XTI[:, b, :], xb[i], ctr=self.XL[b])
            X = self.XTI[:, b, :]
            self.wait("v", self.XL[b], ld)
            self.wait("a", self.XL[b], ld)
            SQ1 = self.SQ1[:, 0:2048]
            self.op("v", lambda e, X=X: e.tensor_reduce(S[:, 0:1], X, AX.X, ALU.add))
            sq = self.op("a", lambda e, X=X, SQ1=SQ1: e.activation(SQ1, X, AF.Square))
            self.wait("v", self.sA, sq)
            self.op("v", lambda e, SQ1=SQ1: e.tensor_reduce(S[:, 1:2], SQ1, AX.X, ALU.add))
            self.dop("v", lambda e: e.tensor_scalar_mul(S[:, 2:3], S[:, 0:1], 1.0 / D))
            self.op("v", lambda e: e.tensor_scalar_mul(S[:, 3:4], S[:, 1:2], 1.0 / D))
            self.dop("v", lambda e: e.tensor_tensor(S[:, 4:5], S[:, 2:3], S[:, 2:3], ALU.mult))
            self.dop("v", lambda e: e.tensor_tensor(S[:, 4:5], S[:, 3:4], S[:, 4:5], ALU.subtract))
            self.dop("v", lambda e: e.tensor_scalar_add(S[:, 5:6], S[:, 4:5], EPS))
            vrec = self.dop("v", lambda e: e.reciprocal(S[:, 6:7], S[:, 5:6]))
            self.wait("a", self.sV, vrec)
            self.op("a", lambda e: e.activation(S[:, 7:8], S[:, 5:6], AF.Ln))
            r1 = self.dop("a", lambda e: e.activation(S[:, 7:8], S[:, 7:8], AF.Exp, scale=-0.5))
            self.wait("v", self.sA, r1)
            m2n = self.dop("v", lambda e: e.tensor_tensor(S[:, 8:9], S[:, 4:5], S[:, 6:7], ALU.mult))
            self.wait("a", self.sV, m2n)
            self.op("a", lambda e: e.activation(S[:, 9:10], S[:, 8:9], AF.Ln, bias=self.EPSC[:, 0:1]))
            r2 = self.dop("a", lambda e: e.activation(S[:, 9:10], S[:, 9:10], AF.Exp, scale=-0.5))
            self.wait("v", self.sA, r2)
            self.dop("v", lambda e: e.tensor_tensor(S[:, 10:11], S[:, 7:8], S[:, 9:10], ALU.mult))
            if i >= 4:
                self.wait("v", self.XST[ob], self.xr_tile_done[i - 4])
            xr_t = self.dop("v", lambda e, X=X, ob=ob: e.tensor_scalar(
                self.XRO[:, ob, :], X, S[:, 2:3], S[:, 10:11], ALU.subtract, ALU.mult))
            self.xti_used = getattr(self, "xti_used", {})
            self.xti_used[i] = xr_t
            self.wait("y", self.sV, xr_t)
            self.xr_tile_done[i] = self.dma("y", xrw[i], self.XRO[:, ob, :], ctr=self.XST[ob])
            if getattr(self, "p1_serial", False):
                self.barrier()
        self.ev_p1_done = [(c, c.n) for c in self.XST]

    def p2a(self):
        for k in range(16):
            self.dma("g", self.WIN16[:, k, :], self.w_in_s[k * 128:(k + 1) * 128, :], ctr=self.WLD)
        self.wait("v", self.WLD, self.WLD.n)
        for k in range(16):
            self.op("v", lambda e, k=k: e.tensor_scalar_mul(
                self.WIN16[:, k, :], self.WIN16[:, k, :], self.LN1W[:, k:k + 1]))
        for c, n in self.ev_p1_done:
            self.wait("v", c, n)
        cw = self.CWPARK
        for j in range(10):
            for k in range(4):
                self.op("v", lambda e, j=j, k=k: e.tensor_scalar_mul(
                    self.CONVD[:, j * 4 + k, :], self.IDB[:, :], cw[:, j, k:k + 1]))
        self.ev_win = self.sV.n
        self.wait("t", self.sV, self.ev_win)

        slot_ap = [self.pA[:, 0:512], self.pA[:, 512:1024], self.pA[:, 1024:1536],
                   self.pA[:, 1536:2048], self.pY[:, 0:512], self.pY[:, 512:1024]]
        slot_busy = {}
        xrv = self.xrscr.ap().rearrange("(c t) d -> c t d", t=512)
        zv = self.zscr.ap()
        dt_ev = []
        self.conv_done_v = 0

        for tc in range(8):
            rhs_t = []
            if tc == 0:
                for c, n in self.ev_p1_done:
                    self.wait("y", c, n)
            if tc >= 1:
                self.wait("y", self.sP, self.rhs_used)
            for k in range(16):
                t = self.dma("y", self.XRS[:, k, :],
                             self.xrscr.ap()[tc * 512:(tc + 1) * 512, k * 128:(k + 1) * 128],
                             transpose=True, ctr=self.RHT)
                rhs_t.append(t)
            self.wait("t", self.RHT, rhs_t[-1])
            if tc == 0:
                hz = self.op("v", lambda e: e.memset(self.CS[:, :, 0:3], 0.0))
            else:
                self.wait("v", self.sA, self.cs_written_a)
                self.wait("v", self.sV, self.cs_written_v)
                hz = self.op("v", lambda e: e.tensor_copy(self.CS[:, :, 0:3], self.CS[:, :, 512:515]))
            self.wait("t", self.sV, hz)

            for m in range(19):
                sl = m % 6
                if sl in slot_busy:
                    eng_c, tgt = slot_busy[sl]
                    self.wait("t", eng_c, tgt)
                ncols = 16 if m == 18 else 128
                dst = slot_ap[sl][0:ncols, :]
                for k in range(16):
                    mm = lambda e, k=k, m=m, dst=dst, ncols=ncols: e.matmul(
                        dst, self.WIN16[:, k, m * 128:m * 128 + ncols],
                        self.XRS[:, k, :], start=(k == 0), stop=(k == 15))
                    if k < 15:
                        self.em("t", mm)
                    else:
                        mm_t = self.op("t", mm)
                if m < 8:
                    zb = m % 4
                    if (tc, m) not in [(0, 0), (0, 1), (0, 2), (0, 3)]:
                        self.wait("a", self.ZSTc[zb], self.zst_free.get(zb, 0))
                    self.wait("a", self.sP, mm_t)
                    ev = self.op("a", lambda e, sl=sl, zb=zb: e.activation(
                        self.SST[:, zb, :], slot_ap[sl][:, :], AF.Silu))
                    slot_busy[sl] = (self.sA, ev)
                    self.wait("y", self.sA, ev)
                    st = self.dma("y", zv[:, m, tc * 512:(tc + 1) * 512], self.SST[:, zb, :], ctr=self.ZSTc[zb])
                    self.zst_free = getattr(self, "zst_free", {})
                    self.zst_free[zb] = st
                elif m < 18:
                    self.wait("v", self.sP, mm_t)
                    ev = self.op("v", lambda e, sl=sl, j=m - 8: e.tensor_copy(
                        self.CS[:, j, 3:515], slot_ap[sl][:, :]))
                    slot_busy[sl] = (self.sV, ev)
                else:
                    if tc >= 1:
                        self.wait("v", self.DTT, self.dtst_free)
                    self.wait("v", self.sP, mm_t)
                    ev = self.op("v", lambda e, sl=sl: e.tensor_copy(
                        self.DTST[:, :], slot_ap[sl][0:16, :]))
                    slot_busy[sl] = (self.sV, ev)
                    self.wait("y", self.sV, ev)
                    self.dtst_free = self.dma("y", self.dtdram.ap()[:, tc * 512:(tc + 1) * 512], self.DTST[:, :], ctr=self.DTT)
                    dt_ev.append(ev)
            self.rhs_used = self.sP.n
            ev_all_v = self.sV.n
            ev_all_a = self.sA.n
            self.wait("t", self.sV, ev_all_v)
            self.wait("t", self.sA, ev_all_a)
            conv_ev_v, conv_ev_a = [], []
            for j in range(10):
                sl = 2 + (j % 2)
                eng_c, tgt = slot_busy.get(sl, (None, 0))
                if eng_c is not None:
                    self.wait("t", eng_c, tgt)
                for k in range(4):
                    mm = lambda e, j=j, k=k, sl=sl: e.matmul(
                        slot_ap[sl][:, :], self.CONVD[:, j * 4 + k, :],
                        self.CS[:, j, k:k + 512], start=(k == 0), stop=(k == 3))
                    if k < 3:
                        self.em("t", mm)
                    else:
                        mm_t = self.op("t", mm)
                self.wait("a", self.sP, mm_t)
                if j < 8:
                    dst = self.XS[:, j, tc * 512:(tc + 1) * 512]
                elif j == 8:
                    dst = self.Bt[:, tc * 512:(tc + 1) * 512]
                else:
                    dst = self.Ct[:, tc * 512:(tc + 1) * 512]
                ev = self.op("a", lambda e, sl=sl, j=j, dst=dst: e.activation(
                    dst, slot_ap[sl][:, :], AF.Silu, bias=self.CONVB[:, j:j + 1]))
                slot_busy[sl] = (self.sA, ev)
                conv_ev_a.append(ev)
            self.cs_written_v = ev_all_v
            self.cs_written_a = self.sA.n
        self.ev_p2a_mm = self.sP.n
        self.ev_p2a_conv = self.sA.n

        if self.dbg:
            self.barrier()
            self.dbg_copy("xs", self.XS[:, :, :], [128, 8, T], BF16)
            self.dbg_copy("B", self.Bt[:, :], [128, T], BF16)
            self.dbg_copy("C", self.Ct[:, :], [128, T], BF16)
            self.dbg_copy("dtraw", self.dtdram.ap(), [16, T], F32)
            self.dbg_copy("sz", self.zscr.ap(), [128, 8, T], BF16)
            self.dbg_copy("xr", self.xrscr.ap(), [T, D], BF16)
            self.dbg_copy("stats", self.statdram.ap(), [32, 128, 16], F32)

    def p2prep(self):
        M = self.MID
        self.ZB = M[:, 0:1024]
        self.GBC2 = M[:, 1024:5120].rearrange("p (b n) -> p b n", b=2)
        self.SELCB = M[:, 5120:5248]
        self.ATB = M[0:32, 5248:5264]
        self.ABUF = M[0:16, 5264:5296]
        self.ALPH32 = M[:, 5296:6320].bitcast(F32)
        self.BTMALL = M[:, 6320:10416]
        self.ONES32B = M[:, 10416:10544]
        self.pAL32 = self.pS[:, 0:512]

        DT, LN_, CU, NB = self.DT16, self.LNDT, self.CUM, self.NB
        self.wait("v", self.sP, self.ev_p2a_mm)
        self.wait("a", self.sP, self.ev_p2a_mm)
        self.wait("v", self.sA, self.ev_p2a_conv)
        self.wait("y", self.DTT, self.DTT.n)
        t_dt = self.dma("y", DT[:, :], self.dtdram.ap(), ctr=self.PRP)
        self.wait("y", self.sP, self.ev_p2a_mm)
        self.dma("y", self.RHS48[32:48, :], self.hsel128I, ctr=self.PRP)
        self.dma("y", self.GNW[:, :], self.gnwI, ctr=self.PRP)
        self.dma("y", self.SUD[:, :], self.su_dpI, ctr=self.PRP)
        self.dma("y", self.SU1[:, :], self.su1I, ctr=self.PRP)
        t_xtra = self.dma("y", self.SU2[:, :], self.su2I, ctr=self.PRP)
        self.wait("v", self.PRP, t_dt)
        self.wait("a", self.PRP, t_dt)
        self.dop("v", lambda e: e.tensor_scalar_add(DT[:, :], DT[:, :], self.DTB[:, 0:1]))
        self.dop("v", lambda e: e.tensor_scalar_max(CU[:, :], DT[:, :], 0.0))
        A_ = self.ARE
        LNI = A_[0:16, 8192:16384].bitcast(I32)
        DTI = A_[0:16, 0:8192].bitcast(I32)
        ab = self.op("v", lambda e: e.tensor_single_scalar(LNI, DTI, 0x7fffffff, ALU.bitwise_and))
        self.wait("a", self.sV, ab)
        self.op("a", lambda e: e.activation(LN_[:, :], LN_[:, :], AF.Exp, scale=-1.0))
        l1 = self.dop("a", lambda e: e.activation(LN_[:, :], LN_[:, :], AF.Ln, bias=self.ONE1C[0:16, 0:1]))
        self.wait("v", self.sA, l1)
        self.dop("v", lambda e: e.tensor_tensor(DT[:, :], CU[:, :], LN_[:, :], ALU.add))
        dt_done = self.sV.n
        self.wait("a", self.sV, dt_done)
        lndt0 = self.op("a", lambda e: e.activation(LN_[:, :], DT[:, :], AF.Ln))
        self.wait("v", self.sA, lndt0)
        lndt = self.dop("v", lambda e: e.tensor_scalar_max(LN_[:, :], LN_[:, :], -60.0))
        self.dop("v", lambda e: e.tensor_scalar_mul(CU[:, :], DT[:, :], self.ACOL[:, 0:1]))
        self.op("v", lambda e: e.memset(NB[:, :], 1.0))
        self.dop("v", lambda e: e.tensor_tensor_scan(DT[:, :], NB[:, :], CU[:, :], 0.0, ALU.mult, ALU.add))
        for c in range(NCH):
            col = self.ZCOL[:, 0:1] if c == 0 else DT[:, c * QL - 1:c * QL]
            fn = lambda e, c=c, col=col: e.tensor_scalar(
                CU[:, c * QL:(c + 1) * QL], DT[:, c * QL:(c + 1) * QL], col, None,
                ALU.subtract, ALU.bypass)
            if c == 0:
                self.dop("v", fn)
            else:
                self.op("v", fn)
        nb_t = self.dop("v", lambda e: e.tensor_tensor(NB[:, :], LN_[:, :], CU[:, :], ALU.subtract))
        self.wait("y", self.sV, nb_t)
        st_c = self.dma("y", self.cumdram.ap(), CU[:, :], ctr=self.PRP)
        self.wait("y", self.PRP, st_c)
        cf_t = self.dma("y", self.RHS48[0:32, :].rearrange("p (h t) -> p h t", h=16),
                        self.cumdram.ap().rearrange("h (c t) -> c h t", c=32), ctr=self.PRP)
        self.wait("a", self.PRP, st_c)
        self.wait("a", self.sV, nb_t)
        g_a = self.dop("a", lambda e: e.activation(CU[:, :], CU[:, :], AF.Exp))
        self.wait("y", self.sA, g_a)
        st_g = self.dma("y", self.gamdram.ap(), CU[:, :], ctr=self.PRP)
        self.wait("v", self.sA, g_a)
        self.op("v", lambda e: e.memset(self.ONES32B[:, :], 1.0))
        abf = self.dop("v", lambda e: e.tensor_copy(
            self.ABUF[:, :], CU[:, QL - 1::QL]))
        self.wait("t", self.sV, abf)
        tr = self.op("t", lambda e: e.transpose(self.pBT[0:32, 0:16], self.ABUF[:, :], self.IDB[0:16, 0:16]))
        self.wait("v", self.sP, tr)
        atb = self.dop("v", lambda e: e.tensor_copy(self.ATB[:, :], self.pBT[0:32, 0:16]))
        sel_ts = []
        for c in range(NCH):
            s_t = self.dop("v", lambda e, c=c: e.tensor_scalar_mul(
                self.SELCB[0:32, :], self.ONES32B[0:32, :], self.ICOL[:, c:c + 1]))
            self.wait("t", self.sV, s_t)
            mm = self.op("t", lambda e, c=c: e.matmul(
                self.pAL32[:, c * 16:(c + 1) * 16],
                self.SELCB[0:32, :], self.ATB[:, :], start=True, stop=True))
            self.wait("v", self.sP, mm)
        alph = self.dop("v", lambda e: e.tensor_copy(self.ALPH32[:, 0:512], self.pAL32[:, :]))
        for g8 in range(4):
            if g8 > 0:
                self.wait("t", self.sV, self.sV.n)
            for j in range(8):
                c = g8 * 8 + j
                mm = lambda e, c=c, j=j: e.transpose(
                    self.pXT[:, j * QL:(j + 1) * QL], self.Bt[:, c * QL:(c + 1) * QL], self.IDB[:, :])
                if j < 7:
                    self.em("t", mm)
                else:
                    tr8 = self.op("t", mm)
            self.wait("v", self.sP, tr8)
            self.op("v", lambda e, g8=g8: e.tensor_copy(
                self.BTMALL[:, g8 * 1024:(g8 + 1) * 1024], self.pXT[:, :]))
        z0 = self.op("v", lambda e: e.memset(self.Zst[:, :], 0.0))
        self.ev_prep_v = self.sV.n
        self.ev_prep_cf = cf_t
        self.ev_prep_gam = st_g

    def p2b(self):
        NEGB = self.NEG[:, :].unsqueeze(1).broadcast_to([128, 16, QL])
        zv = self.zscr.ap()
        for eng in "tva":
            self.wait(eng, self.sV, self.ev_prep_v)
        gb_free = {}
        sz_free = {}
        sz_t = {}
        gb_t = {}
        self.xtm_used = {}
        self.y_used = {}
        self.g_used = {}
        self.d2_used = {}
        for c in range(getattr(self, 'p2b_chunks', NCH)):
            b = c % 2
            if c % 2 == 0:
                sb = (c // 2) % 2
                if c == 0:
                    for zc in self.ZSTc:
                        self.wait("y", zc, zc.n)
                if c >= 4:
                    self.wait("y", self.sV, sz_free.get(sb, 0))
                sz_t[c // 2] = self.dma(
                    "y", self.SZ[:, sb, :, :],
                    zv[:, :, c * QL:(c + 2) * QL], ctr=self.SZL[sb])
            if c >= 2:
                self.wait("g", self.sV, gb_free.get(b, 0))
            if c == 0:
                self.wait("g", self.PRP, self.ev_prep_gam)
            gb_t[c] = self.dma(
                "g", self.GBC2[:, b, :].rearrange("p (h t) -> p h t", h=16),
                self.gamdram.ap().rearrange("h (c t) -> c h t", c=NCH)[c].unsqueeze(0).broadcast_to([128, 16, QL]),
                ctr=self.GBL[b])

            lh1 = self.dop("v", lambda e, c=c: e.tensor_scalar_mul(
                self.LHS48[0:32, :], self.ONES32[0:32, :], self.ICOL[:, c:c + 1]))
            lh2 = self.op("v", lambda e, c=c: e.tensor_copy(
                self.LHS48[32:48, :], self.NB[:, c * QL:(c + 1) * QL]))
            if c > 0:
                self.wait("v", self.sP, self.ymm_prev)
            zb_t = self.dop("v", lambda e: e.tensor_copy(self.ZB[:, :], self.Zst[:, :]))

            if c == 0:
                self.wait("t", self.PRP, self.ev_prep_cf)
            else:
                self.wait("t", self.sV, self.d2_used[c - 1])
                self.wait("t", self.sV, self.g_used[c - 1])
            self.wait("t", self.sV, lh2)
            for i in range(4):
                mm = lambda e, i=i: e.matmul(
                    self.pA[:, i * 512:(i + 1) * 512], self.LHS48[:, :],
                    self.RHS48[:, i * 512:(i + 1) * 512], start=True, stop=True)
                if i < 3:
                    self.em("t", mm)
                else:
                    dmm = self.op("t", mm)
            if c > 0:
                self.wait("t", self.sV, self.cbtm_ev)
            cbt = self.op("t", lambda e, c=c: e.matmul(
                self.pB[:, :], self.Bt[:, c * QL:(c + 1) * QL],
                self.Ct[:, c * QL:(c + 1) * QL], start=True, stop=True))
            if c > 0:
                self.wait("t", self.sV, self.xtm_ev)
            for j in range(8):
                mm = lambda e, c=c, j=j: e.transpose(
                    self.pXT[:, j * QL:(j + 1) * QL], self.XS[:, j, c * QL:(c + 1) * QL], self.IDB[:, :])
                if j < 7:
                    self.em("t", mm)
                else:
                    xtr = self.op("t", mm)

            self.wait("v", self.sP, dmm)
            d2t = self.dop("v", lambda e: e.tensor_tensor(
                self.D2[:, :].rearrange("p (a b) -> p a b", a=16), self.pA[:, :].rearrange("p (a b) -> p a b", a=16),
                NEGB, ALU.add))
            self.d2_used[c] = d2t
            self.wait("v", self.sP, cbt)
            self.cbtm_ev = self.op("v", lambda e: e.tensor_copy(self.CBTM[:, :], self.pB[:, :]))
            self.wait("v", self.sP, xtr)
            self.xtm_ev = self.op("v", lambda e, b=b: e.tensor_copy(self.XTM[:, b, :], self.pXT[:, :]))
            self.wait("a", self.sV, d2t)
            e2t = self.op("a", lambda e: e.activation(self.E2[:, :], self.D2[:, :], AF.Exp))
            self.wait("v", self.sA, e2t)
            mt_t = self.dop("v", lambda e: e.tensor_tensor(
                self.MT[:, :].rearrange("p (a b) -> p a b", a=16),
                self.E2[:, :].rearrange("p (a b) -> p a b", a=16),
                self.CBTM.unsqueeze(1).broadcast_to([128, 16, QL]), ALU.mult))
            self.wait("v", self.GBL[b], gb_t[c])
            wt_t = self.op("v", lambda e, c=c, b=b: e.tensor_tensor(
                self.WTl[:, :].rearrange("p (a b) -> p a b", a=16),
                self.GBC2[:, b, :].rearrange("p (a b) -> p a b", a=16),
                self.Ct[:, c * QL:(c + 1) * QL].unsqueeze(1).broadcast_to([128, 16, QL]), ALU.mult))
            gb_free[b] = wt_t
            self.op("v", lambda e: e.tensor_copy(self.STAT[:, :], self.E2[:, QL - 1::QL]))
            self.wait("v", self.sV, self.sV.n)
            for h in range(16):
                self.op("v", lambda e, h=h, b=b: e.tensor_scalar_mul(
                    self.XDD[:, h * 64:(h + 1) * 64], self.XTM[:, b, h * 64:(h + 1) * 64],
                    self.STAT[:, h:h + 1]))
            xdd_t = self.sV.n

            if c > 0:
                self.wait("t", self.sV, self.y_used[c - 1])
            self.wait("t", self.sV, mt_t)
            self.wait("t", self.sV, wt_t)
            self.wait("t", self.sV, zb_t)
            self.wait("t", self.sV, self.xtm_ev)
            for h in range(16):
                u, jj = h % 2, h // 2
                out = self.pY[u * 64:u * 64 + 64, jj * QL:(jj + 1) * QL]
                self.em("t", lambda e, h=h, b=b, out=out: e.matmul(
                    out, self.XTM[:, b, h * 64:(h + 1) * 64],
                    self.MT[:, h * QL:(h + 1) * QL], start=True, stop=False))
                mm = lambda e, h=h, out=out: e.matmul(
                    out, self.ZB[:, h * 64:(h + 1) * 64],
                    self.WTl[:, h * QL:(h + 1) * QL], start=False, stop=True)
                if h < 15:
                    self.em("t", mm)
                else:
                    ymm = self.op("t", mm)
                    self.ymm_prev = ymm
            self.wait("t", self.sV, d2t)
            self.wait("t", self.sV, xdd_t)
            for i in range(2):
                mm = lambda e, c=c, i=i: e.matmul(
                    self.pA[:, i * 512:(i + 1) * 512], self.BTMALL[:, c * QL:(c + 1) * QL],
                    self.XDD[:, i * 512:(i + 1) * 512], start=True, stop=True)
                if i < 1:
                    self.em("t", mm)
                else:
                    gmm = self.op("t", mm)

            sb = (c // 2) % 2
            self.wait("v", self.SZL[sb], sz_t[c // 2])
            self.wait("v", self.sP, ymm)
            for jj in range(8):
                self.op("v", lambda e, jj=jj, c=c: e.scalar_tensor_tensor(
                    self.O1[:, jj * QL:(jj + 1) * QL], self.XS[:, jj, c * QL:(c + 1) * QL],
                    self.DBIG[:, jj:jj + 1], self.pY[:, jj * QL:(jj + 1) * QL],
                    ALU.mult, ALU.add))
            self.wait("v", self.sV, self.sV.n)
            for jj in range(8):
                self.op("v", lambda e, jj=jj, c=c, sb=sb: e.tensor_tensor(
                    self.XS[:, jj, c * QL:(c + 1) * QL], self.O1[:, jj * QL:(jj + 1) * QL],
                    self.SZ[:, sb, jj, (c % 2) * QL:(c % 2) * QL + QL], ALU.mult))
            self.y_used[c] = self.sV.n
            if c % 2 == 1:
                sz_free[sb] = self.sV.n
            self.wait("v", self.sP, gmm)
            self.wait("v", self.sV, self.sV.n)
            for h in range(16):
                self.op("v", lambda e, h=h, c=c: e.scalar_tensor_tensor(
                    self.Zst[:, h * 64:(h + 1) * 64], self.Zst[:, h * 64:(h + 1) * 64],
                    self.ALPH32[:, c * 16 + h:c * 16 + h + 1], self.pA[:, h * 64:(h + 1) * 64],
                    ALU.mult, ALU.add))
            self.g_used[c] = self.sV.n
        self.ev_ssm_v = self.sV.n

        if self.dbg:
            self.barrier()
            self.dbg_copy("y2", self.XS[:, :, :], [128, 8, T], BF16)
            self.dbg_copy("dt", self.dtdram.ap(), [16, T], F32)
            self.dbg_copy("cum", self.cumdram.ap(), [16, T], F32)

    def wtrans(self):
        TSTG = self.XRS.rearrange("p a b -> p (a b)")
        c1 = self.dma("g", self.wupbf.ap(), self.wup, ctr=self.WCST)
        c2 = self.dma("g", self.wgatebf.ap(), self.wgate, ctr=self.WCST)
        c3 = self.dma("g", self.wdownbf.ap(), self.wdown, ctr=self.WCST)
        jobs = [(self.wupbf, self.wupT, 8192, 2048, c1),
                (self.wgatebf, self.wgateT, 8192, 2048, c2),
                (self.wdownbf, self.wdownT, 2048, 8192, c3)]
        st_t = 0
        for src, dst, R, Cc, cast_t in jobs:
            self.wait("y", self.WCST, cast_t)
            for k in range(Cc // 128):
                if st_t:
                    self.wait("y", self.WTB, st_t)
                ld = self.dma("y", TSTG[:, 0:R], src.ap()[:, k * 128:(k + 1) * 128],
                              transpose=True, ctr=self.WTA)
                self.wait("y", self.WTA, ld)
                st_t = self.dma("y", dst.ap()[k * 128:(k + 1) * 128, :], TSTG[:, 0:R], ctr=self.WTB)
        self.ev_wtrans = st_t

    def p2c(self):
        A = self.ARE
        SQS = A[:, 12288:20480].bitcast(F32)
        TMP = A[:, 0:8192].bitcast(F32)
        ROWA = A[0:1, 24576:32768].bitcast(F32)
        RGBC = A[:, 8192:12288]
        self.wait("y", self.WTB, self.ev_wtrans)
        self.dma("y", self.GNW[:, :], self.gnwI, ctr=self.PR2)
        self.dma("y", self.SUD[:, :], self.su_dpI, ctr=self.PR2)
        self.dma("y", self.SU1[:, :], self.su1I, ctr=self.PR2)
        g_ld = self.dma("y", self.SU2[:, :], self.su2I, ctr=self.PR2)
        self.ev_pr2 = g_ld
        for eng in "va":
            self.wait(eng, self.sV, self.ev_ssm_v)
            for zc in self.SZL:
                self.wait(eng, zc, zc.n)
        sq0 = self.op("a", lambda e: e.activation(SQS[:, :], self.XS[:, 0, :], AF.Square))
        prev_add = 0
        for jj in range(1, 8):
            if prev_add:
                self.wait("a", self.sV, prev_add)
            t = self.dop("a", lambda e, jj=jj: e.activation(TMP[:, :], self.XS[:, jj, :], AF.Square))
            self.wait("v", self.sA, t)
            prev_add = self.dop("v", lambda e: e.tensor_tensor(SQS[:, :], SQS[:, :], TMP[:, :], ALU.add))
        sq_v = self.sV.n
        if "sqs" in self.dbg:
            self.barrier()
            self.dbg_copy("sqs", SQS[:, :], [128, T], F32)
        self.wait("t", self.sV, sq_v)
        self.wait("t", self.sV, self.g_used[NCH - 1])
        for half in range(2):
            for i in range(4):
                mm = lambda e, half=half, i=i: e.matmul(
                    self.pA[0:1, i * 512:(i + 1) * 512], self.ONESC[:, 0:1],
                    SQS[:, half * 2048 + i * 512:half * 2048 + (i + 1) * 512],
                    start=True, stop=True)
                mmt = self.op("t", mm)
            self.wait("v", self.sP, mmt)
            self.dop("v", lambda e, half=half: e.tensor_copy(ROWA[:, half * 2048:(half + 1) * 2048], self.pA[0:1, 0:2048]))
            if half == 0:
                self.wait("t", self.sV, self.sV.n)
        row_v = self.sV.n
        self.wait("y", self.sV, row_v)
        ar_st = self.dma("y", self.ar_in.ap(), ROWA[:, :], ctr=self.PR2)
        self.wait("g", self.PR2, ar_st)
        self.em("g", lambda e: e.collective_compute(
            "AllReduce", ALU.add, replica_groups=[[0, 1, 2, 3], [4, 5, 6, 7]],
            ins=[self.ar_in.ap().opt()], outs=[self.ar_out.ap().opt()]).then_inc(self.cc.sem))
        self.cc.inc(1)
        self.wait("y", self.cc, self.cc.n)
        ar_ld = self.dma("y", ROWA[:, :], self.ar_out.ap(), ctr=self.PR2)
        self.wait("a", self.PR2, ar_ld)
        self.dop("a", lambda e: e.activation(ROWA[:, :], ROWA[:, :], AF.Ln, bias=self.EPSC[0:1, 0:1], scale=1.0 / 4096.0))
        rg_a = self.dop("a", lambda e: e.activation(ROWA[:, :], ROWA[:, :], AF.Exp, scale=-0.5))
        self.wait("t", self.sA, rg_a)
        self.wait("t", self.sV, self.sV.n)
        for half in range(2):
            for i in range(4):
                mm = lambda e, half=half, i=i: e.matmul(
                    self.pA[:, i * 512:(i + 1) * 512], self.ONESR[0:1, :],
                    ROWA[:, half * 2048 + i * 512:half * 2048 + (i + 1) * 512],
                    start=True, stop=True)
                mmt = self.op("t", mm)
            self.wait("v", self.sP, mmt)
            ev = self.dop("v", lambda e, half=half: e.tensor_copy(RGBC[:, half * 2048:(half + 1) * 2048], self.pA[:, 0:2048]))
            self.wait("t", self.sV, ev)
        self.wait("v", self.PR2, g_ld)
        for jj in range(8):
            self.dop("v", lambda e, jj=jj: e.scalar_tensor_tensor(
                self.XS[:, jj, :], self.XS[:, jj, :], self.GNW[:, jj:jj + 1],
                RGBC[:, :], ALU.mult, ALU.mult))
        y2n_v = self.sV.n
        if "y2n" in self.dbg:
            self.barrier()
            self.dbg_copy("y2n", self.XS[:, :, :], [128, 8, T], BF16)
            self.dbg_copy("rgb", A[:, 8192:12288], [128, T], BF16)
        WOBv = A[:, 12288:20480].rearrange("p (b k m) -> p b k m", b=2, k=8)
        EV2 = A[:, 0:8192].bitcast(F32).rearrange("p (b n) -> p b n", b=2)
        wob_t = {}
        self.wait("t", self.sV, y2n_v)
        ev_done = {}
        st_done = {}
        wv = self.wout_s.rearrange("(k p) o -> p k o", p=128)
        mm_done_n = {}
        for n in range(4):
            if n >= 2:
                self.wait("g", self.sP, mm_done_n[n - 2])
            wob_t[n] = self.dma("g", WOBv[:, n % 2, :, :], wv[:, :, n * 512:(n + 1) * 512], ctr=self.WOBc[n % 2])
            self.wait("t", self.WOBc[n % 2], wob_t[n])
            for mt in range(32):
                sl = mt % 4
                if (n, mt) not in [(0, 0), (0, 1), (0, 2), (0, 3)]:
                    eng_c, tgt = ev_done.get(sl, (None, 0))
                    if eng_c is not None:
                        self.wait("t", eng_c, tgt)
                for kk in range(8):
                    mm = lambda e, n=n, mt=mt, kk=kk, sl=sl: e.matmul(
                        self.pA[:, sl * 512:(sl + 1) * 512],
                        self.XS[:, kk, mt * 128:(mt + 1) * 128],
                        WOBv[:, n % 2, kk, :], start=(kk == 0), stop=(kk == 7))
                    if kk < 7:
                        self.em("t", mm)
                    else:
                        mmt = self.op("t", mm)
                eb = mt % 2
                self.wait("v", self.RSQ[eb], st_done.get(eb, 0))
                self.wait("v", self.sP, mmt)
                ev = self.op("v", lambda e, sl=sl, eb=eb: e.tensor_copy(
                    EV2[:, eb, 0:512], self.pA[:, sl * 512:(sl + 1) * 512]))
                ev_done[sl] = (self.sV, ev)
                self.wait("y", self.sV, ev)
                st_done[eb] = self.dma("y", self.rs_in.ap()[mt * 128:(mt + 1) * 128, n * 512:(n + 1) * 512],
                                       EV2[:, eb, 0:512], ctr=self.RSQ[eb])
            mm_done_n[n] = self.sP.n
        for eb in range(2):
            self.wait("g", self.RSQ[eb], self.RSQ[eb].n)
        self.em("g", lambda e: e.collective_compute(
            "ReduceScatter", ALU.add, replica_groups=[[0, 1, 2, 3], [4, 5, 6, 7]],
            ins=[self.rs_in.ap().opt()], outs=[self.rs_out.ap().opt()]).then_inc(self.cc.sem))
        self.cc.inc(1)
        self.ev_rs = self.cc.n
        if self.dbg:
            self.barrier()
            self.dbg_copy("rsi", self.rs_in.ap(), [T, D], F32)
            self.dbg_copy("rso", self.rs_out.ap(), [TOK, D], F32)
            self.dbg_copy("arr", self.ar_out.ap(), [1, T], F32)
            self.dbg_copy("ari", self.ar_in.ap(), [1, T], F32)

    def p3(self):
        A = self.ARE
        S = self.STAT
        HT = A[:, 0:8192].bitcast(F32).rearrange("p (b n) -> p b n", b=2)
        XOT = A[:, 8192:16384].bitcast(F32).rearrange("p (b n) -> p b n", b=2)
        H2S = A[:, 24576:28672].rearrange("p (b n) -> p b n", b=2)
        XNS = A[:, 28672:32768].rearrange("p (b n) -> p b n", b=2)
        self.wait("y", self.cc, self.ev_rs)
        xov = self.x_own.rearrange("(n p) d -> n p d", p=128)
        rov = self.rs_out.ap().rearrange("(n p) d -> n p d", p=128)
        h2v = self.h2nscr.ap().rearrange("(n p) d -> n p d", p=128)
        xnv = self.xnownscr.ap().rearrange("(n p) d -> n p d", p=128)
        for mt in range(8):
            b = mt % 2
            if mt >= 2:
                self.wait("y", self.sV, self.p3_used[mt - 2])
            lx = self.dma("y", XOT[:, b, :], xov[mt], ctr=self.XOL[b])
            lr = self.dma("y", HT[:, b, :], rov[mt], ctr=self.RSL[b])
            self.wait("v", self.XOL[b], lx)
            self.wait("a", self.XOL[b], lx)
            X = XOT[:, b, :]
            self.op("v", lambda e, X=X: e.tensor_reduce(S[:, 0:1], X, AX.X, ALU.add))
            if getattr(self, "p3_sq_rd", 0):
                self.wait("a", self.sV, self.p3_sq_rd)
            sq = self.op("a", lambda e, X=X: e.activation(self.ARE[:, 33280:37376].bitcast(F32), X, AF.Square))
            self.wait("v", self.sA, sq)
            rd1 = self.op("v", lambda e: e.tensor_reduce(S[:, 1:2], self.ARE[:, 33280:37376].bitcast(F32), AX.X, ALU.add))
            self.dop("v", lambda e: e.tensor_scalar_mul(S[:, 2:3], S[:, 0:1], 1.0 / D))
            self.op("v", lambda e: e.tensor_scalar_mul(S[:, 3:4], S[:, 1:2], 1.0 / D))
            self.dop("v", lambda e: e.tensor_tensor(S[:, 4:5], S[:, 2:3], S[:, 2:3], ALU.mult))
            self.dop("v", lambda e: e.tensor_tensor(S[:, 4:5], S[:, 3:4], S[:, 4:5], ALU.subtract))
            va = self.dop("v", lambda e: e.tensor_scalar_add(S[:, 5:6], S[:, 4:5], EPS))
            self.wait("a", self.sV, va)
            self.op("a", lambda e: e.activation(S[:, 6:7], S[:, 5:6], AF.Ln))
            r1 = self.dop("a", lambda e: e.activation(S[:, 6:7], S[:, 6:7], AF.Exp, scale=-0.5))
            self.wait("v", self.sA, r1)
            xnt = self.dop("v", lambda e, X=X, b=b: e.tensor_scalar(
                XNS[:, b, :], X, S[:, 2:3], S[:, 6:7], ALU.subtract, ALU.mult))
            self.wait("v", self.RSL[b], lr)
            ht = self.dop("v", lambda e, b=b: e.tensor_tensor(HT[:, b, :], XNS[:, b, :], HT[:, b, :], ALU.add))
            ot = self.dop("v", lambda e, X=X, b=b, mt=mt: e.tensor_tensor(
                self.OUT[:, mt, :], X, HT[:, b, :], ALU.add))
            self.wait("a", self.sV, ot)
            sq2 = self.op("a", lambda e, b=b: e.activation(self.ARE[:, 33280:37376].bitcast(F32), HT[:, b, :], AF.Square))
            self.wait("v", self.sA, sq2)
            rd2 = self.op("v", lambda e: e.tensor_reduce(S[:, 8:9], self.ARE[:, 33280:37376].bitcast(F32), AX.X, ALU.add))
            self.p3_sq_rd = rd2
            r2v = self.dop("v", lambda e: e.tensor_scalar(S[:, 9:10], S[:, 8:9], 1.0 / D, EPS, ALU.mult, ALU.add))
            self.wait("a", self.sV, r2v)
            self.op("a", lambda e: e.activation(S[:, 10:11], S[:, 9:10], AF.Ln))
            r2 = self.dop("a", lambda e: e.activation(S[:, 10:11], S[:, 10:11], AF.Exp, scale=-0.5))
            self.wait("v", self.sA, r2)
            h2t = self.dop("v", lambda e, b=b: e.tensor_scalar_mul(H2S[:, b, :], HT[:, b, :], S[:, 10:11]))
            self.p3_used = getattr(self, "p3_used", {})
            self.p3_used[mt] = h2t
            self.wait("y", self.sV, h2t)
            self.dma("y", h2v[mt], H2S[:, b, :], ctr=self.H2T[b])
            self.dma("y", xnv[mt], XNS[:, b, :], ctr=self.H2T[b])
        self.ev_p3_v = self.sV.n
        self.ev_p3_h2 = [(c, c.n) for c in self.H2T]
        if self.dbg:
            self.barrier()
            self.dbg_copy("h2n", self.h2nscr.ap(), [TOK, D], BF16)
            self.dbg_copy("xno", self.xnownscr.ap(), [TOK, D], BF16)
            t3 = self.nc.dram_tensor("dbg_out3", [128, 8, D], F32, kind="ExternalOutput").ap()
            self.dma("y", t3, self.OUT[:, :, :], ctr=self.STO)

    def p4a(self):
        A = self.ARE
        H2F = self.MID[:, 0:16384].rearrange("p (k m) -> p k m", k=16)
        X2Q = A[:, 0:16384].rearrange("p (k m) -> p k m", k=16)
        WBK = A[:, 16384:32768].rearrange("p (s w k m) -> p s w k m", s=2, w=2, k=16)
        SG = A[:, 36864:38912].rearrange("p (b n) -> p b n", b=2)
        self.dr_used = 0
        self.pA_ev = {}
        for c, nn_ in self.ev_p3_h2:
            self.wait("y", c, nn_)
        for gc in self.GBL:
            self.wait("y", gc, gc.n)
        h2l = 0
        for k in range(16):
            h2l = self.dma("y", H2F[:, k, :], self.h2nscr.ap()[:, k * 128:(k + 1) * 128],
                           transpose=True, ctr=self.H2L)
        self.wait("v", self.H2L, h2l)
        self.wait("v", self.sV, self.ev_p3_v)
        for k in range(16):
            self.op("v", lambda e, k=k: e.tensor_scalar_mul(H2F[:, k, :], H2F[:, k, :], self.LN2W[:, k:k + 1]))
        h2s_v = self.sV.n
        self.wait("t", self.sV, h2s_v)
        gv = self.gate_w.rearrange("(k p) i -> p k i", p=128)
        uv = self.up_w.rearrange("(k p) i -> p k i", p=128)
        dv = self.down_w.rearrange("(k p) o -> p k o", p=128)
        if "h2f" in self.dbg:
            self.barrier()
            th = self.nc.dram_tensor("dbg_h2f", [128, 16, TOK], BF16, kind="ExternalOutput").ap()
            self.dma("y", th, H2F[:, :, :], ctr=self.STO)
        self.wait("g", self.sV, h2s_v)
        self._swiglu_common(H2F, X2Q, WBK, SG, gv, uv, dv, su=False)
        if "x2q" in self.dbg:
            self.barrier()
            tx = self.nc.dram_tensor("dbg_x2q", [128, 16, TOK], BF16, kind="ExternalOutput").ap()
            self.dma("y", tx, X2Q[:, :, :], ctr=self.STO)
        self.ev_p4a = self.sV.n
        self.ev_p4a_mm = self.sP.n
        if "out4a" in self.dbg:
            self.barrier()
            t4 = self.nc.dram_tensor("dbg_out4a", [128, 8, D], F32, kind="ExternalOutput").ap()
            self.dma("y", t4, self.OUT[:, :, :], ctr=self.STO)

    def _swiglu_common(self, H2F, X2Q, WBK, SG, gv, uv, dv, su):
        A = self.ARE
        wb_used = {}
        ev_pA = 0
        dq_mm = 0
        for q in range(4):
            x2_done = 0
            for ib in range(8):
                i0 = q * 16 + ib * 2
                slot = ib % 2
                if wb_used.get(slot, 0):
                    self.wait("g", self.sP, wb_used[slot])
                if ib < 2 and dq_mm:
                    self.wait("g", self.sP, dq_mm)
                tg = self.dma("g", WBK[:, slot, 0, :, :], gv[:, :, i0 * 128:(i0 + 2) * 128], ctr=self.WBL[slot])
                tu = self.dma("g", WBK[:, slot, 1, :, :], uv[:, :, i0 * 128:(i0 + 2) * 128], ctr=self.WBL[slot])
                self.wait("t", self.WBL[slot], tu)
                for ii in range(2):
                    if ev_pA:
                        self.wait("t", self.sV, ev_pA)
                    for w in range(2):
                        for n2 in range(2):
                            for k in range(16):
                                mm = lambda e, w=w, ii=ii, n2=n2, k=k, slot=slot: e.matmul(
                                    self.pA[:, (w * 2 + n2) * 512:(w * 2 + n2 + 1) * 512],
                                    WBK[:, slot, w, k, ii * 128:(ii + 1) * 128],
                                    H2F[:, k, n2 * 512:(n2 + 1) * 512],
                                    start=(k == 0), stop=(k == 15))
                                if k < 15 or w == 0:
                                    self.em("t", mm)
                                else:
                                    mmt = self.op("t", mm)
                    self.wait("v", self.sP, mmt)
                    sg_t = self.dop("v", lambda e: e.tensor_copy(SG[:, 0, :], self.pA[:, 0:1024]))
                    self.wait("a", self.sV, sg_t)
                    sl_t = self.op("a", lambda e: e.activation(SG[:, 1, :], SG[:, 0, :], AF.Silu))
                    self.wait("v", self.sA, sl_t)
                    x2_done = self.dop("v", lambda e, i=ib * 2 + ii: e.tensor_tensor(
                        X2Q[:, i, :], SG[:, 1, :], self.pA[:, 1024:2048], ALU.mult))
                    ev_pA = x2_done
                wb_used[slot] = self.sP.n
            q_mm = self.sP.n
            self.wait("t", self.sV, x2_done)
            DR8v = self.ARE[:, 22528:38912].rearrange("p (k m) -> p k m", k=8)
            for hf in range(2):
                if hf == 0:
                    self.wait("g", self.sP, q_mm)
                    self.wait("g", self.sV, x2_done)
                if self.dr_used:
                    self.wait("g", self.sV, self.dr_used)
                drt = self.dma("g", DR8v[:, :, :], dv[:, q * 16 + hf * 8:q * 16 + (hf + 1) * 8, :], ctr=self.DRL[0])
                self.wait("t", self.DRL[0], drt)
                for gg in range(8):
                    pend = []
                    for j in range(4):
                        pidx = gg * 4 + j
                        m, n = pidx // 4, pidx % 4
                        if self.pA_ev.get(j, 0):
                            self.wait("t", self.sV, self.pA_ev[j])
                        for k in range(8):
                            mm = lambda e, j=j, m=m, n=n, k=k, hf=hf: e.matmul(
                                self.pA[:, j * 512:(j + 1) * 512],
                                X2Q[:, hf * 8 + k, m * 128:(m + 1) * 128],
                                DR8v[:, k, n * 512:(n + 1) * 512],
                                start=(k == 0), stop=(k == 7))
                            if k < 7:
                                self.em("t", mm)
                            else:
                                mmt = self.op("t", mm)
                        pend.append((j, m, n, mmt))
                    for j, m, n, mmt in pend:
                        self.wait("v", self.sP, mmt)
                        ev = self.dop("v", lambda e, j=j, m=m, n=n: e.tensor_tensor(
                            self.OUT[:, m, n * 512:(n + 1) * 512], self.OUT[:, m, n * 512:(n + 1) * 512],
                            self.pA[:, j * 512:(j + 1) * 512], ALU.add))
                        self.pA_ev[j] = ev
                self.dr_used = self.sV.n
                dq_mm = self.sP.n

    def p4b(self):
        A = self.ARE
        XNF = self.MID[:, 0:16384].rearrange("p (k m) -> p k m", k=16)
        SG = A[:, 36864:38912].rearrange("p (b n) -> p b n", b=2)
        EVB = A[:, 32768:36864].rearrange("p (b n) -> p b n", b=2)
        for c, nn_ in self.ev_p3_h2:
            self.wait("y", c, nn_)
        self.wait("y", self.sV, self.ev_p4a)
        self.wait("y", self.sP, self.ev_p4a_mm)
        xl = 0
        for k in range(16):
            xl = self.dma("y", XNF[:, k, :], self.xnownscr.ap()[:, k * 128:(k + 1) * 128],
                          transpose=True, ctr=self.H2L)
        self.wait("v", self.H2L, xl)
        self.wait("v", self.sV, self.ev_p4a)
        for k in range(16):
            self.op("v", lambda e, k=k: e.tensor_scalar_mul(XNF[:, k, :], XNF[:, k, :], self.SUU[:, k:k + 1]))
        xh_v = self.sV.n
        self.wait("t", self.sV, xh_v)
        s1v = self.s1scr.ap().rearrange("(a o) t -> o a t", o=16)
        s2v = self.s2scr.ap().rearrange("(o j u) t -> j o u t", o=16, j=16, u=8)
        pyev = 0
        for k in range(16):
            if pyev:
                self.wait("t", self.sV, pyev)
            for n2 in range(2):
                mm = lambda e, k=k, n2=n2: e.matmul(
                    self.pY[:, n2 * 512:(n2 + 1) * 512], self.H128[:, :],
                    XNF[:, k, n2 * 512:(n2 + 1) * 512], start=True, stop=True)
                if n2 == 0:
                    self.em("t", mm)
                else:
                    mmt = self.op("t", mm)
            self.wait("v", self.sP, mmt)
            eb = k % 2
            if k >= 2:
                self.wait("v", self.SH1[eb], self.sh1_t.get(k - 2, 0))
            ev = self.dop("v", lambda e, eb=eb: e.tensor_copy(EVB[:, eb, 0:1024], self.pY[:, :]))
            pyev = ev
            self.wait("y", self.sV, ev)
            self.sh1_t = getattr(self, "sh1_t", {})
            self.sh1_t[k] = self.dma("y", s1v[k], EVB[:, eb, 0:1024], ctr=self.SH1[eb])
        ld_t = {}
        for k in range(16):
            eb = k % 2
            if k >= 2:
                self.wait("y", self.sP, self.sh2_mm.get(k - 2, 0))
            if k < 2:
                for kk in range(16):
                    self.wait("y", self.SH1[kk % 2], self.sh1_t[kk])
            if k >= 1:
                self.wait("y", self.SH1[(k - 1) % 2], self.sh3_t[k - 1])
            ld_t[k] = self.dma("y", EVB[:, eb, 0:1024], self.s1scr.ap()[k * 128:(k + 1) * 128, :], ctr=self.SH2[eb])
            self.wait("t", self.SH2[eb], ld_t[k])
            if pyev:
                self.wait("t", self.sV, pyev)
            for n2 in range(2):
                mm = lambda e, eb=eb, n2=n2: e.matmul(
                    self.pY[:, n2 * 512:(n2 + 1) * 512], self.BD16[:, :],
                    EVB[:, eb, n2 * 512:(n2 + 1) * 512], start=True, stop=True)
                if n2 == 0:
                    self.em("t", mm)
                else:
                    mmt = self.op("t", mm)
            self.sh2_mm = getattr(self, "sh2_mm", {})
            self.sh2_mm[k] = mmt
            self.wait("v", self.sP, mmt)
            ev = self.dop("v", lambda e, eb=eb: e.tensor_copy(EVB[:, 1 - eb, 0:1024], self.pY[:, :]))
            pyev = ev
            self.wait("y", self.sV, ev)
            self.sh3_t = getattr(self, "sh3_t", {})
            self.sh3_t[k] = self.dma("y", s2v[k], EVB[:, 1 - eb, 0:1024], ctr=self.SH1[eb])
        for kk in range(16):
            self.wait("y", self.SH1[kk % 2], self.sh3_t[kk])
        x3l = self.dma("y", XNF[:, :, :], self.s2scr.ap().rearrange("(k p) t -> p k t", p=128), ctr=self.H2L)
        self.wait("t", self.H2L, x3l)
        if "xh3" in self.dbg:
            self.barrier()
            self.dbg_copy("xh3", self.s2scr.ap(), [D, TOK], BF16)
        self.wait("g", self.sV, self.ev_p4a)
        self.wait("g", self.sP, self.ev_p4a_mm)
        self.wait("v", self.PR2, self.ev_pr2)
        self.op("v", lambda e: e.tensor_scalar_mul(self.SUD[:, :], self.SUD[:, :], 1.0 / np.sqrt(8192.0)))
        WB2 = A[:, 6144:22528].rearrange("p (s w k m) -> p s w k m", s=2, w=2, k=16)
        s3v = self.s3scr.ap().rearrange("(a o) t -> o a t", o=64)
        upv = self.wupT.ap().rearrange("(k p) i -> p k i", p=128)
        gvv = self.wgateT.ap().rearrange("(k p) i -> p k i", p=128)
        wb_used = {}
        pAev = 0
        sh4 = {}
        for it in range(64):
            ib, ii = it // 2, it % 2
            slot = ib % 2
            if ii == 0:
                if wb_used.get(slot, 0):
                    self.wait("g", self.sP, wb_used[slot])
                tg = self.dma("g", WB2[:, slot, 0, :, :], upv[:, :, ib * 256:(ib + 1) * 256], ctr=self.WBL[slot])
                tu = self.dma("g", WB2[:, slot, 1, :, :], gvv[:, :, ib * 256:(ib + 1) * 256], ctr=self.WBL[slot])
                self.wait("t", self.WBL[slot], tu)
            if pAev:
                self.wait("t", self.sV, pAev)
            for w in range(2):
                for n2 in range(2):
                    for k in range(16):
                        mm = lambda e, w=w, ii=ii, n2=n2, k=k, slot=slot: e.matmul(
                            self.pA[:, (w * 2 + n2) * 512:(w * 2 + n2 + 1) * 512],
                            WB2[:, slot, w, k, ii * 128:(ii + 1) * 128],
                            XNF[:, k, n2 * 512:(n2 + 1) * 512],
                            start=(k == 0), stop=(k == 15))
                        if k < 15 or w == 0:
                            self.em("t", mm)
                        else:
                            mmt = self.op("t", mm)
            if ii == 1:
                wb_used[slot] = self.sP.n
            self.wait("v", self.sP, mmt)
            ut = self.dop("v", lambda e, it=it: e.tensor_scalar_mul(SG[:, 0, :], self.pA[:, 0:1024], self.SU1[:, it:it + 1]))
            gt = self.op("v", lambda e, it=it: e.tensor_scalar_mul(SG[:, 1, :], self.pA[:, 1024:2048], self.SU2[:, it:it + 1]))
            pAev = gt
            self.wait("a", self.sV, gt)
            st = self.op("a", lambda e: e.activation(SG[:, 1, :], SG[:, 1, :], AF.Silu))
            self.wait("v", self.sA, st)
            x2t = self.dop("v", lambda e, it=it: e.scalar_tensor_tensor(
                SG[:, 0, :], SG[:, 1, :], self.SUD[:, it:it + 1], SG[:, 0, :], ALU.mult, ALU.mult))
            self.wait("t", self.sV, x2t)
            if sh4.get("pyev", 0):
                self.wait("t", self.sV, sh4["pyev"])
            for n2 in range(2):
                mm = lambda e, n2=n2: e.matmul(
                    self.pY[:, n2 * 512:(n2 + 1) * 512], self.H128[:, :],
                    SG[:, 0, n2 * 512:(n2 + 1) * 512], start=True, stop=True)
                if n2 == 0:
                    self.em("t", mm)
                else:
                    p1mm = self.op("t", mm)
            self.wait("v", self.sP, p1mm)
            eb = it % 2
            if it >= 2:
                self.wait("v", self.SH1[eb], sh4.get(it - 2, 0))
            ev = self.dop("v", lambda e, eb=eb: e.tensor_copy(EVB[:, eb, 0:1024], self.pY[:, :]))
            sh4["pyev"] = ev
            self.wait("y", self.sV, ev)
            sh4[it] = self.dma("y", s3v[it], EVB[:, eb, 0:1024], ctr=self.SH1[eb])
        s4v = self.s4scr.ap().rearrange("(o j u) t -> j o u t", o=64, j=64, u=2)
        pyev = sh4["pyev"]
        sh5 = {}
        for k in range(64):
            eb = k % 2
            if k < 2:
                for kk in range(64):
                    self.wait("y", self.SH1[kk % 2], sh4[kk])
            if k >= 2:
                self.wait("y", self.sP, sh5.get(("mm", k - 2), 0))
            if k >= 1:
                self.wait("y", self.SH1[(k - 1) % 2], sh5[k - 1])
            lt = self.dma("y", EVB[:, eb, 0:1024], self.s3scr.ap()[k * 128:(k + 1) * 128, :], ctr=self.SH2[eb])
            self.wait("t", self.SH2[eb], lt)
            if pyev:
                self.wait("t", self.sV, pyev)
            for n2 in range(2):
                mm = lambda e, eb=eb, n2=n2: e.matmul(
                    self.pY[:, n2 * 512:(n2 + 1) * 512], self.BD64[:, :],
                    EVB[:, eb, n2 * 512:(n2 + 1) * 512], start=True, stop=True)
                if n2 == 0:
                    self.em("t", mm)
                else:
                    mmt = self.op("t", mm)
            sh5[("mm", k)] = mmt
            self.wait("v", self.sP, mmt)
            ev = self.dop("v", lambda e, eb=eb: e.tensor_copy(EVB[:, 1 - eb, 0:1024], self.pY[:, :]))
            pyev = ev
            self.wait("y", self.sV, ev)
            sh5[k] = self.dma("y", s4v[k], EVB[:, 1 - eb, 0:1024], ctr=self.SH1[eb])
        if "x3f" in self.dbg:
            self.barrier()
            self.dbg_copy("x3f", self.s4scr.ap(), [8192, TOK], BF16)
        X3Q = self.MID[:, 0:16384].rearrange("p (k m) -> p k m", k=16)
        DR8v = A[:, 22528:38912].rearrange("p (k m) -> p k m", k=8)
        SDB = A[:, 0:4096].bitcast(F32)
        TSF = A[:, 4096:6144].bitcast(F32)
        sdl = self.dma("y", SDB[:, :], self.sdbigI, ctr=self.H2L)
        dnv = self.wdownT.ap().rearrange("(k p) o -> p k o", p=128)
        for kk in range(64):
            self.wait("y", self.SH1[kk % 2], sh5[kk])
        self.wait("v", self.H2L, sdl)
        dr_used = 0
        for q in range(4):
            if dr_used:
                self.wait("y", self.sV, dr_used)
            x3l = self.dma("y", X3Q[:, :, :],
                           self.s4scr.ap().rearrange("(k p) t -> p k t", p=128)[:, q * 16:(q + 1) * 16, :],
                           ctr=self.H2L)
            self.wait("t", self.H2L, x3l)
            for hf in range(2):
                if dr_used:
                    self.wait("y", self.sV, dr_used)
                drt = self.dma("y", DR8v[:, :, :], dnv[:, q * 16 + hf * 8:q * 16 + (hf + 1) * 8, :], ctr=self.DRL2)
                self.wait("t", self.DRL2, drt)
                for gg in range(8):
                    pend = []
                    for j in range(4):
                        pidx = gg * 4 + j
                        m, n = pidx // 4, pidx % 4
                        if self.pA_ev.get(j, 0):
                            self.wait("t", self.sV, self.pA_ev[j])
                        for k in range(8):
                            mm = lambda e, j=j, m=m, n=n, k=k, hf=hf: e.matmul(
                                self.pA[:, j * 512:(j + 1) * 512],
                                X3Q[:, hf * 8 + k, m * 128:(m + 1) * 128],
                                DR8v[:, k, n * 512:(n + 1) * 512],
                                start=(k == 0), stop=(k == 7))
                            if k < 7:
                                self.em("t", mm)
                            else:
                                mmt = self.op("t", mm)
                        pend.append((j, m, n, mmt))
                    for j, m, n, mmt in pend:
                        self.wait("v", self.sP, mmt)
                        t1 = self.dop("v", lambda e, j=j, n=n: e.tensor_tensor(
                            TSF[:, 0:512], self.pA[:, j * 512:(j + 1) * 512],
                            SDB[:, n * 512:(n + 1) * 512], ALU.mult))
                        ev = self.dop("v", lambda e, m=m, n=n: e.tensor_tensor(
                            self.OUT[:, m, n * 512:(n + 1) * 512], self.OUT[:, m, n * 512:(n + 1) * 512],
                            TSF[:, 0:512], ALU.add))
                        self.pA_ev[j] = ev
                dr_used = self.sV.n
        self.ev_p4b = self.sV.n

    def pfin(self):
        ov = self.outT.rearrange("(n p) d -> n p d", p=128)
        self.wait("y", self.sV, self.ev_p4b)
        for mt in range(8):
            self.dma("y", ov[mt], self.OUT[:, mt, :], ctr=self.FIN)
        self.barrier()

    def emit(self, ctx):
        block = ctx.enter_context(self.nc.Block())
        block.gpsimd(lambda e: [f(e) for f in self.ops["g"]])
        block.vector(lambda e: [f(e) for f in self.ops["v"]])
        block.scalar(lambda e: [f(e) for f in self.ops["a"]])
        block.tensor(lambda e: [f(e) for f in self.ops["t"]])
        block.sync(lambda e: [f(e) for f in self.ops["y"]])

    def finalize_dev(self):
        self.dma("g", self.outT, self.xrscr.ap()[0:TOK, :], ctr=self.FIN)
        self.barrier()


def make_consts():
    h128 = had(128)
    def permkron(O, U):
        hh = had(O)
        m = np.zeros((O * U, O * U), np.float32)
        for u in range(U):
            for o in range(O):
                for o2 in range(O):
                    m[u * O + o, o2 * U + u] = hh[o, o2]
        return m
    bd16 = permkron(16, 8)
    bd64 = permkron(64, 2)
    ident = np.eye(128, dtype=np.float32)
    ut = np.triu(np.ones((128, 128), np.float32))
    negm = np.where(ut > 0, 0.0, -1e30).astype(np.float32)
    icol32 = np.eye(32, dtype=np.float32)
    hsel = np.zeros((16, 2048), np.float32)
    for j in range(16):
        hsel[j, j * 128:(j + 1) * 128] = 1.0
    return h128, bd16, bd64, ident, ut, negm, icol32, hsel


def prep_inputs(inputs):
    x = inputs["L0_attn_hidden_states"]
    w_in = inputs["L0_attn_in_proj_w"]
    conv_w = inputs["L0_attn_conv_w"][:, 0, :]
    conv_b = inputs["L0_attn_conv_b"]
    dt_bias = inputs["L0_attn_dt_bias"]
    A_log = inputs["L0_attn_A_log"]
    Dv = inputs["L0_attn_D"]
    gnorm_w = inputs["L0_attn_gnorm_w"]
    out_proj = inputs["L0_attn_out_proj_w"]
    ln1 = inputs["L0_attn_ln1_w"]
    ln2 = inputs["L0_attn_ln2_w"]
    h128, bd16, bd64, ident, ut, negm, icol32, hsel = make_consts()
    f32 = np.float32
    com = {
        "gate_w": np.ascontiguousarray(inputs["L0_attn_gate_w"]),
        "up_w": np.ascontiguousarray(inputs["L0_attn_up_w"]),
        "down_w": np.ascontiguousarray(inputs["L0_attn_down_w"]),
        "wup": np.ascontiguousarray(inputs["L0_ffn_W_up"]),
        "wgate": np.ascontiguousarray(inputs["L0_ffn_W_gate"]),
        "wdown": np.ascontiguousarray(inputs["L0_ffn_W_down"]),
        "su_ugI": np.ascontiguousarray(inputs["L0_ffn_SU_ug"].reshape(16, 128).T),
        "su_dpI": np.ascontiguousarray(inputs["L0_ffn_SU_dp"].reshape(64, 128).T),
        "su1I": np.ascontiguousarray(inputs["L0_ffn_Wscale_ug"][:8192].reshape(64, 128).T),
        "su2I": np.ascontiguousarray(inputs["L0_ffn_Wscale_ug"][8192:].reshape(64, 128).T),
        "sdbigI": np.ascontiguousarray(np.broadcast_to(inputs["L0_ffn_Wscale_dp"][None, :], (128, 2048))),
        "ln1wI": np.ascontiguousarray(ln1.reshape(16, 128).T),
        "ln2wI": np.ascontiguousarray(ln2.reshape(16, 128).T),

        "h128I": h128, "bd16I": bd16, "bd64I": bd64, "identI": ident,
        "utI": ut, "negmaskI": negm, "icol32I": icol32, "hsel128I": hsel,
    }
    com = {k: np.asarray(v, f32) for k, v in com.items()}
    maps = []
    for core in range(8):
        g, j = core // 4, core % 4
        w_in_slice = np.concatenate([
            w_in[:, j * 1024:(j + 1) * 1024],
            w_in[:, 4096 + j * 1024:4096 + (j + 1) * 1024],
            w_in[:, 8192:8448],
            w_in[:, 8448 + 16 * j:8448 + 16 * (j + 1)],
        ], axis=1)
        ch = np.concatenate([np.arange(j * 1024, (j + 1) * 1024),
                             np.arange(4096, 4352)])
        cw = conv_w[ch].reshape(10, 128, 4).transpose(1, 0, 2)
        cb = conv_b[ch].reshape(10, 128).T
        Ds = Dv[16 * j:16 * (j + 1)]
        dbig = np.empty((128, 8), f32)
        for jj in range(8):
            dbig[0:64, jj] = Ds[2 * jj]
            dbig[64:128, jj] = Ds[2 * jj + 1]
        m = dict(com)
        m.update({
            "xb": np.ascontiguousarray(x[g]),
            "x_own": np.ascontiguousarray(x[g, j * 1024:(j + 1) * 1024]),
            "w_in_s": np.ascontiguousarray(w_in_slice),
            "conv_w_s": np.ascontiguousarray(cw),
            "conv_b_s": np.ascontiguousarray(cb),
            "dtb": np.ascontiguousarray(dt_bias[16 * j:16 * (j + 1)].reshape(16, 1)),
            "alog": np.ascontiguousarray(A_log[16 * j:16 * (j + 1)].reshape(16, 1)),
            "dbigI": dbig,
            "wout_s": np.ascontiguousarray(out_proj[j * 1024:(j + 1) * 1024]),
            "gnwI": np.ascontiguousarray(gnorm_w[j * 1024:(j + 1) * 1024].reshape(8, 128).T),
        })
        maps.append({k: np.asarray(v, f32) for k, v in m.items()})
    return maps


def build(dbg=(), upto="full"):
    b = Build(dbg)
    b.declare()
    with ExitStack() as ctx:
        b.alloc(ctx)
        b.p0()
        b.p1()
        if upto in ("p2a", "p2b", "full"):
            b.p2a()
        if upto in ("p2b", "full"):
            b.p2prep()
            b.wtrans()
            b.p2b()
        if upto == "full":
            b.p2c()
            b.p3()
            b.p4a()
            b.p4b()
            b.pfin()
        else:
            b.finalize_dev()
        b.emit(ctx)
    return b



def kernel(**inputs):
    maps = prep_inputs(inputs)
    b = build(dbg=())
    res = run_bass_kernel_spmd(b.nc, maps, core_ids=list(range(8)))
    outs = [np.asarray(res.results[c]["out"], np.float32) for c in range(8)]
    g0 = np.concatenate(outs[0:4], axis=0)
    g1 = np.concatenate(outs[4:8], axis=0)
    return np.stack([g0, g1], axis=0)


# version 03
# speedup vs baseline: 1.2211x; 1.2211x over previous
import numpy as np
import ml_dtypes
from contextlib import ExitStack
import concourse.bass as bass
import concourse.mybir as mybir
from concourse.bass_utils import run_bass_kernel_spmd

F32 = mybir.dt.float32
BF16 = mybir.dt.bfloat16
AF = mybir.ActivationFunctionType
ALU = mybir.AluOpType
AX = mybir.AxisListType
I32 = mybir.dt.int32

EPS = 1e-5
T = 4096
D = 2048
TOK = 1024
NCH = 32
QL = 128


def had(n):
    h = np.array([[1.0]], np.float32)
    while h.shape[0] < n:
        h = np.block([[h, h], [h, -h]])
    return h


class Ctr:
    def __init__(self, ctx, nc, name):
        self.sem = ctx.enter_context(nc.semaphore(name))
        self.n = 0

    def inc(self, k=1):
        self.n += k
        return self.n


class Build:

    def __init__(self, dbg=()):
        self.dbg = set(dbg)
        self.nc = bass.Bass("TRN2", target_bir_lowering=False)
        self.ops = {k: [] for k in "gvaty"}

    def em(self, eng, fn):
        self.ops[eng].append(fn)

    def dma(self, eng, out, in_, ctr=None, **kw):
        c = ctr if ctr is not None else {"g": self.dG, "y": self.dS, "a": self.dA}[eng]
        tgt = c.inc(16)
        self.em(eng, lambda e, o=out, i=in_, k=dict(kw): e.dma_start(out=o, in_=i, **k).then_inc(c.sem, 16))
        return tgt

    def wait(self, eng, ctr, tgt):
        if tgt > 0:
            self.em(eng, lambda e, t=tgt: e.wait_ge(ctr.sem, t))

    def op(self, eng, fn):
        c = {"v": self.sV, "a": self.sA, "t": self.sP, "g": self.sG}[eng]
        tgt = c.inc(1)
        self.em(eng, lambda e, f=fn: f(e).then_inc(c.sem, 1))
        return tgt

    def dop(self, eng, fn):
        c = {"v": self.sV, "a": self.sA, "t": self.sP, "g": self.sG}[eng]
        self.wait(eng, c, c.n)
        return self.op(eng, fn)

    def barrier(self, name=""):
        snap = [(c, c.n) for c in self.ctrs]
        for eng in "gvaty":
            for c, n in snap:
                self.wait(eng, c, n)

    def dbg_copy(self, name, src_ap, shape, dtype):
        if name not in self.dbg:
            return
        t = self.nc.dram_tensor("dbg_" + name, shape, dtype, kind="ExternalOutput").ap()
        self.dma("y", t, src_ap, ctr=self.STO)

    def declare(self):
        nc = self.nc
        dr = lambda n, s, d, k: nc.dram_tensor(n, s, d, kind=k).ap()
        I, O = "ExternalInput", "ExternalOutput"
        self.xb = dr("xb", [T, D], F32, I)
        self.x_own = dr("x_own", [TOK, D], F32, I)
        self.w_in_s = dr("w_in_s", [D, 2320], F32, I)
        self.conv_w_s = dr("conv_w_s", [128, 10, 4], F32, I)
        self.conv_b_s = dr("conv_b_s", [128, 10], F32, I)
        self.dtb = dr("dtb", [16, 1], F32, I)
        self.alog = dr("alog", [16, 1], F32, I)
        self.dbigI = dr("dbigI", [128, 8], F32, I)
        self.ln1wI = dr("ln1wI", [128, 16], F32, I)
        self.ln2wI = dr("ln2wI", [128, 16], F32, I)
        self.gnwI = dr("gnwI", [128, 8], F32, I)
        self.wout_s = dr("wout_s", [1024, D], F32, I)
        self.gate_w = dr("gate_w", [D, 8192], F32, I)
        self.up_w = dr("up_w", [D, 8192], F32, I)
        self.down_w = dr("down_w", [8192, D], F32, I)
        self.wupTI = dr("wupTI", [D, 8192], F32, I)
        self.wgateTI = dr("wgateTI", [D, 8192], F32, I)
        self.wdownTI = dr("wdownTI", [8192, D], F32, I)
        self.su_ugI = dr("su_ugI", [128, 16], F32, I)
        self.su_dpI = dr("su_dpI", [128, 64], F32, I)
        self.su1I = dr("su1I", [128, 64], F32, I)
        self.su2I = dr("su2I", [128, 64], F32, I)
        self.sdbigI = dr("sdbigI", [128, D], F32, I)
        self.h128I = dr("h128I", [128, 128], F32, I)
        self.bd16I = dr("bd16I", [128, 128], F32, I)
        self.bd64I = dr("bd64I", [128, 128], F32, I)
        self.identI = dr("identI", [128, 128], F32, I)
        self.utI = dr("utI", [128, 128], F32, I)
        self.negmaskI = dr("negmaskI", [128, 128], F32, I)
        self.icol32I = dr("icol32I", [32, 32], F32, I)
        self.hsel128I = dr("hsel128I", [16, 2048], F32, I)
        self.outT = dr("out", [TOK, D], F32, O)

        dint = lambda n, s, d: nc.dram_tensor(n, s, d, kind="Internal")
        self.xrscr = dint("xrscr", [T, D], BF16)
        self.statdram = dint("statdram", [32, 128, 16], F32)
        self.dtdram = dint("dtdram", [16, T], F32)
        self.zscr = dint("zscr", [128, 8, T], BF16)
        self.cumdram = dint("cumdram", [16, T], F32)
        self.gamdram = dint("gamdram", [16, T], F32)
        self.rs_in = dint("rs_in", [T, D], F32)
        self.rs_out = dint("rs_out", [TOK, D], F32)
        self.ar_in = dint("ar_in", [1, T], F32)
        self.ar_out = dint("ar_out", [1, T], F32)
        self.wdownTbf = dint("wdownTbf", [8192, D], BF16)
        self.downAbf = dint("downAbf", [8192, D], BF16)
        self.h2nscr = dint("h2nscr", [TOK, D], BF16)
        self.xnownscr = dint("xnownscr", [TOK, D], BF16)
        self.s1scr = dint("s1scr", [D, TOK], BF16)
        self.s2scr = dint("s2scr", [D, TOK], BF16)
        self.s3scr = dint("s3scr", [8192, TOK], BF16)
        self.s4scr = dint("s4scr", [8192, TOK], BF16)

    def alloc(self, ctx):
        nc = self.nc
        E = ctx.enter_context
        sb = lambda n, s, d: E(nc.sbuf_tensor(n, s, d))
        self.STAT = sb("STAT", [128, 16], F32)
        self.EPSC = sb("EPSC", [128, 1], F32)
        self.ONE1C = sb("ONE1C", [128, 1], F32)
        self.ARE = sb("ARE", [128, 38912], BF16)
        self.BIG = sb("BIG", [128, 32768], BF16)
        self.MID = sb("MID", [128, 16384], BF16)
        self.XRS = sb("XRS", [128, 16, 512], BF16)
        self.Bt = sb("Bt", [128, T], BF16)
        self.Ct = sb("Ct", [128, T], BF16)
        self.H128 = sb("H128", [128, 128], BF16)
        self.BD16 = sb("BD16", [128, 128], BF16)
        self.BD64 = sb("BD64", [128, 128], BF16)
        self.IDB = sb("IDB", [128, 128], BF16)
        self.UT = sb("UT", [128, 128], F32)
        self.NEG = sb("NEG", [128, 128], F32)
        self.ICOL = sb("ICOL", [32, 32], F32)
        self.ONES32 = sb("ONES32", [32, 128], F32)
        self.ONESC = sb("ONESC", [128, 1], F32)
        self.ONESR = sb("ONESR", [1, 128], F32)
        self.CONVB = sb("CONVB", [128, 10], F32)
        self.LN1W = sb("LN1W", [128, 16], F32)
        self.LN2W = sb("LN2W", [128, 16], F32)
        self.DBIG = sb("DBIG", [128, 8], F32)
        self.DTB = sb("DTB", [16, 1], F32)
        self.ACOL = sb("ACOL", [16, 1], F32)
        self.ZCOL = sb("ZCOL", [16, 1], F32)
        self.SUU = sb("SUU", [128, 16], F32)
        self.AT = sb("AT", [32, 16], F32)
        self.ALPHAS = sb("ALPHAS", [128, 16], F32)

        A = self.ARE
        self.WIN16 = A[:, 0:16 * 2320].rearrange("p (k m) -> p k m", k=16)
        self.DT16 = A[0:16, 0:8192].bitcast(F32)
        self.LNDT = A[0:16, 8192:16384].bitcast(F32)
        self.CUM = A[0:16, 16384:24576].bitcast(F32)
        self.NB = A[0:16, 24576:32768].bitcast(F32)
        self.E2 = A[:, 0:2048]
        self.MT = A[:, 2048:4096]
        self.WTl = A[:, 4096:6144]
        self.GBC = A[:, 6144:8192]
        self.XTM = A[:, 8192:10240].rearrange("p (b n) -> p b n", b=2)
        self.BTM = A[:, 10240:10496].rearrange("p (b n) -> p b n", b=2)
        self.SZ = A[:, 10496:14592].rearrange("p (b j n) -> p b j n", b=2, j=8)
        self.CBTM = A[:, 14592:14720]
        self.XDD = A[:, 14720:15744]
        self.D2 = A[:, 16384:20480].bitcast(F32)
        self.O1 = A[:, 20480:22528].bitcast(F32)
        self.Zst = A[:, 22528:24576].bitcast(F32)
        self.RHS48 = A[0:48, 33280:37376].bitcast(F32)
        self.LHS48 = A[0:48, 37376:37632].bitcast(F32)
        XF = self.XRS.rearrange("p a b -> p (a b)")
        self.GNW = XF[:, 0:16].bitcast(F32)
        self.SUD = XF[:, 16:144].bitcast(F32)
        self.SU1 = XF[:, 144:272].bitcast(F32)
        self.SU2 = XF[:, 272:400].bitcast(F32)
        self.EVST = A[:, 0:8192].bitcast(F32).rearrange("p (b n) -> p b n", b=2)
        self.RGBC = A[:, 8192:12288]
        self.SQS = A[:, 12288:20480].bitcast(F32)
        self.SDB = A[:, 20480:24576].bitcast(F32)
        self.ROWA = A[0:1, 24576:32768].bitcast(F32)
        self.ROWB = A[0:1, 32768:36864].bitcast(F32)
        self.H2NS = A[:, 36864:38912].rearrange("p (b n) -> p b n", b=2)
        self.X2A = A[:, 8192:24576].rearrange("p (k m) -> p k m", k=64)
        self.WKT8 = A[:, 24576:32768]
        self.WKD = A[:, 32768:36864].rearrange("p (b n) -> p b n", b=2)
        self.XH3 = A[:, 0:4096].rearrange("p (k m) -> p k m", k=16)
        self.WUPT = A[:, 8192:24576].rearrange("p (k m) -> p k m", k=16)
        self.XU = A[:, 24576:38912 - 2048].rearrange("p (k m) -> p k m", k=48)

        self.XS = self.BIG.rearrange("p (j t) -> p j t", j=8)
        self.OUT = self.BIG[:, 0:32768].bitcast(F32).rearrange("p (j t) -> p j t", j=8)
        self.XTI = self.MID[:, 0:8192].bitcast(F32).rearrange("p (b n) -> p b n", b=2)
        self.XRO = self.MID[:, 8192:16384].rearrange("p (b n) -> p b n", b=4)
        self.CONVD = self.MID[:, 0:5120].rearrange("p (k m) -> p k m", k=40)
        self.DTST = self.MID[0:16, 5120:6144].bitcast(F32)
        self.CS = self.MID[:, 6144:11344].rearrange("p (a b) -> p a b", a=10)
        self.SST = self.MID[:, 11344:13392].rearrange("p (b n) -> p b n", b=4)
        self.SQ1 = self.XRS.rearrange("p a b -> p (a b)")[:, 0:4096].bitcast(F32)
        self.CWPARK = self.Bt[:, 0:80].bitcast(F32).rearrange("p (a b) -> p a b", a=10)
        self.XNOWN = self.MID[:, 0:16384].rearrange("p (j t) -> p j t", j=8)

        ps = lambda n, s, d: E(nc.psum_tensor(n, s, d))
        self.pA = ps("pA", [128, 2048], F32)
        self.pY = ps("pY", [128, 1024], F32)
        self.pXT = ps("pXT", [128, 1024], BF16)
        self.pS = ps("pS", [128, 512], F32)
        self.pB = self.pS[:, 0:128]
        self.pBT = self.pS[:, 128:192].bitcast(BF16)
        self.pAL = self.pS[:, 192:208]

        self.CSTy = Ctr(ctx, nc, "CSTy")
        self.CSTg = Ctr(ctx, nc, "CSTg")
        self.WLD = Ctr(ctx, nc, "WLD")
        self.XL = [Ctr(ctx, nc, f"XL{i}") for i in range(2)]
        self.XST = [Ctr(ctx, nc, f"XST{i}") for i in range(4)]
        self.RHT = Ctr(ctx, nc, "RHT")
        self.ZSTc = [Ctr(ctx, nc, f"ZST{i}") for i in range(4)]
        self.DTT = Ctr(ctx, nc, "DTT")
        self.STO = Ctr(ctx, nc, "STO")
        self.PRP = Ctr(ctx, nc, "PRP")
        self.FIN = Ctr(ctx, nc, "FIN")
        self.WCST = Ctr(ctx, nc, "WCST")
        self.WTA = Ctr(ctx, nc, "WTA")
        self.WTB = Ctr(ctx, nc, "WTB")
        self.PR2 = Ctr(ctx, nc, "PR2")
        self.RSQ = [Ctr(ctx, nc, f"RSQ{i}") for i in range(2)]
        self.WOBc = [Ctr(ctx, nc, f"WOB{i}") for i in range(2)]
        self.XOL = [Ctr(ctx, nc, f"XOL{i}") for i in range(2)]
        self.RSL = [Ctr(ctx, nc, f"RSL{i}") for i in range(2)]
        self.H2T = [Ctr(ctx, nc, f"H2T{i}") for i in range(2)]
        self.H2L = Ctr(ctx, nc, "H2L")
        self.WBL = [Ctr(ctx, nc, f"WBL{i}") for i in range(2)]
        self.DRL = [Ctr(ctx, nc, "DRL0")]
        self.DRL2 = Ctr(ctx, nc, "DRL2")
        self.SH1 = [Ctr(ctx, nc, f"SH1{i}") for i in range(2)]
        self.SH2 = [Ctr(ctx, nc, f"SH2{i}") for i in range(2)]
        self.GBL = [Ctr(ctx, nc, f"GBL{i}") for i in range(2)]
        self.SZL = [Ctr(ctx, nc, f"SZL{i}") for i in range(2)]
        self.dG = Ctr(ctx, nc, "dG")
        self.dS = Ctr(ctx, nc, "dS")
        self.dA = Ctr(ctx, nc, "dA")
        self.cc = Ctr(ctx, nc, "cc")
        self.sV = Ctr(ctx, nc, "sV")
        self.sA = Ctr(ctx, nc, "sA")
        self.sP = Ctr(ctx, nc, "sP")
        self.sG = Ctr(ctx, nc, "sG")
        self.ctrs = ([self.dG, self.dS, self.dA, self.cc, self.sV, self.sA, self.sP, self.sG,
                      self.CSTy, self.CSTg, self.WLD, self.RHT, self.DTT, self.STO,
                      self.PRP, self.FIN, self.WCST, self.WTA, self.WTB, self.PR2, self.H2L]
                     + self.XL + self.XST + self.ZSTc + self.GBL + self.SZL
                     + self.RSQ + self.WOBc + self.XOL + self.RSL + self.H2T
                     + self.WBL + self.DRL + [self.DRL2] + self.SH1 + self.SH2)

    def p0(self):
        for dst, src in [(self.H128, self.h128I), (self.BD16, self.bd16I),
                         (self.BD64, self.bd64I), (self.IDB, self.identI)]:
            self.dma("g", dst[:, :], src, ctr=self.CSTg)
        for dst, src in [(self.UT, self.utI),
                         (self.NEG, self.negmaskI), (self.ICOL, self.icol32I),
                         (self.CONVB, self.conv_b_s), (self.LN1W, self.ln1wI),
                         (self.LN2W, self.ln2wI),
                         (self.DBIG, self.dbigI), (self.DTB, self.dtb),
                         (self.SUU, self.su_ugI),
                         (self.ACOL, self.alog)]:
            self.dma("y", dst[:, :], src, ctr=self.CSTy)
        cwt = self.dma("y", self.CWPARK[:, :, :], self.conv_w_s, ctr=self.CSTy)
        gld, sld = self.CSTg.n, self.CSTy.n
        for eng in "va":
            self.wait(eng, self.CSTg, gld)
            self.wait(eng, self.CSTy, sld)
        self.op("v", lambda e: e.memset(self.ONES32[:, :], 1.0))
        self.op("v", lambda e: e.memset(self.ONESC[:, :], 1.0))
        self.op("v", lambda e: e.memset(self.ONESR[:, :], 1.0))
        self.op("v", lambda e: e.memset(self.ZCOL[:, :], 0.0))
        self.op("v", lambda e: e.memset(self.EPSC[:, :], EPS))
        self.op("v", lambda e: e.memset(self.ONE1C[:, :], 1.0))
        self.op("v", lambda e: e.tensor_scalar_mul(self.SUU[:, :], self.SUU[:, :], 1.0 / np.sqrt(2048.0)))
        ae = self.op("a", lambda e: e.activation(self.ACOL[:, :], self.ACOL[:, :], AF.Exp))
        self.wait("v", self.sA, ae)
        self.op("v", lambda e: e.tensor_scalar_mul(self.ACOL[:, :], self.ACOL[:, :], -1.0))
        self.ev_p0_v = self.sV.n
        self.ev_p0_a = self.sA.n

    def p1(self):
        xb = self.xb.rearrange("(n p) d -> n p d", p=128)
        xrw = self.xrscr.ap().rearrange("(n p) d -> n p d", p=128)
        S = self.STAT
        self.xr_tile_done = {}
        xro_free = {}
        for eng in "va":
            self.wait(eng, self.sV, self.ev_p0_v)
        for i in range(getattr(self, 'p1_tiles', 32)):
            b = i % 2
            ob = i % 4
            if i >= 2:
                self.wait("y", self.sV, self.xti_used[i - 2])
            ld = self.dma("y", self.XTI[:, b, :], xb[i], ctr=self.XL[b])
            X = self.XTI[:, b, :]
            self.wait("v", self.XL[b], ld)
            self.wait("a", self.XL[b], ld)
            SQ1 = self.SQ1[:, 0:2048]
            self.op("v", lambda e, X=X: e.tensor_reduce(S[:, 0:1], X, AX.X, ALU.add))
            sq = self.op("a", lambda e, X=X, SQ1=SQ1: e.activation(SQ1, X, AF.Square))
            self.wait("v", self.sA, sq)
            self.op("v", lambda e, SQ1=SQ1: e.tensor_reduce(S[:, 1:2], SQ1, AX.X, ALU.add))
            self.dop("v", lambda e: e.tensor_scalar_mul(S[:, 2:3], S[:, 0:1], 1.0 / D))
            self.op("v", lambda e: e.tensor_scalar_mul(S[:, 3:4], S[:, 1:2], 1.0 / D))
            self.dop("v", lambda e: e.tensor_tensor(S[:, 4:5], S[:, 2:3], S[:, 2:3], ALU.mult))
            self.dop("v", lambda e: e.tensor_tensor(S[:, 4:5], S[:, 3:4], S[:, 4:5], ALU.subtract))
            self.dop("v", lambda e: e.tensor_scalar_add(S[:, 5:6], S[:, 4:5], EPS))
            vrec = self.dop("v", lambda e: e.reciprocal(S[:, 6:7], S[:, 5:6]))
            self.wait("a", self.sV, vrec)
            self.op("a", lambda e: e.activation(S[:, 7:8], S[:, 5:6], AF.Ln))
            r1 = self.dop("a", lambda e: e.activation(S[:, 7:8], S[:, 7:8], AF.Exp, scale=-0.5))
            self.wait("v", self.sA, r1)
            m2n = self.dop("v", lambda e: e.tensor_tensor(S[:, 8:9], S[:, 4:5], S[:, 6:7], ALU.mult))
            self.wait("a", self.sV, m2n)
            self.op("a", lambda e: e.activation(S[:, 9:10], S[:, 8:9], AF.Ln, bias=self.EPSC[:, 0:1]))
            r2 = self.dop("a", lambda e: e.activation(S[:, 9:10], S[:, 9:10], AF.Exp, scale=-0.5))
            self.wait("v", self.sA, r2)
            self.dop("v", lambda e: e.tensor_tensor(S[:, 10:11], S[:, 7:8], S[:, 9:10], ALU.mult))
            if i >= 4:
                self.wait("v", self.XST[ob], self.xr_tile_done[i - 4])
            xr_t = self.dop("v", lambda e, X=X, ob=ob: e.tensor_scalar(
                self.XRO[:, ob, :], X, S[:, 2:3], S[:, 10:11], ALU.subtract, ALU.mult))
            self.xti_used = getattr(self, "xti_used", {})
            self.xti_used[i] = xr_t
            self.wait("y", self.sV, xr_t)
            self.xr_tile_done[i] = self.dma("y", xrw[i], self.XRO[:, ob, :], ctr=self.XST[ob])
            if getattr(self, "p1_serial", False):
                self.barrier()
        self.ev_p1_done = [(c, c.n) for c in self.XST]

    def p2a(self):
        for k in range(16):
            self.dma("g", self.WIN16[:, k, :], self.w_in_s[k * 128:(k + 1) * 128, :], ctr=self.WLD)
        self.cast_dn1 = self.dma("g", self.wdownTbf.ap(), self.wdownTI, ctr=self.WCST)
        self.cast_dn2 = self.dma("g", self.downAbf.ap(), self.down_w, ctr=self.WCST)
        self.wait("v", self.WLD, self.WLD.n)
        for k in range(16):
            self.op("v", lambda e, k=k: e.tensor_scalar_mul(
                self.WIN16[:, k, :], self.WIN16[:, k, :], self.LN1W[:, k:k + 1]))
        for c, n in self.ev_p1_done:
            self.wait("v", c, n)
        cw = self.CWPARK
        for j in range(10):
            for k in range(4):
                self.op("v", lambda e, j=j, k=k: e.tensor_scalar_mul(
                    self.CONVD[:, j * 4 + k, :], self.IDB[:, :], cw[:, j, k:k + 1]))
        self.ev_win = self.sV.n
        self.wait("t", self.sV, self.ev_win)

        slot_ap = [self.pA[:, 0:512], self.pA[:, 512:1024], self.pA[:, 1024:1536],
                   self.pA[:, 1536:2048], self.pY[:, 0:512], self.pY[:, 512:1024]]
        slot_busy = {}
        xrv = self.xrscr.ap().rearrange("(c t) d -> c t d", t=512)
        zv = self.zscr.ap()
        dt_ev = []
        self.conv_done_v = 0

        for tc in range(8):
            rhs_t = []
            if tc == 0:
                for c, n in self.ev_p1_done:
                    self.wait("y", c, n)
            if tc >= 1:
                self.wait("y", self.sP, self.rhs_used)
            for k in range(16):
                t = self.dma("y", self.XRS[:, k, :],
                             self.xrscr.ap()[tc * 512:(tc + 1) * 512, k * 128:(k + 1) * 128],
                             transpose=True, ctr=self.RHT)
                rhs_t.append(t)
            self.wait("t", self.RHT, rhs_t[-1])
            if tc == 0:
                hz = self.op("v", lambda e: e.memset(self.CS[:, :, 0:3], 0.0))
            else:
                self.wait("v", self.sA, self.cs_written_a)
                self.wait("v", self.sV, self.cs_written_v)
                hz = self.op("v", lambda e: e.tensor_copy(self.CS[:, :, 0:3], self.CS[:, :, 512:515]))
            self.wait("t", self.sV, hz)

            for m in range(19):
                sl = m % 6
                if sl in slot_busy:
                    eng_c, tgt = slot_busy[sl]
                    self.wait("t", eng_c, tgt)
                ncols = 16 if m == 18 else 128
                dst = slot_ap[sl][0:ncols, :]
                for k in range(16):
                    mm = lambda e, k=k, m=m, dst=dst, ncols=ncols: e.matmul(
                        dst, self.WIN16[:, k, m * 128:m * 128 + ncols],
                        self.XRS[:, k, :], start=(k == 0), stop=(k == 15))
                    if k < 15:
                        self.em("t", mm)
                    else:
                        mm_t = self.op("t", mm)
                if m < 8:
                    zb = m % 4
                    if (tc, m) not in [(0, 0), (0, 1), (0, 2), (0, 3)]:
                        self.wait("a", self.ZSTc[zb], self.zst_free.get(zb, 0))
                    self.wait("a", self.sP, mm_t)
                    ev = self.op("a", lambda e, sl=sl, zb=zb: e.activation(
                        self.SST[:, zb, :], slot_ap[sl][:, :], AF.Silu))
                    slot_busy[sl] = (self.sA, ev)
                    self.wait("y", self.sA, ev)
                    st = self.dma("y", zv[:, m, tc * 512:(tc + 1) * 512], self.SST[:, zb, :], ctr=self.ZSTc[zb])
                    self.zst_free = getattr(self, "zst_free", {})
                    self.zst_free[zb] = st
                elif m < 18:
                    self.wait("v", self.sP, mm_t)
                    ev = self.op("v", lambda e, sl=sl, j=m - 8: e.tensor_copy(
                        self.CS[:, j, 3:515], slot_ap[sl][:, :]))
                    slot_busy[sl] = (self.sV, ev)
                else:
                    if tc >= 1:
                        self.wait("v", self.DTT, self.dtst_free)
                    self.wait("v", self.sP, mm_t)
                    ev = self.op("v", lambda e, sl=sl: e.tensor_copy(
                        self.DTST[:, :], slot_ap[sl][0:16, :]))
                    slot_busy[sl] = (self.sV, ev)
                    self.wait("y", self.sV, ev)
                    self.dtst_free = self.dma("y", self.dtdram.ap()[:, tc * 512:(tc + 1) * 512], self.DTST[:, :], ctr=self.DTT)
                    dt_ev.append(ev)
            self.rhs_used = self.sP.n
            ev_all_v = self.sV.n
            ev_all_a = self.sA.n
            self.wait("t", self.sV, ev_all_v)
            self.wait("t", self.sA, ev_all_a)
            conv_ev_v, conv_ev_a = [], []
            for j in range(10):
                sl = 2 + (j % 2)
                eng_c, tgt = slot_busy.get(sl, (None, 0))
                if eng_c is not None:
                    self.wait("t", eng_c, tgt)
                for k in range(4):
                    mm = lambda e, j=j, k=k, sl=sl: e.matmul(
                        slot_ap[sl][:, :], self.CONVD[:, j * 4 + k, :],
                        self.CS[:, j, k:k + 512], start=(k == 0), stop=(k == 3))
                    if k < 3:
                        self.em("t", mm)
                    else:
                        mm_t = self.op("t", mm)
                self.wait("a", self.sP, mm_t)
                if j < 8:
                    dst = self.XS[:, j, tc * 512:(tc + 1) * 512]
                elif j == 8:
                    dst = self.Bt[:, tc * 512:(tc + 1) * 512]
                else:
                    dst = self.Ct[:, tc * 512:(tc + 1) * 512]
                ev = self.op("a", lambda e, sl=sl, j=j, dst=dst: e.activation(
                    dst, slot_ap[sl][:, :], AF.Silu, bias=self.CONVB[:, j:j + 1]))
                slot_busy[sl] = (self.sA, ev)
                conv_ev_a.append(ev)
            self.cs_written_v = ev_all_v
            self.cs_written_a = self.sA.n
        self.ev_p2a_mm = self.sP.n
        self.ev_p2a_conv = self.sA.n

        if self.dbg:
            self.barrier()
            self.dbg_copy("xs", self.XS[:, :, :], [128, 8, T], BF16)
            self.dbg_copy("B", self.Bt[:, :], [128, T], BF16)
            self.dbg_copy("C", self.Ct[:, :], [128, T], BF16)
            self.dbg_copy("dtraw", self.dtdram.ap(), [16, T], F32)
            self.dbg_copy("sz", self.zscr.ap(), [128, 8, T], BF16)
            self.dbg_copy("xr", self.xrscr.ap(), [T, D], BF16)
            self.dbg_copy("stats", self.statdram.ap(), [32, 128, 16], F32)

    def p2prep(self):
        M = self.MID
        self.ZB = M[:, 0:1024]
        self.GBC2 = M[:, 1024:5120].rearrange("p (b n) -> p b n", b=2)
        self.SELCB = M[:, 5120:5248]
        self.ATB = M[0:32, 5248:5264]
        self.ABUF = M[0:16, 5264:5296]
        self.ALPH32 = M[:, 5296:6320].bitcast(F32)
        self.BTMALL = M[:, 6320:10416]
        self.ONES32B = M[:, 10416:10544]
        self.pAL32 = self.pS[:, 0:512]

        DT, LN_, CU, NB = self.DT16, self.LNDT, self.CUM, self.NB
        self.wait("v", self.sP, self.ev_p2a_mm)
        self.wait("a", self.sP, self.ev_p2a_mm)
        self.wait("v", self.sA, self.ev_p2a_conv)
        self.wait("y", self.DTT, self.DTT.n)
        t_dt = self.dma("y", DT[:, :], self.dtdram.ap(), ctr=self.PRP)
        self.wait("y", self.sP, self.ev_p2a_mm)
        self.dma("y", self.RHS48[32:48, :], self.hsel128I, ctr=self.PRP)
        self.dma("y", self.GNW[:, :], self.gnwI, ctr=self.PRP)
        self.dma("y", self.SUD[:, :], self.su_dpI, ctr=self.PRP)
        self.dma("y", self.SU1[:, :], self.su1I, ctr=self.PRP)
        t_xtra = self.dma("y", self.SU2[:, :], self.su2I, ctr=self.PRP)
        self.wait("v", self.PRP, t_dt)
        self.wait("a", self.PRP, t_dt)
        self.dop("v", lambda e: e.tensor_scalar_add(DT[:, :], DT[:, :], self.DTB[:, 0:1]))
        self.dop("v", lambda e: e.tensor_scalar_max(CU[:, :], DT[:, :], 0.0))
        A_ = self.ARE
        LNI = A_[0:16, 8192:16384].bitcast(I32)
        DTI = A_[0:16, 0:8192].bitcast(I32)
        ab = self.op("v", lambda e: e.tensor_single_scalar(LNI, DTI, 0x7fffffff, ALU.bitwise_and))
        self.wait("a", self.sV, ab)
        self.op("a", lambda e: e.activation(LN_[:, :], LN_[:, :], AF.Exp, scale=-1.0))
        l1 = self.dop("a", lambda e: e.activation(LN_[:, :], LN_[:, :], AF.Ln, bias=self.ONE1C[0:16, 0:1]))
        self.wait("v", self.sA, l1)
        self.dop("v", lambda e: e.tensor_tensor(DT[:, :], CU[:, :], LN_[:, :], ALU.add))
        dt_done = self.sV.n
        self.wait("a", self.sV, dt_done)
        lndt0 = self.op("a", lambda e: e.activation(LN_[:, :], DT[:, :], AF.Ln))
        self.wait("v", self.sA, lndt0)
        lndt = self.dop("v", lambda e: e.tensor_scalar_max(LN_[:, :], LN_[:, :], -60.0))
        self.dop("v", lambda e: e.tensor_scalar_mul(CU[:, :], DT[:, :], self.ACOL[:, 0:1]))
        self.op("v", lambda e: e.memset(NB[:, :], 1.0))
        self.dop("v", lambda e: e.tensor_tensor_scan(DT[:, :], NB[:, :], CU[:, :], 0.0, ALU.mult, ALU.add))
        for c in range(NCH):
            col = self.ZCOL[:, 0:1] if c == 0 else DT[:, c * QL - 1:c * QL]
            fn = lambda e, c=c, col=col: e.tensor_scalar(
                CU[:, c * QL:(c + 1) * QL], DT[:, c * QL:(c + 1) * QL], col, None,
                ALU.subtract, ALU.bypass)
            if c == 0:
                self.dop("v", fn)
            else:
                self.op("v", fn)
        nb_t = self.dop("v", lambda e: e.tensor_tensor(NB[:, :], LN_[:, :], CU[:, :], ALU.subtract))
        self.wait("y", self.sV, nb_t)
        st_c = self.dma("y", self.cumdram.ap(), CU[:, :], ctr=self.PRP)
        self.wait("y", self.PRP, st_c)
        cf_t = self.dma("y", self.RHS48[0:32, :].rearrange("p (h t) -> p h t", h=16),
                        self.cumdram.ap().rearrange("h (c t) -> c h t", c=32), ctr=self.PRP)
        self.wait("a", self.PRP, st_c)
        self.wait("a", self.sV, nb_t)
        g_a = self.dop("a", lambda e: e.activation(CU[:, :], CU[:, :], AF.Exp))
        self.wait("y", self.sA, g_a)
        st_g = self.dma("y", self.gamdram.ap(), CU[:, :], ctr=self.PRP)
        self.wait("v", self.sA, g_a)
        self.op("v", lambda e: e.memset(self.ONES32B[:, :], 1.0))
        abf = self.dop("v", lambda e: e.tensor_copy(
            self.ABUF[:, :], CU[:, QL - 1::QL]))
        self.wait("t", self.sV, abf)
        tr = self.op("t", lambda e: e.transpose(self.pBT[0:32, 0:16], self.ABUF[:, :], self.IDB[0:16, 0:16]))
        self.wait("v", self.sP, tr)
        atb = self.dop("v", lambda e: e.tensor_copy(self.ATB[:, :], self.pBT[0:32, 0:16]))
        sel_ts = []
        for c in range(NCH):
            s_t = self.dop("v", lambda e, c=c: e.tensor_scalar_mul(
                self.SELCB[0:32, :], self.ONES32B[0:32, :], self.ICOL[:, c:c + 1]))
            self.wait("t", self.sV, s_t)
            mm = self.op("t", lambda e, c=c: e.matmul(
                self.pAL32[:, c * 16:(c + 1) * 16],
                self.SELCB[0:32, :], self.ATB[:, :], start=True, stop=True))
            self.wait("v", self.sP, mm)
        alph = self.dop("v", lambda e: e.tensor_copy(self.ALPH32[:, 0:512], self.pAL32[:, :]))
        for g8 in range(4):
            if g8 > 0:
                self.wait("t", self.sV, self.sV.n)
            for j in range(8):
                c = g8 * 8 + j
                mm = lambda e, c=c, j=j: e.transpose(
                    self.pXT[:, j * QL:(j + 1) * QL], self.Bt[:, c * QL:(c + 1) * QL], self.IDB[:, :])
                if j < 7:
                    self.em("t", mm)
                else:
                    tr8 = self.op("t", mm)
            self.wait("v", self.sP, tr8)
            self.op("v", lambda e, g8=g8: e.tensor_copy(
                self.BTMALL[:, g8 * 1024:(g8 + 1) * 1024], self.pXT[:, :]))
        z0 = self.op("v", lambda e: e.memset(self.Zst[:, :], 0.0))
        self.ev_prep_v = self.sV.n
        self.ev_prep_cf = cf_t
        self.ev_prep_gam = st_g

    def p2b(self):
        NEGB = self.NEG[:, :].unsqueeze(1).broadcast_to([128, 16, QL])
        zv = self.zscr.ap()
        for eng in "tva":
            self.wait(eng, self.sV, self.ev_prep_v)
        gb_free = {}
        sz_free = {}
        sz_t = {}
        gb_t = {}
        self.xtm_used = {}
        self.y_used = {}
        self.g_used = {}
        self.d2_used = {}
        for c in range(getattr(self, 'p2b_chunks', NCH)):
            b = c % 2
            if c % 2 == 0:
                sb = (c // 2) % 2
                if c == 0:
                    for zc in self.ZSTc:
                        self.wait("y", zc, zc.n)
                if c >= 4:
                    self.wait("y", self.sV, sz_free.get(sb, 0))
                sz_t[c // 2] = self.dma(
                    "y", self.SZ[:, sb, :, :],
                    zv[:, :, c * QL:(c + 2) * QL], ctr=self.SZL[sb])
            if c >= 2:
                self.wait("g", self.sV, gb_free.get(b, 0))
            if c == 0:
                self.wait("g", self.PRP, self.ev_prep_gam)
            gb_t[c] = self.dma(
                "g", self.GBC2[:, b, :].rearrange("p (h t) -> p h t", h=16),
                self.gamdram.ap().rearrange("h (c t) -> c h t", c=NCH)[c].unsqueeze(0).broadcast_to([128, 16, QL]),
                ctr=self.GBL[b])

            lh1 = self.dop("v", lambda e, c=c: e.tensor_scalar_mul(
                self.LHS48[0:32, :], self.ONES32[0:32, :], self.ICOL[:, c:c + 1]))
            lh2 = self.op("v", lambda e, c=c: e.tensor_copy(
                self.LHS48[32:48, :], self.NB[:, c * QL:(c + 1) * QL]))
            if c > 0:
                self.wait("v", self.sP, self.ymm_prev)
            zb_t = self.dop("v", lambda e: e.tensor_copy(self.ZB[:, :], self.Zst[:, :]))

            if c == 0:
                self.wait("t", self.PRP, self.ev_prep_cf)
            else:
                self.wait("t", self.sV, self.d2_used[c - 1])
                self.wait("t", self.sV, self.g_used[c - 1])
            self.wait("t", self.sV, lh2)
            for i in range(4):
                mm = lambda e, i=i: e.matmul(
                    self.pA[:, i * 512:(i + 1) * 512], self.LHS48[:, :],
                    self.RHS48[:, i * 512:(i + 1) * 512], start=True, stop=True)
                if i < 3:
                    self.em("t", mm)
                else:
                    dmm = self.op("t", mm)
            if c > 0:
                self.wait("t", self.sV, self.cbtm_ev)
            cbt = self.op("t", lambda e, c=c: e.matmul(
                self.pB[:, :], self.Bt[:, c * QL:(c + 1) * QL],
                self.Ct[:, c * QL:(c + 1) * QL], start=True, stop=True))
            if c > 0:
                self.wait("t", self.sV, self.xtm_ev)
            for j in range(8):
                mm = lambda e, c=c, j=j: e.transpose(
                    self.pXT[:, j * QL:(j + 1) * QL], self.XS[:, j, c * QL:(c + 1) * QL], self.IDB[:, :])
                if j < 7:
                    self.em("t", mm)
                else:
                    xtr = self.op("t", mm)

            self.wait("v", self.sP, dmm)
            d2t = self.dop("v", lambda e: e.tensor_tensor(
                self.D2[:, :].rearrange("p (a b) -> p a b", a=16), self.pA[:, :].rearrange("p (a b) -> p a b", a=16),
                NEGB, ALU.add))
            self.d2_used[c] = d2t
            self.wait("v", self.sP, cbt)
            self.cbtm_ev = self.op("v", lambda e: e.tensor_copy(self.CBTM[:, :], self.pB[:, :]))
            self.wait("v", self.sP, xtr)
            self.xtm_ev = self.op("v", lambda e, b=b: e.tensor_copy(self.XTM[:, b, :], self.pXT[:, :]))
            self.wait("a", self.sV, d2t)
            e2t = self.op("a", lambda e: e.activation(self.E2[:, :], self.D2[:, :], AF.Exp))
            self.wait("v", self.sA, e2t)
            mt_t = self.dop("v", lambda e: e.tensor_tensor(
                self.MT[:, :].rearrange("p (a b) -> p a b", a=16),
                self.E2[:, :].rearrange("p (a b) -> p a b", a=16),
                self.CBTM.unsqueeze(1).broadcast_to([128, 16, QL]), ALU.mult))
            self.wait("v", self.GBL[b], gb_t[c])
            wt_t = self.op("v", lambda e, c=c, b=b: e.tensor_tensor(
                self.WTl[:, :].rearrange("p (a b) -> p a b", a=16),
                self.GBC2[:, b, :].rearrange("p (a b) -> p a b", a=16),
                self.Ct[:, c * QL:(c + 1) * QL].unsqueeze(1).broadcast_to([128, 16, QL]), ALU.mult))
            gb_free[b] = wt_t
            self.op("v", lambda e: e.tensor_copy(self.STAT[:, :], self.E2[:, QL - 1::QL]))
            self.wait("v", self.sV, self.sV.n)
            for h in range(16):
                self.op("v", lambda e, h=h, b=b: e.tensor_scalar_mul(
                    self.XDD[:, h * 64:(h + 1) * 64], self.XTM[:, b, h * 64:(h + 1) * 64],
                    self.STAT[:, h:h + 1]))
            xdd_t = self.sV.n

            if c > 0:
                self.wait("t", self.sV, self.y_used[c - 1])
            self.wait("t", self.sV, mt_t)
            self.wait("t", self.sV, wt_t)
            self.wait("t", self.sV, zb_t)
            self.wait("t", self.sV, self.xtm_ev)
            for h in range(16):
                u, jj = h % 2, h // 2
                out = self.pY[u * 64:u * 64 + 64, jj * QL:(jj + 1) * QL]
                self.em("t", lambda e, h=h, b=b, out=out: e.matmul(
                    out, self.XTM[:, b, h * 64:(h + 1) * 64],
                    self.MT[:, h * QL:(h + 1) * QL], start=True, stop=False))
                mm = lambda e, h=h, out=out: e.matmul(
                    out, self.ZB[:, h * 64:(h + 1) * 64],
                    self.WTl[:, h * QL:(h + 1) * QL], start=False, stop=True)
                if h < 15:
                    self.em("t", mm)
                else:
                    ymm = self.op("t", mm)
                    self.ymm_prev = ymm
            self.wait("t", self.sV, d2t)
            self.wait("t", self.sV, xdd_t)
            for i in range(2):
                mm = lambda e, c=c, i=i: e.matmul(
                    self.pA[:, i * 512:(i + 1) * 512], self.BTMALL[:, c * QL:(c + 1) * QL],
                    self.XDD[:, i * 512:(i + 1) * 512], start=True, stop=True)
                if i < 1:
                    self.em("t", mm)
                else:
                    gmm = self.op("t", mm)

            sb = (c // 2) % 2
            self.wait("v", self.SZL[sb], sz_t[c // 2])
            self.wait("v", self.sP, ymm)
            for jj in range(8):
                self.op("v", lambda e, jj=jj, c=c: e.scalar_tensor_tensor(
                    self.O1[:, jj * QL:(jj + 1) * QL], self.XS[:, jj, c * QL:(c + 1) * QL],
                    self.DBIG[:, jj:jj + 1], self.pY[:, jj * QL:(jj + 1) * QL],
                    ALU.mult, ALU.add))
            self.wait("v", self.sV, self.sV.n)
            for jj in range(8):
                self.op("v", lambda e, jj=jj, c=c, sb=sb: e.tensor_tensor(
                    self.XS[:, jj, c * QL:(c + 1) * QL], self.O1[:, jj * QL:(jj + 1) * QL],
                    self.SZ[:, sb, jj, (c % 2) * QL:(c % 2) * QL + QL], ALU.mult))
            self.y_used[c] = self.sV.n
            if c % 2 == 1:
                sz_free[sb] = self.sV.n
            self.wait("v", self.sP, gmm)
            self.wait("v", self.sV, self.sV.n)
            for h in range(16):
                self.op("v", lambda e, h=h, c=c: e.scalar_tensor_tensor(
                    self.Zst[:, h * 64:(h + 1) * 64], self.Zst[:, h * 64:(h + 1) * 64],
                    self.ALPH32[:, c * 16 + h:c * 16 + h + 1], self.pA[:, h * 64:(h + 1) * 64],
                    ALU.mult, ALU.add))
            self.g_used[c] = self.sV.n
        self.ev_ssm_v = self.sV.n

        if self.dbg:
            self.barrier()
            self.dbg_copy("y2", self.XS[:, :, :], [128, 8, T], BF16)
            self.dbg_copy("dt", self.dtdram.ap(), [16, T], F32)
            self.dbg_copy("cum", self.cumdram.ap(), [16, T], F32)

    def p2c(self):
        A = self.ARE
        SQS = A[:, 12288:20480].bitcast(F32)
        TMP = A[:, 0:8192].bitcast(F32)
        ROWA = A[0:1, 24576:32768].bitcast(F32)
        RGBC = A[:, 8192:12288]
        self.wait("y", self.sP, self.ev_p2a_mm)
        self.dma("y", self.GNW[:, :], self.gnwI, ctr=self.PR2)
        self.dma("y", self.SUD[:, :], self.su_dpI, ctr=self.PR2)
        self.dma("y", self.SU1[:, :], self.su1I, ctr=self.PR2)
        g_ld = self.dma("y", self.SU2[:, :], self.su2I, ctr=self.PR2)
        self.ev_pr2 = g_ld
        for eng in "va":
            self.wait(eng, self.sV, self.ev_ssm_v)
            for zc in self.SZL:
                self.wait(eng, zc, zc.n)
        sq0 = self.op("a", lambda e: e.activation(SQS[:, :], self.XS[:, 0, :], AF.Square))
        prev_add = 0
        for jj in range(1, 8):
            if prev_add:
                self.wait("a", self.sV, prev_add)
            t = self.dop("a", lambda e, jj=jj: e.activation(TMP[:, :], self.XS[:, jj, :], AF.Square))
            self.wait("v", self.sA, t)
            prev_add = self.dop("v", lambda e: e.tensor_tensor(SQS[:, :], SQS[:, :], TMP[:, :], ALU.add))
        sq_v = self.sV.n
        if "sqs" in self.dbg:
            self.barrier()
            self.dbg_copy("sqs", SQS[:, :], [128, T], F32)
        self.wait("t", self.sV, sq_v)
        self.wait("t", self.sV, self.g_used[NCH - 1])
        for half in range(2):
            for i in range(4):
                mm = lambda e, half=half, i=i: e.matmul(
                    self.pA[0:1, i * 512:(i + 1) * 512], self.ONESC[:, 0:1],
                    SQS[:, half * 2048 + i * 512:half * 2048 + (i + 1) * 512],
                    start=True, stop=True)
                mmt = self.op("t", mm)
            self.wait("v", self.sP, mmt)
            self.dop("v", lambda e, half=half: e.tensor_copy(ROWA[:, half * 2048:(half + 1) * 2048], self.pA[0:1, 0:2048]))
            if half == 0:
                self.wait("t", self.sV, self.sV.n)
        row_v = self.sV.n
        self.wait("y", self.sV, row_v)
        ar_st = self.dma("y", self.ar_in.ap(), ROWA[:, :], ctr=self.PR2)
        self.wait("g", self.PR2, ar_st)
        self.em("g", lambda e: e.collective_compute(
            "AllReduce", ALU.add, replica_groups=[[0, 1, 2, 3], [4, 5, 6, 7]],
            ins=[self.ar_in.ap().opt()], outs=[self.ar_out.ap().opt()]).then_inc(self.cc.sem))
        self.cc.inc(1)
        self.wait("y", self.cc, self.cc.n)
        ar_ld = self.dma("y", ROWA[:, :], self.ar_out.ap(), ctr=self.PR2)
        self.wait("a", self.PR2, ar_ld)
        self.dop("a", lambda e: e.activation(ROWA[:, :], ROWA[:, :], AF.Ln, bias=self.EPSC[0:1, 0:1], scale=1.0 / 4096.0))
        rg_a = self.dop("a", lambda e: e.activation(ROWA[:, :], ROWA[:, :], AF.Exp, scale=-0.5))
        self.wait("t", self.sA, rg_a)
        self.wait("t", self.sV, self.sV.n)
        for half in range(2):
            for i in range(4):
                mm = lambda e, half=half, i=i: e.matmul(
                    self.pA[:, i * 512:(i + 1) * 512], self.ONESR[0:1, :],
                    ROWA[:, half * 2048 + i * 512:half * 2048 + (i + 1) * 512],
                    start=True, stop=True)
                mmt = self.op("t", mm)
            self.wait("v", self.sP, mmt)
            ev = self.dop("v", lambda e, half=half: e.tensor_copy(RGBC[:, half * 2048:(half + 1) * 2048], self.pA[:, 0:2048]))
            self.wait("t", self.sV, ev)
        self.wait("v", self.PR2, g_ld)
        for jj in range(8):
            self.dop("v", lambda e, jj=jj: e.scalar_tensor_tensor(
                self.XS[:, jj, :], self.XS[:, jj, :], self.GNW[:, jj:jj + 1],
                RGBC[:, :], ALU.mult, ALU.mult))
        y2n_v = self.sV.n
        if "y2n" in self.dbg:
            self.barrier()
            self.dbg_copy("y2n", self.XS[:, :, :], [128, 8, T], BF16)
            self.dbg_copy("rgb", A[:, 8192:12288], [128, T], BF16)
        WOBv = A[:, 12288:20480].rearrange("p (b k m) -> p b k m", b=2, k=8)
        EV2 = A[:, 0:8192].bitcast(F32).rearrange("p (b n) -> p b n", b=2)
        wob_t = {}
        self.wait("t", self.sV, y2n_v)
        ev_done = {}
        st_done = {}
        wv = self.wout_s.rearrange("(k p) o -> p k o", p=128)
        mm_done_n = {}
        for n in range(4):
            if n >= 2:
                self.wait("g", self.sP, mm_done_n[n - 2])
            wob_t[n] = self.dma("g", WOBv[:, n % 2, :, :], wv[:, :, n * 512:(n + 1) * 512], ctr=self.WOBc[n % 2])
            self.wait("t", self.WOBc[n % 2], wob_t[n])
            for mt in range(32):
                sl = mt % 4
                if (n, mt) not in [(0, 0), (0, 1), (0, 2), (0, 3)]:
                    eng_c, tgt = ev_done.get(sl, (None, 0))
                    if eng_c is not None:
                        self.wait("t", eng_c, tgt)
                for kk in range(8):
                    mm = lambda e, n=n, mt=mt, kk=kk, sl=sl: e.matmul(
                        self.pA[:, sl * 512:(sl + 1) * 512],
                        self.XS[:, kk, mt * 128:(mt + 1) * 128],
                        WOBv[:, n % 2, kk, :], start=(kk == 0), stop=(kk == 7))
                    if kk < 7:
                        self.em("t", mm)
                    else:
                        mmt = self.op("t", mm)
                eb = mt % 2
                self.wait("v", self.RSQ[eb], st_done.get(eb, 0))
                self.wait("v", self.sP, mmt)
                ev = self.op("v", lambda e, sl=sl, eb=eb: e.tensor_copy(
                    EV2[:, eb, 0:512], self.pA[:, sl * 512:(sl + 1) * 512]))
                ev_done[sl] = (self.sV, ev)
                self.wait("y", self.sV, ev)
                st_done[eb] = self.dma("y", self.rs_in.ap()[mt * 128:(mt + 1) * 128, n * 512:(n + 1) * 512],
                                       EV2[:, eb, 0:512], ctr=self.RSQ[eb])
            mm_done_n[n] = self.sP.n
        for eb in range(2):
            self.wait("g", self.RSQ[eb], self.RSQ[eb].n)
        self.em("g", lambda e: e.collective_compute(
            "ReduceScatter", ALU.add, replica_groups=[[0, 1, 2, 3], [4, 5, 6, 7]],
            ins=[self.rs_in.ap().opt()], outs=[self.rs_out.ap().opt()]).then_inc(self.cc.sem))
        self.cc.inc(1)
        self.ev_rs = self.cc.n
        if self.dbg:
            self.barrier()
            self.dbg_copy("rsi", self.rs_in.ap(), [T, D], F32)
            self.dbg_copy("rso", self.rs_out.ap(), [TOK, D], F32)
            self.dbg_copy("arr", self.ar_out.ap(), [1, T], F32)
            self.dbg_copy("ari", self.ar_in.ap(), [1, T], F32)

    def p3(self):
        A = self.ARE
        S = self.STAT
        HT = A[:, 0:8192].bitcast(F32).rearrange("p (b n) -> p b n", b=2)
        XOT = A[:, 8192:16384].bitcast(F32).rearrange("p (b n) -> p b n", b=2)
        H2S = A[:, 24576:28672].rearrange("p (b n) -> p b n", b=2)
        XNS = A[:, 28672:32768].rearrange("p (b n) -> p b n", b=2)
        self.wait("y", self.cc, self.ev_rs)
        xov = self.x_own.rearrange("(n p) d -> n p d", p=128)
        rov = self.rs_out.ap().rearrange("(n p) d -> n p d", p=128)
        h2v = self.h2nscr.ap().rearrange("(n p) d -> n p d", p=128)
        xnv = self.xnownscr.ap().rearrange("(n p) d -> n p d", p=128)
        for mt in range(8):
            b = mt % 2
            if mt >= 2:
                self.wait("y", self.sV, self.p3_used[mt - 2])
            lx = self.dma("y", XOT[:, b, :], xov[mt], ctr=self.XOL[b])
            lr = self.dma("y", HT[:, b, :], rov[mt], ctr=self.RSL[b])
            self.wait("v", self.XOL[b], lx)
            self.wait("a", self.XOL[b], lx)
            X = XOT[:, b, :]
            self.op("v", lambda e, X=X: e.tensor_reduce(S[:, 0:1], X, AX.X, ALU.add))
            if getattr(self, "p3_sq_rd", 0):
                self.wait("a", self.sV, self.p3_sq_rd)
            sq = self.op("a", lambda e, X=X: e.activation(self.ARE[:, 33280:37376].bitcast(F32), X, AF.Square))
            self.wait("v", self.sA, sq)
            rd1 = self.op("v", lambda e: e.tensor_reduce(S[:, 1:2], self.ARE[:, 33280:37376].bitcast(F32), AX.X, ALU.add))
            self.dop("v", lambda e: e.tensor_scalar_mul(S[:, 2:3], S[:, 0:1], 1.0 / D))
            self.op("v", lambda e: e.tensor_scalar_mul(S[:, 3:4], S[:, 1:2], 1.0 / D))
            self.dop("v", lambda e: e.tensor_tensor(S[:, 4:5], S[:, 2:3], S[:, 2:3], ALU.mult))
            self.dop("v", lambda e: e.tensor_tensor(S[:, 4:5], S[:, 3:4], S[:, 4:5], ALU.subtract))
            va = self.dop("v", lambda e: e.tensor_scalar_add(S[:, 5:6], S[:, 4:5], EPS))
            self.wait("a", self.sV, va)
            self.op("a", lambda e: e.activation(S[:, 6:7], S[:, 5:6], AF.Ln))
            r1 = self.dop("a", lambda e: e.activation(S[:, 6:7], S[:, 6:7], AF.Exp, scale=-0.5))
            self.wait("v", self.sA, r1)
            xnt = self.dop("v", lambda e, X=X, b=b: e.tensor_scalar(
                XNS[:, b, :], X, S[:, 2:3], S[:, 6:7], ALU.subtract, ALU.mult))
            self.wait("v", self.RSL[b], lr)
            ht = self.dop("v", lambda e, b=b: e.tensor_tensor(HT[:, b, :], XNS[:, b, :], HT[:, b, :], ALU.add))
            ot = self.dop("v", lambda e, X=X, b=b, mt=mt: e.tensor_tensor(
                self.OUT[:, mt, :], X, HT[:, b, :], ALU.add))
            self.wait("a", self.sV, ot)
            sq2 = self.op("a", lambda e, b=b: e.activation(self.ARE[:, 33280:37376].bitcast(F32), HT[:, b, :], AF.Square))
            self.wait("v", self.sA, sq2)
            rd2 = self.op("v", lambda e: e.tensor_reduce(S[:, 8:9], self.ARE[:, 33280:37376].bitcast(F32), AX.X, ALU.add))
            self.p3_sq_rd = rd2
            r2v = self.dop("v", lambda e: e.tensor_scalar(S[:, 9:10], S[:, 8:9], 1.0 / D, EPS, ALU.mult, ALU.add))
            self.wait("a", self.sV, r2v)
            self.op("a", lambda e: e.activation(S[:, 10:11], S[:, 9:10], AF.Ln))
            r2 = self.dop("a", lambda e: e.activation(S[:, 10:11], S[:, 10:11], AF.Exp, scale=-0.5))
            self.wait("v", self.sA, r2)
            h2t = self.dop("v", lambda e, b=b: e.tensor_scalar_mul(H2S[:, b, :], HT[:, b, :], S[:, 10:11]))
            self.p3_used = getattr(self, "p3_used", {})
            self.p3_used[mt] = h2t
            self.wait("y", self.sV, h2t)
            self.dma("y", h2v[mt], H2S[:, b, :], ctr=self.H2T[b])
            self.dma("y", xnv[mt], XNS[:, b, :], ctr=self.H2T[b])
        self.ev_p3_v = self.sV.n
        self.ev_p3_h2 = [(c, c.n) for c in self.H2T]
        if self.dbg:
            self.barrier()
            self.dbg_copy("h2n", self.h2nscr.ap(), [TOK, D], BF16)
            self.dbg_copy("xno", self.xnownscr.ap(), [TOK, D], BF16)
            t3 = self.nc.dram_tensor("dbg_out3", [128, 8, D], F32, kind="ExternalOutput").ap()
            self.dma("y", t3, self.OUT[:, :, :], ctr=self.STO)

    def p4a(self):
        A = self.ARE
        H2F = self.MID[:, 0:16384].rearrange("p (k m) -> p k m", k=16)
        X2Q = A[:, 0:16384].rearrange("p (k m) -> p k m", k=16)
        WBK = A[:, 16384:32768].rearrange("p (s w k m) -> p s w k m", s=2, w=2, k=16)
        SG = A[:, 36864:38912].rearrange("p (b n) -> p b n", b=2)
        self.dr_used = 0
        self.pA_ev = {}
        for c, nn_ in self.ev_p3_h2:
            self.wait("y", c, nn_)
        for gc in self.GBL:
            self.wait("y", gc, gc.n)
        h2l = 0
        for k in range(16):
            h2l = self.dma("y", H2F[:, k, :], self.h2nscr.ap()[:, k * 128:(k + 1) * 128],
                           transpose=True, ctr=self.H2L)
        self.wait("v", self.H2L, h2l)
        self.wait("v", self.sV, self.ev_p3_v)
        for k in range(16):
            self.op("v", lambda e, k=k: e.tensor_scalar_mul(H2F[:, k, :], H2F[:, k, :], self.LN2W[:, k:k + 1]))
        h2s_v = self.sV.n
        self.wait("t", self.sV, h2s_v)
        gv = self.gate_w.rearrange("(k p) i -> p k i", p=128)
        uv = self.up_w.rearrange("(k p) i -> p k i", p=128)
        dv = self.downAbf.ap().rearrange("(k p) o -> p k o", p=128)
        self.wait("g", self.WCST, self.cast_dn2)
        if "h2f" in self.dbg:
            self.barrier()
            th = self.nc.dram_tensor("dbg_h2f", [128, 16, TOK], BF16, kind="ExternalOutput").ap()
            self.dma("y", th, H2F[:, :, :], ctr=self.STO)
        self.wait("g", self.sV, h2s_v)
        self._swiglu_common(H2F, X2Q, WBK, SG, gv, uv, dv, su=False)
        if "x2q" in self.dbg:
            self.barrier()
            tx = self.nc.dram_tensor("dbg_x2q", [128, 16, TOK], BF16, kind="ExternalOutput").ap()
            self.dma("y", tx, X2Q[:, :, :], ctr=self.STO)
        self.ev_p4a = self.sV.n
        self.ev_p4a_mm = self.sP.n
        if "out4a" in self.dbg:
            self.barrier()
            t4 = self.nc.dram_tensor("dbg_out4a", [128, 8, D], F32, kind="ExternalOutput").ap()
            self.dma("y", t4, self.OUT[:, :, :], ctr=self.STO)

    def _swiglu_common(self, H2F, X2Q, WBK, SG, gv, uv, dv, su):
        A = self.ARE
        wb_used = {}
        ev_pA = 0
        dq_mm = 0
        for q in range(4):
            x2_done = 0
            for ib in range(8):
                i0 = q * 16 + ib * 2
                slot = ib % 2
                if wb_used.get(slot, 0):
                    self.wait("g", self.sP, wb_used[slot])
                if ib < 2 and dq_mm:
                    self.wait("g", self.sP, dq_mm)
                tg = self.dma("g", WBK[:, slot, 0, :, :], gv[:, :, i0 * 128:(i0 + 2) * 128], ctr=self.WBL[slot])
                tu = self.dma("g", WBK[:, slot, 1, :, :], uv[:, :, i0 * 128:(i0 + 2) * 128], ctr=self.WBL[slot])
                self.wait("t", self.WBL[slot], tu)
                for ii in range(2):
                    if ev_pA:
                        self.wait("t", self.sV, ev_pA)
                    for w in range(2):
                        for n2 in range(2):
                            for k in range(16):
                                mm = lambda e, w=w, ii=ii, n2=n2, k=k, slot=slot: e.matmul(
                                    self.pA[:, (w * 2 + n2) * 512:(w * 2 + n2 + 1) * 512],
                                    WBK[:, slot, w, k, ii * 128:(ii + 1) * 128],
                                    H2F[:, k, n2 * 512:(n2 + 1) * 512],
                                    start=(k == 0), stop=(k == 15))
                                if k < 15 or w == 0:
                                    self.em("t", mm)
                                else:
                                    mmt = self.op("t", mm)
                    self.wait("v", self.sP, mmt)
                    sg_t = self.dop("v", lambda e: e.tensor_copy(SG[:, 0, :], self.pA[:, 0:1024]))
                    self.wait("a", self.sV, sg_t)
                    sl_t = self.op("a", lambda e: e.activation(SG[:, 1, :], SG[:, 0, :], AF.Silu))
                    self.wait("v", self.sA, sl_t)
                    x2_done = self.dop("v", lambda e, i=ib * 2 + ii: e.tensor_tensor(
                        X2Q[:, i, :], SG[:, 1, :], self.pA[:, 1024:2048], ALU.mult))
                    ev_pA = x2_done
                wb_used[slot] = self.sP.n
            q_mm = self.sP.n
            self.wait("t", self.sV, x2_done)
            DR8v = self.ARE[:, 22528:38912].rearrange("p (k m) -> p k m", k=8)
            for hf in range(2):
                if hf == 0:
                    self.wait("g", self.sP, q_mm)
                    self.wait("g", self.sV, x2_done)
                if self.dr_used:
                    self.wait("g", self.sV, self.dr_used)
                drt = self.dma("g", DR8v[:, :, :], dv[:, q * 16 + hf * 8:q * 16 + (hf + 1) * 8, :], ctr=self.DRL[0])
                self.wait("t", self.DRL[0], drt)
                for gg in range(8):
                    pend = []
                    for j in range(4):
                        pidx = gg * 4 + j
                        m, n = pidx // 4, pidx % 4
                        if self.pA_ev.get(j, 0):
                            self.wait("t", self.sV, self.pA_ev[j])
                        for k in range(8):
                            mm = lambda e, j=j, m=m, n=n, k=k, hf=hf: e.matmul(
                                self.pA[:, j * 512:(j + 1) * 512],
                                X2Q[:, hf * 8 + k, m * 128:(m + 1) * 128],
                                DR8v[:, k, n * 512:(n + 1) * 512],
                                start=(k == 0), stop=(k == 7))
                            if k < 7:
                                self.em("t", mm)
                            else:
                                mmt = self.op("t", mm)
                        pend.append((j, m, n, mmt))
                    for j, m, n, mmt in pend:
                        self.wait("v", self.sP, mmt)
                        ev = self.dop("v", lambda e, j=j, m=m, n=n: e.tensor_tensor(
                            self.OUT[:, m, n * 512:(n + 1) * 512], self.OUT[:, m, n * 512:(n + 1) * 512],
                            self.pA[:, j * 512:(j + 1) * 512], ALU.add))
                        self.pA_ev[j] = ev
                self.dr_used = self.sV.n
                dq_mm = self.sP.n

    def p4b(self):
        A = self.ARE
        XNF = self.MID[:, 0:16384].rearrange("p (k m) -> p k m", k=16)
        SG = A[:, 36864:38912].rearrange("p (b n) -> p b n", b=2)
        EVB = A[:, 32768:36864].rearrange("p (b n) -> p b n", b=2)
        for c, nn_ in self.ev_p3_h2:
            self.wait("y", c, nn_)
        self.wait("y", self.sV, self.ev_p4a)
        self.wait("y", self.sP, self.ev_p4a_mm)
        xl = 0
        for k in range(16):
            xl = self.dma("y", XNF[:, k, :], self.xnownscr.ap()[:, k * 128:(k + 1) * 128],
                          transpose=True, ctr=self.H2L)
        self.wait("v", self.H2L, xl)
        self.wait("v", self.sV, self.ev_p4a)
        for k in range(16):
            self.op("v", lambda e, k=k: e.tensor_scalar_mul(XNF[:, k, :], XNF[:, k, :], self.SUU[:, k:k + 1]))
        xh_v = self.sV.n
        self.wait("t", self.sV, xh_v)
        s1v = self.s1scr.ap().rearrange("(a o) t -> o a t", o=16)
        s2v = self.s2scr.ap().rearrange("(o j u) t -> j o u t", o=16, j=16, u=8)
        pyev = 0
        for k in range(16):
            if pyev:
                self.wait("t", self.sV, pyev)
            for n2 in range(2):
                mm = lambda e, k=k, n2=n2: e.matmul(
                    self.pY[:, n2 * 512:(n2 + 1) * 512], self.H128[:, :],
                    XNF[:, k, n2 * 512:(n2 + 1) * 512], start=True, stop=True)
                if n2 == 0:
                    self.em("t", mm)
                else:
                    mmt = self.op("t", mm)
            self.wait("v", self.sP, mmt)
            eb = k % 2
            if k >= 2:
                self.wait("v", self.SH1[eb], self.sh1_t.get(k - 2, 0))
            ev = self.dop("v", lambda e, eb=eb: e.tensor_copy(EVB[:, eb, 0:1024], self.pY[:, :]))
            pyev = ev
            self.wait("y", self.sV, ev)
            self.sh1_t = getattr(self, "sh1_t", {})
            self.sh1_t[k] = self.dma("y", s1v[k], EVB[:, eb, 0:1024], ctr=self.SH1[eb])
        ld_t = {}
        for k in range(16):
            eb = k % 2
            if k >= 2:
                self.wait("y", self.sP, self.sh2_mm.get(k - 2, 0))
            if k < 2:
                for kk in range(16):
                    self.wait("y", self.SH1[kk % 2], self.sh1_t[kk])
            if k >= 1:
                self.wait("y", self.SH1[(k - 1) % 2], self.sh3_t[k - 1])
            ld_t[k] = self.dma("y", EVB[:, eb, 0:1024], self.s1scr.ap()[k * 128:(k + 1) * 128, :], ctr=self.SH2[eb])
            self.wait("t", self.SH2[eb], ld_t[k])
            if pyev:
                self.wait("t", self.sV, pyev)
            for n2 in range(2):
                mm = lambda e, eb=eb, n2=n2: e.matmul(
                    self.pY[:, n2 * 512:(n2 + 1) * 512], self.BD16[:, :],
                    EVB[:, eb, n2 * 512:(n2 + 1) * 512], start=True, stop=True)
                if n2 == 0:
                    self.em("t", mm)
                else:
                    mmt = self.op("t", mm)
            self.sh2_mm = getattr(self, "sh2_mm", {})
            self.sh2_mm[k] = mmt
            self.wait("v", self.sP, mmt)
            ev = self.dop("v", lambda e, eb=eb: e.tensor_copy(EVB[:, 1 - eb, 0:1024], self.pY[:, :]))
            pyev = ev
            self.wait("y", self.sV, ev)
            self.sh3_t = getattr(self, "sh3_t", {})
            self.sh3_t[k] = self.dma("y", s2v[k], EVB[:, 1 - eb, 0:1024], ctr=self.SH1[eb])
        for kk in range(16):
            self.wait("y", self.SH1[kk % 2], self.sh3_t[kk])
        x3l = self.dma("y", XNF[:, :, :], self.s2scr.ap().rearrange("(k p) t -> p k t", p=128), ctr=self.H2L)
        self.wait("t", self.H2L, x3l)
        if "xh3" in self.dbg:
            self.barrier()
            self.dbg_copy("xh3", self.s2scr.ap(), [D, TOK], BF16)
        self.wait("g", self.sV, self.ev_p4a)
        self.wait("g", self.sP, self.ev_p4a_mm)
        self.wait("v", self.PR2, self.ev_pr2)
        self.op("v", lambda e: e.tensor_scalar_mul(self.SUD[:, :], self.SUD[:, :], 1.0 / np.sqrt(8192.0)))
        WB2 = A[:, 6144:22528].rearrange("p (s w k m) -> p s w k m", s=2, w=2, k=16)
        s3v = self.s3scr.ap().rearrange("(a o) t -> o a t", o=64)
        upv = self.wupTI.rearrange("(k p) i -> p k i", p=128)
        gvv = self.wgateTI.rearrange("(k p) i -> p k i", p=128)
        wb_used = {}
        pAev = 0
        sh4 = {}
        for it in range(64):
            ib, ii = it // 2, it % 2
            slot = ib % 2
            if ii == 0:
                if wb_used.get(slot, 0):
                    self.wait("g", self.sP, wb_used[slot])
                tg = self.dma("g", WB2[:, slot, 0, :, :], upv[:, :, ib * 256:(ib + 1) * 256], ctr=self.WBL[slot])
                tu = self.dma("g", WB2[:, slot, 1, :, :], gvv[:, :, ib * 256:(ib + 1) * 256], ctr=self.WBL[slot])
                self.wait("t", self.WBL[slot], tu)
            if pAev:
                self.wait("t", self.sV, pAev)
            for w in range(2):
                for n2 in range(2):
                    for k in range(16):
                        mm = lambda e, w=w, ii=ii, n2=n2, k=k, slot=slot: e.matmul(
                            self.pA[:, (w * 2 + n2) * 512:(w * 2 + n2 + 1) * 512],
                            WB2[:, slot, w, k, ii * 128:(ii + 1) * 128],
                            XNF[:, k, n2 * 512:(n2 + 1) * 512],
                            start=(k == 0), stop=(k == 15))
                        if k < 15 or w == 0:
                            self.em("t", mm)
                        else:
                            mmt = self.op("t", mm)
            if ii == 1:
                wb_used[slot] = self.sP.n
            self.wait("v", self.sP, mmt)
            ut = self.dop("v", lambda e, it=it: e.tensor_scalar_mul(SG[:, 0, :], self.pA[:, 0:1024], self.SU1[:, it:it + 1]))
            gt = self.op("v", lambda e, it=it: e.tensor_scalar_mul(SG[:, 1, :], self.pA[:, 1024:2048], self.SU2[:, it:it + 1]))
            pAev = gt
            self.wait("a", self.sV, gt)
            st = self.op("a", lambda e: e.activation(SG[:, 1, :], SG[:, 1, :], AF.Silu))
            self.wait("v", self.sA, st)
            x2t = self.dop("v", lambda e, it=it: e.scalar_tensor_tensor(
                SG[:, 0, :], SG[:, 1, :], self.SUD[:, it:it + 1], SG[:, 0, :], ALU.mult, ALU.mult))
            self.wait("t", self.sV, x2t)
            if sh4.get("pyev", 0):
                self.wait("t", self.sV, sh4["pyev"])
            for n2 in range(2):
                mm = lambda e, n2=n2: e.matmul(
                    self.pY[:, n2 * 512:(n2 + 1) * 512], self.H128[:, :],
                    SG[:, 0, n2 * 512:(n2 + 1) * 512], start=True, stop=True)
                if n2 == 0:
                    self.em("t", mm)
                else:
                    p1mm = self.op("t", mm)
            self.wait("v", self.sP, p1mm)
            eb = it % 2
            if it >= 2:
                self.wait("v", self.SH1[eb], sh4.get(it - 2, 0))
            ev = self.dop("v", lambda e, eb=eb: e.tensor_copy(EVB[:, eb, 0:1024], self.pY[:, :]))
            sh4["pyev"] = ev
            self.wait("y", self.sV, ev)
            sh4[it] = self.dma("y", s3v[it], EVB[:, eb, 0:1024], ctr=self.SH1[eb])
        s4v = self.s4scr.ap().rearrange("(o j u) t -> j o u t", o=64, j=64, u=2)
        pyev = sh4["pyev"]
        sh5 = {}
        for k in range(64):
            eb = k % 2
            if k < 2:
                for kk in range(64):
                    self.wait("y", self.SH1[kk % 2], sh4[kk])
            if k >= 2:
                self.wait("y", self.sP, sh5.get(("mm", k - 2), 0))
            if k >= 1:
                self.wait("y", self.SH1[(k - 1) % 2], sh5[k - 1])
            lt = self.dma("y", EVB[:, eb, 0:1024], self.s3scr.ap()[k * 128:(k + 1) * 128, :], ctr=self.SH2[eb])
            self.wait("t", self.SH2[eb], lt)
            if pyev:
                self.wait("t", self.sV, pyev)
            for n2 in range(2):
                mm = lambda e, eb=eb, n2=n2: e.matmul(
                    self.pY[:, n2 * 512:(n2 + 1) * 512], self.BD64[:, :],
                    EVB[:, eb, n2 * 512:(n2 + 1) * 512], start=True, stop=True)
                if n2 == 0:
                    self.em("t", mm)
                else:
                    mmt = self.op("t", mm)
            sh5[("mm", k)] = mmt
            self.wait("v", self.sP, mmt)
            ev = self.dop("v", lambda e, eb=eb: e.tensor_copy(EVB[:, 1 - eb, 0:1024], self.pY[:, :]))
            pyev = ev
            self.wait("y", self.sV, ev)
            sh5[k] = self.dma("y", s4v[k], EVB[:, 1 - eb, 0:1024], ctr=self.SH1[eb])
        if "x3f" in self.dbg:
            self.barrier()
            self.dbg_copy("x3f", self.s4scr.ap(), [8192, TOK], BF16)
        X3Q = self.MID[:, 0:16384].rearrange("p (k m) -> p k m", k=16)
        DR8v = A[:, 22528:38912].rearrange("p (k m) -> p k m", k=8)
        SDB = A[:, 0:4096].bitcast(F32)
        TSF = A[:, 4096:6144].bitcast(F32)
        sdl = self.dma("y", SDB[:, :], self.sdbigI, ctr=self.H2L)
        dnv = self.wdownTbf.ap().rearrange("(k p) o -> p k o", p=128)
        self.wait("y", self.WCST, self.cast_dn1)
        for kk in range(64):
            self.wait("y", self.SH1[kk % 2], sh5[kk])
        self.wait("v", self.H2L, sdl)
        dr_used = 0
        for q in range(4):
            if dr_used:
                self.wait("y", self.sV, dr_used)
            x3l = self.dma("y", X3Q[:, :, :],
                           self.s4scr.ap().rearrange("(k p) t -> p k t", p=128)[:, q * 16:(q + 1) * 16, :],
                           ctr=self.H2L)
            self.wait("t", self.H2L, x3l)
            for hf in range(2):
                if dr_used:
                    self.wait("y", self.sV, dr_used)
                drt = self.dma("y", DR8v[:, :, :], dnv[:, q * 16 + hf * 8:q * 16 + (hf + 1) * 8, :], ctr=self.DRL2)
                self.wait("t", self.DRL2, drt)
                for gg in range(8):
                    pend = []
                    for j in range(4):
                        pidx = gg * 4 + j
                        m, n = pidx // 4, pidx % 4
                        if self.pA_ev.get(j, 0):
                            self.wait("t", self.sV, self.pA_ev[j])
                        for k in range(8):
                            mm = lambda e, j=j, m=m, n=n, k=k, hf=hf: e.matmul(
                                self.pA[:, j * 512:(j + 1) * 512],
                                X3Q[:, hf * 8 + k, m * 128:(m + 1) * 128],
                                DR8v[:, k, n * 512:(n + 1) * 512],
                                start=(k == 0), stop=(k == 7))
                            if k < 7:
                                self.em("t", mm)
                            else:
                                mmt = self.op("t", mm)
                        pend.append((j, m, n, mmt))
                    for j, m, n, mmt in pend:
                        self.wait("v", self.sP, mmt)
                        t1 = self.dop("v", lambda e, j=j, n=n: e.tensor_tensor(
                            TSF[:, 0:512], self.pA[:, j * 512:(j + 1) * 512],
                            SDB[:, n * 512:(n + 1) * 512], ALU.mult))
                        ev = self.dop("v", lambda e, m=m, n=n: e.tensor_tensor(
                            self.OUT[:, m, n * 512:(n + 1) * 512], self.OUT[:, m, n * 512:(n + 1) * 512],
                            TSF[:, 0:512], ALU.add))
                        self.pA_ev[j] = ev
                dr_used = self.sV.n
        self.ev_p4b = self.sV.n

    def pfin(self):
        ov = self.outT.rearrange("(n p) d -> n p d", p=128)
        self.wait("y", self.sV, self.ev_p4b)
        for mt in range(8):
            self.dma("y", ov[mt], self.OUT[:, mt, :], ctr=self.FIN)
        self.barrier()

    def emit(self, ctx):
        block = ctx.enter_context(self.nc.Block())
        block.gpsimd(lambda e: [f(e) for f in self.ops["g"]])
        block.vector(lambda e: [f(e) for f in self.ops["v"]])
        block.scalar(lambda e: [f(e) for f in self.ops["a"]])
        block.tensor(lambda e: [f(e) for f in self.ops["t"]])
        block.sync(lambda e: [f(e) for f in self.ops["y"]])

    def finalize_dev(self):
        self.dma("g", self.outT, self.xrscr.ap()[0:TOK, :], ctr=self.FIN)
        self.barrier()


def make_consts():
    h128 = had(128)
    def permkron(O, U):
        hh = had(O)
        m = np.zeros((O * U, O * U), np.float32)
        for u in range(U):
            for o in range(O):
                for o2 in range(O):
                    m[u * O + o, o2 * U + u] = hh[o, o2]
        return m
    bd16 = permkron(16, 8)
    bd64 = permkron(64, 2)
    ident = np.eye(128, dtype=np.float32)
    ut = np.triu(np.ones((128, 128), np.float32))
    negm = np.where(ut > 0, 0.0, -1e30).astype(np.float32)
    icol32 = np.eye(32, dtype=np.float32)
    hsel = np.zeros((16, 2048), np.float32)
    for j in range(16):
        hsel[j, j * 128:(j + 1) * 128] = 1.0
    return h128, bd16, bd64, ident, ut, negm, icol32, hsel


def prep_inputs(inputs):
    x = inputs["L0_attn_hidden_states"]
    w_in = inputs["L0_attn_in_proj_w"]
    conv_w = inputs["L0_attn_conv_w"][:, 0, :]
    conv_b = inputs["L0_attn_conv_b"]
    dt_bias = inputs["L0_attn_dt_bias"]
    A_log = inputs["L0_attn_A_log"]
    Dv = inputs["L0_attn_D"]
    gnorm_w = inputs["L0_attn_gnorm_w"]
    out_proj = inputs["L0_attn_out_proj_w"]
    ln1 = inputs["L0_attn_ln1_w"]
    ln2 = inputs["L0_attn_ln2_w"]
    h128, bd16, bd64, ident, ut, negm, icol32, hsel = make_consts()
    f32 = np.float32
    com = {
        "gate_w": np.ascontiguousarray(inputs["L0_attn_gate_w"]),
        "up_w": np.ascontiguousarray(inputs["L0_attn_up_w"]),
        "down_w": np.ascontiguousarray(inputs["L0_attn_down_w"]),
        "wupTI": np.ascontiguousarray(inputs["L0_ffn_W_up"].T),
        "wgateTI": np.ascontiguousarray(inputs["L0_ffn_W_gate"].T),
        "wdownTI": np.ascontiguousarray(inputs["L0_ffn_W_down"].T),
        "su_ugI": np.ascontiguousarray(inputs["L0_ffn_SU_ug"].reshape(16, 128).T),
        "su_dpI": np.ascontiguousarray(inputs["L0_ffn_SU_dp"].reshape(64, 128).T),
        "su1I": np.ascontiguousarray(inputs["L0_ffn_Wscale_ug"][:8192].reshape(64, 128).T),
        "su2I": np.ascontiguousarray(inputs["L0_ffn_Wscale_ug"][8192:].reshape(64, 128).T),
        "sdbigI": np.ascontiguousarray(np.broadcast_to(inputs["L0_ffn_Wscale_dp"][None, :], (128, 2048))),
        "ln1wI": np.ascontiguousarray(ln1.reshape(16, 128).T),
        "ln2wI": np.ascontiguousarray(ln2.reshape(16, 128).T),

        "h128I": h128, "bd16I": bd16, "bd64I": bd64, "identI": ident,
        "utI": ut, "negmaskI": negm, "icol32I": icol32, "hsel128I": hsel,
    }
    com = {k: np.asarray(v, f32) for k, v in com.items()}
    maps = []
    for core in range(8):
        g, j = core // 4, core % 4
        w_in_slice = np.concatenate([
            w_in[:, j * 1024:(j + 1) * 1024],
            w_in[:, 4096 + j * 1024:4096 + (j + 1) * 1024],
            w_in[:, 8192:8448],
            w_in[:, 8448 + 16 * j:8448 + 16 * (j + 1)],
        ], axis=1)
        ch = np.concatenate([np.arange(j * 1024, (j + 1) * 1024),
                             np.arange(4096, 4352)])
        cw = conv_w[ch].reshape(10, 128, 4).transpose(1, 0, 2)
        cb = conv_b[ch].reshape(10, 128).T
        Ds = Dv[16 * j:16 * (j + 1)]
        dbig = np.empty((128, 8), f32)
        for jj in range(8):
            dbig[0:64, jj] = Ds[2 * jj]
            dbig[64:128, jj] = Ds[2 * jj + 1]
        m = dict(com)
        m.update({
            "xb": np.ascontiguousarray(x[g]),
            "x_own": np.ascontiguousarray(x[g, j * 1024:(j + 1) * 1024]),
            "w_in_s": np.ascontiguousarray(w_in_slice),
            "conv_w_s": np.ascontiguousarray(cw),
            "conv_b_s": np.ascontiguousarray(cb),
            "dtb": np.ascontiguousarray(dt_bias[16 * j:16 * (j + 1)].reshape(16, 1)),
            "alog": np.ascontiguousarray(A_log[16 * j:16 * (j + 1)].reshape(16, 1)),
            "dbigI": dbig,
            "wout_s": np.ascontiguousarray(out_proj[j * 1024:(j + 1) * 1024]),
            "gnwI": np.ascontiguousarray(gnorm_w[j * 1024:(j + 1) * 1024].reshape(8, 128).T),
        })
        maps.append({k: np.asarray(v, f32) for k, v in m.items()})
    return maps


def build(dbg=(), upto="full"):
    b = Build(dbg)
    b.declare()
    with ExitStack() as ctx:
        b.alloc(ctx)
        b.p0()
        b.p1()
        if upto in ("p2a", "p2b", "full"):
            b.p2a()
        if upto in ("p2b", "full"):
            b.p2prep()
            b.p2b()
        if upto == "full":
            b.p2c()
            b.p3()
            b.p4a()
            b.p4b()
            b.pfin()
        else:
            b.finalize_dev()
        b.emit(ctx)
    return b



def kernel(**inputs):
    maps = prep_inputs(inputs)
    b = build(dbg=())
    res = run_bass_kernel_spmd(b.nc, maps, core_ids=list(range(8)))
    outs = [np.asarray(res.results[c]["out"], np.float32) for c in range(8)]
    g0 = np.concatenate(outs[0:4], axis=0)
    g1 = np.concatenate(outs[4:8], axis=0)
    return np.stack([g0, g1], axis=0)


# version 06
# speedup vs baseline: 1.2591x; 1.0311x over previous
import numpy as np
import ml_dtypes
from contextlib import ExitStack
import concourse.bass as bass
import concourse.mybir as mybir
from concourse.bass_utils import run_bass_kernel_spmd

F32 = mybir.dt.float32
BF16 = mybir.dt.bfloat16
AF = mybir.ActivationFunctionType
ALU = mybir.AluOpType
AX = mybir.AxisListType
I32 = mybir.dt.int32

EPS = 1e-5
T = 4096
D = 2048
TOK = 1024
NCH = 32
QL = 128


def had(n):
    h = np.array([[1.0]], np.float32)
    while h.shape[0] < n:
        h = np.block([[h, h], [h, -h]])
    return h


class Ctr:
    def __init__(self, ctx, nc, name):
        self.sem = ctx.enter_context(nc.semaphore(name))
        self.n = 0

    def inc(self, k=1):
        self.n += k
        return self.n


class Build:

    def __init__(self, dbg=()):
        self.dbg = set(dbg)
        self.nc = bass.Bass("TRN2", target_bir_lowering=False)
        self.ops = {k: [] for k in "gvaty"}

    def em(self, eng, fn):
        self.ops[eng].append(fn)

    def dma(self, eng, out, in_, ctr=None, **kw):
        c = ctr if ctr is not None else {"g": self.dG, "y": self.dS, "a": self.dA}[eng]
        tgt = c.inc(16)
        self.em(eng, lambda e, o=out, i=in_, k=dict(kw): e.dma_start(out=o, in_=i, **k).then_inc(c.sem, 16))
        return tgt

    def wait(self, eng, ctr, tgt):
        if tgt > 0:
            self.em(eng, lambda e, t=tgt: e.wait_ge(ctr.sem, t))

    def op(self, eng, fn):
        c = {"v": self.sV, "a": self.sA, "t": self.sP, "g": self.sG}[eng]
        tgt = c.inc(1)
        self.em(eng, lambda e, f=fn: f(e).then_inc(c.sem, 1))
        return tgt

    def dop(self, eng, fn):
        c = {"v": self.sV, "a": self.sA, "t": self.sP, "g": self.sG}[eng]
        self.wait(eng, c, c.n)
        return self.op(eng, fn)

    def barrier(self, name=""):
        snap = [(c, c.n) for c in self.ctrs]
        for eng in "gvaty":
            for c, n in snap:
                self.wait(eng, c, n)

    def dbg_copy(self, name, src_ap, shape, dtype):
        if name not in self.dbg:
            return
        t = self.nc.dram_tensor("dbg_" + name, shape, dtype, kind="ExternalOutput").ap()
        self.dma("y", t, src_ap, ctr=self.STO)

    def declare(self):
        nc = self.nc
        dr = lambda n, s, d, k: nc.dram_tensor(n, s, d, kind=k).ap()
        I, O = "ExternalInput", "ExternalOutput"
        self.xb = dr("xb", [T, D], F32, I)
        self.x_own = dr("x_own", [TOK, D], F32, I)
        self.w_in_s = dr("w_in_s", [D, 2320], F32, I)
        self.conv_w_s = dr("conv_w_s", [128, 10, 4], F32, I)
        self.conv_b_s = dr("conv_b_s", [128, 10], F32, I)
        self.dtb = dr("dtb", [16, 1], F32, I)
        self.alog = dr("alog", [16, 1], F32, I)
        self.dbigI = dr("dbigI", [128, 8], F32, I)
        self.ln1wI = dr("ln1wI", [128, 16], F32, I)
        self.ln2wI = dr("ln2wI", [128, 16], F32, I)
        self.gnwI = dr("gnwI", [128, 8], F32, I)
        self.wout_s = dr("wout_s", [1024, D], F32, I)
        self.gate_w = dr("gate_w", [D, 8192], F32, I)
        self.up_w = dr("up_w", [D, 8192], F32, I)
        self.down_w = dr("down_w", [8192, D], F32, I)
        self.wupTI = dr("wupTI", [D, 8192], F32, I)
        self.wgateTI = dr("wgateTI", [D, 8192], F32, I)
        self.wdownTI = dr("wdownTI", [8192, D], F32, I)
        self.su_ugI = dr("su_ugI", [128, 16], F32, I)
        self.su_dpI = dr("su_dpI", [128, 64], F32, I)
        self.su1I = dr("su1I", [128, 64], F32, I)
        self.su2I = dr("su2I", [128, 64], F32, I)
        self.sdbigI = dr("sdbigI", [128, D], F32, I)
        self.h128I = dr("h128I", [128, 128], F32, I)
        self.bd16I = dr("bd16I", [128, 128], F32, I)
        self.bd64I = dr("bd64I", [128, 128], F32, I)
        self.identI = dr("identI", [128, 128], F32, I)
        self.utI = dr("utI", [128, 128], F32, I)
        self.negmaskI = dr("negmaskI", [128, 128], F32, I)
        self.icol32I = dr("icol32I", [32, 32], F32, I)
        self.hsel128I = dr("hsel128I", [16, 2048], F32, I)
        self.outT = dr("out", [TOK, D], F32, O)

        dint = lambda n, s, d: nc.dram_tensor(n, s, d, kind="Internal")
        self.xrscr = dint("xrscr", [T, D], BF16)
        self.statdram = dint("statdram", [32, 128, 16], F32)
        self.dtdram = dint("dtdram", [16, T], F32)
        self.zscr = dint("zscr", [128, 8, T], BF16)
        self.cumdram = dint("cumdram", [16, T], F32)
        self.gamdram = dint("gamdram", [16, T], F32)
        self.rs_in = dint("rs_in", [T, D], F32)
        self.rs_out = dint("rs_out", [TOK, D], F32)
        self.ar_in = dint("ar_in", [1, T], F32)
        self.ar_out = dint("ar_out", [1, T], F32)
        self.ar2_out = dint("ar2_out", [1, TOK], F32)
        self.wdownTbf = dint("wdownTbf", [8192, D], BF16)
        self.downAbf = dint("downAbf", [8192, D], BF16)
        self.h2nscr = dint("h2nscr", [TOK, D], BF16)
        self.xnownscr = dint("xnownscr", [TOK, D], BF16)
        self.s1scr = dint("s1scr", [D, TOK], BF16)
        self.s2scr = dint("s2scr", [D, TOK], BF16)
        self.s3scr = dint("s3scr", [8192, TOK], BF16)
        self.s4scr = dint("s4scr", [8192, TOK], BF16)

    def alloc(self, ctx):
        nc = self.nc
        E = ctx.enter_context
        sb = lambda n, s, d: E(nc.sbuf_tensor(n, s, d))
        self.STAT = sb("STAT", [128, 16], F32)
        self.EPSC = sb("EPSC", [128, 1], F32)
        self.ONE1C = sb("ONE1C", [128, 1], F32)
        self.ARE = sb("ARE", [128, 38912], BF16)
        self.BIG = sb("BIG", [128, 32768], BF16)
        self.MID = sb("MID", [128, 16384], BF16)
        self.XRS = sb("XRS", [128, 16, 512], BF16)
        self.Bt = sb("Bt", [128, T], BF16)
        self.Ct = sb("Ct", [128, T], BF16)
        self.H128 = sb("H128", [128, 128], BF16)
        self.BD16 = sb("BD16", [128, 128], BF16)
        self.BD64 = sb("BD64", [128, 128], BF16)
        self.IDB = sb("IDB", [128, 128], BF16)
        self.UT = sb("UT", [128, 128], F32)
        self.NEG = sb("NEG", [128, 128], F32)
        self.ICOL = sb("ICOL", [32, 32], F32)
        self.ONES32 = sb("ONES32", [32, 128], F32)
        self.ONESC = sb("ONESC", [128, 1], F32)
        self.ONESR = sb("ONESR", [1, 128], F32)
        self.CONVB = sb("CONVB", [128, 10], F32)
        self.LN1W = sb("LN1W", [128, 16], F32)
        self.LN2W = sb("LN2W", [128, 16], F32)
        self.DBIG = sb("DBIG", [128, 8], F32)
        self.DTB = sb("DTB", [16, 1], F32)
        self.ACOL = sb("ACOL", [16, 1], F32)
        self.ZCOL = sb("ZCOL", [16, 1], F32)
        self.SUU = sb("SUU", [128, 16], F32)
        self.AT = sb("AT", [32, 16], F32)
        self.ALPHAS = sb("ALPHAS", [128, 16], F32)

        A = self.ARE
        self.WIN16 = A[:, 0:16 * 2320].rearrange("p (k m) -> p k m", k=16)
        self.DT16 = A[0:16, 0:8192].bitcast(F32)
        self.LNDT = A[0:16, 8192:16384].bitcast(F32)
        self.CUM = A[0:16, 16384:24576].bitcast(F32)
        self.NB = A[0:16, 24576:32768].bitcast(F32)
        self.E2 = A[:, 0:2048]
        self.MT = A[:, 2048:4096]
        self.WTl = A[:, 4096:6144]
        self.GBC = A[:, 6144:8192]
        self.XTM = A[:, 8192:10240].rearrange("p (b n) -> p b n", b=2)
        self.BTM = A[:, 10240:10496].rearrange("p (b n) -> p b n", b=2)
        self.SZ = A[:, 10496:14592].rearrange("p (b j n) -> p b j n", b=2, j=8)
        self.CBTM = A[:, 14592:14720]
        self.XDD = A[:, 14720:15744]
        self.D2 = A[:, 16384:20480].bitcast(F32)
        self.O1 = A[:, 20480:22528].bitcast(F32)
        self.Zst = A[:, 22528:24576].bitcast(F32)
        self.RHS48 = A[0:48, 33280:37376].bitcast(F32)
        self.LHS48 = A[0:48, 37376:37632].bitcast(F32)
        XF = self.XRS.rearrange("p a b -> p (a b)")
        self.GNW = XF[:, 0:16].bitcast(F32)
        self.SUD = XF[:, 16:144].bitcast(F32)
        self.SU1 = XF[:, 144:272].bitcast(F32)
        self.SU2 = XF[:, 272:400].bitcast(F32)
        self.EVST = A[:, 0:8192].bitcast(F32).rearrange("p (b n) -> p b n", b=2)
        self.RGBC = A[:, 8192:12288]
        self.SQS = A[:, 12288:20480].bitcast(F32)
        self.SDB = A[:, 20480:24576].bitcast(F32)
        self.ROWA = A[0:1, 24576:32768].bitcast(F32)
        self.ROWB = A[0:1, 32768:36864].bitcast(F32)
        self.H2NS = A[:, 36864:38912].rearrange("p (b n) -> p b n", b=2)
        self.X2A = A[:, 8192:24576].rearrange("p (k m) -> p k m", k=64)
        self.WKT8 = A[:, 24576:32768]
        self.WKD = A[:, 32768:36864].rearrange("p (b n) -> p b n", b=2)
        self.XH3 = A[:, 0:4096].rearrange("p (k m) -> p k m", k=16)
        self.WUPT = A[:, 8192:24576].rearrange("p (k m) -> p k m", k=16)
        self.XU = A[:, 24576:38912 - 2048].rearrange("p (k m) -> p k m", k=48)

        self.XS = self.BIG.rearrange("p (j t) -> p j t", j=8)
        self.OUT = self.BIG[:, 0:32768].bitcast(F32).rearrange("p (j t) -> p j t", j=8)
        self.XTI = self.MID[:, 0:8192].bitcast(F32).rearrange("p (b n) -> p b n", b=2)
        self.XRO = self.MID[:, 8192:16384].rearrange("p (b n) -> p b n", b=4)
        self.CONVD = self.MID[:, 0:5120].rearrange("p (k m) -> p k m", k=40)
        self.DTST = self.MID[0:16, 5120:6144].bitcast(F32)
        self.CS = self.MID[:, 6144:11344].rearrange("p (a b) -> p a b", a=10)
        self.SST = self.MID[:, 11344:13392].rearrange("p (b n) -> p b n", b=4)
        self.SQ1 = self.XRS.rearrange("p a b -> p (a b)")[:, 0:4096].bitcast(F32)
        self.CWPARK = self.Bt[:, 0:80].bitcast(F32).rearrange("p (a b) -> p a b", a=10)
        self.XNOWN = self.MID[:, 0:16384].rearrange("p (j t) -> p j t", j=8)

        ps = lambda n, s, d: E(nc.psum_tensor(n, s, d))
        self.pA = ps("pA", [128, 2048], F32)
        self.pY = ps("pY", [128, 1024], F32)
        self.pXT = ps("pXT", [128, 1024], BF16)
        self.pS = ps("pS", [128, 512], F32)
        self.pB = self.pS[:, 0:128]
        self.pBT = self.pS[:, 128:192].bitcast(BF16)
        self.pAL = self.pS[:, 192:208]

        self.CSTy = Ctr(ctx, nc, "CSTy")
        self.CSTg = Ctr(ctx, nc, "CSTg")
        self.WLD = Ctr(ctx, nc, "WLD")
        self.XL = [Ctr(ctx, nc, f"XL{i}") for i in range(2)]
        self.XST = [Ctr(ctx, nc, f"XST{i}") for i in range(4)]
        self.RHT = Ctr(ctx, nc, "RHT")
        self.ZSTc = [Ctr(ctx, nc, f"ZST{i}") for i in range(4)]
        self.DTT = Ctr(ctx, nc, "DTT")
        self.STO = Ctr(ctx, nc, "STO")
        self.PRP = Ctr(ctx, nc, "PRP")
        self.FIN = Ctr(ctx, nc, "FIN")
        self.WCST = Ctr(ctx, nc, "WCST")
        self.WTA = Ctr(ctx, nc, "WTA")
        self.WTB = Ctr(ctx, nc, "WTB")
        self.PR2 = Ctr(ctx, nc, "PR2")
        self.RSQ = [Ctr(ctx, nc, f"RSQ{i}") for i in range(2)]
        self.WOBc = [Ctr(ctx, nc, f"WOB{i}") for i in range(2)]
        self.XOL = [Ctr(ctx, nc, f"XOL{i}") for i in range(2)]
        self.RSL = [Ctr(ctx, nc, f"RSL{i}") for i in range(2)]
        self.H2T = [Ctr(ctx, nc, f"H2T{i}") for i in range(2)]
        self.H2L = Ctr(ctx, nc, "H2L")
        self.WBL = [Ctr(ctx, nc, f"WBL{i}") for i in range(2)]
        self.DRL = [Ctr(ctx, nc, "DRL0")]
        self.DRL2 = Ctr(ctx, nc, "DRL2")
        self.SH1 = [Ctr(ctx, nc, f"SH1{i}") for i in range(2)]
        self.SH2 = [Ctr(ctx, nc, f"SH2{i}") for i in range(2)]
        self.GBL = [Ctr(ctx, nc, f"GBL{i}") for i in range(2)]
        self.SZL = [Ctr(ctx, nc, f"SZL{i}") for i in range(2)]
        self.dG = Ctr(ctx, nc, "dG")
        self.dS = Ctr(ctx, nc, "dS")
        self.dA = Ctr(ctx, nc, "dA")
        self.cc = Ctr(ctx, nc, "cc")
        self.sV = Ctr(ctx, nc, "sV")
        self.sA = Ctr(ctx, nc, "sA")
        self.sP = Ctr(ctx, nc, "sP")
        self.sG = Ctr(ctx, nc, "sG")
        self.ctrs = ([self.dG, self.dS, self.dA, self.cc, self.sV, self.sA, self.sP, self.sG,
                      self.CSTy, self.CSTg, self.WLD, self.RHT, self.DTT, self.STO,
                      self.PRP, self.FIN, self.WCST, self.WTA, self.WTB, self.PR2, self.H2L]
                     + self.XL + self.XST + self.ZSTc + self.GBL + self.SZL
                     + self.RSQ + self.WOBc + self.XOL + self.RSL + self.H2T
                     + self.WBL + self.DRL + [self.DRL2] + self.SH1 + self.SH2)

    def p0(self):
        for dst, src in [(self.H128, self.h128I), (self.BD16, self.bd16I),
                         (self.BD64, self.bd64I), (self.IDB, self.identI)]:
            self.dma("g", dst[:, :], src, ctr=self.CSTg)
        for dst, src in [(self.UT, self.utI),
                         (self.NEG, self.negmaskI), (self.ICOL, self.icol32I),
                         (self.CONVB, self.conv_b_s), (self.LN1W, self.ln1wI),
                         (self.LN2W, self.ln2wI),
                         (self.DBIG, self.dbigI), (self.DTB, self.dtb),
                         (self.SUU, self.su_ugI),
                         (self.ACOL, self.alog)]:
            self.dma("y", dst[:, :], src, ctr=self.CSTy)
        cwt = self.dma("y", self.CWPARK[:, :, :], self.conv_w_s, ctr=self.CSTy)
        gld, sld = self.CSTg.n, self.CSTy.n
        for eng in "va":
            self.wait(eng, self.CSTg, gld)
            self.wait(eng, self.CSTy, sld)
        self.op("v", lambda e: e.memset(self.ONES32[:, :], 1.0))
        self.op("v", lambda e: e.memset(self.ONESC[:, :], 1.0))
        self.op("v", lambda e: e.memset(self.ONESR[:, :], 1.0))
        self.op("v", lambda e: e.memset(self.ZCOL[:, :], 0.0))
        self.op("v", lambda e: e.memset(self.EPSC[:, :], EPS))
        self.op("v", lambda e: e.memset(self.ONE1C[:, :], 1.0))
        self.op("v", lambda e: e.tensor_scalar_mul(self.SUU[:, :], self.SUU[:, :], 1.0 / np.sqrt(2048.0)))
        ae = self.op("a", lambda e: e.activation(self.ACOL[:, :], self.ACOL[:, :], AF.Exp))
        self.wait("v", self.sA, ae)
        self.op("v", lambda e: e.tensor_scalar_mul(self.ACOL[:, :], self.ACOL[:, :], -1.0))
        self.ev_p0_v = self.sV.n
        self.ev_p0_a = self.sA.n

    def p1(self):
        xb = self.xb.rearrange("(n p) d -> n p d", p=128)
        xrw = self.xrscr.ap().rearrange("(n p) d -> n p d", p=128)
        S = self.STAT
        self.xr_tile_done = {}
        xro_free = {}
        for eng in "va":
            self.wait(eng, self.sV, self.ev_p0_v)
        for i in range(getattr(self, 'p1_tiles', 32)):
            b = i % 2
            ob = i % 4
            if i >= 2:
                self.wait("y", self.sV, self.xti_used[i - 2])
            ld = self.dma("y", self.XTI[:, b, :], xb[i], ctr=self.XL[b])
            X = self.XTI[:, b, :]
            self.wait("v", self.XL[b], ld)
            self.wait("a", self.XL[b], ld)
            SQ1 = self.SQ1[:, 0:2048]
            self.op("v", lambda e, X=X: e.tensor_reduce(S[:, 0:1], X, AX.X, ALU.add))
            sq = self.op("a", lambda e, X=X, SQ1=SQ1: e.activation(SQ1, X, AF.Square))
            self.wait("v", self.sA, sq)
            self.op("v", lambda e, SQ1=SQ1: e.tensor_reduce(S[:, 1:2], SQ1, AX.X, ALU.add))
            self.dop("v", lambda e: e.tensor_scalar_mul(S[:, 2:3], S[:, 0:1], 1.0 / D))
            self.op("v", lambda e: e.tensor_scalar_mul(S[:, 3:4], S[:, 1:2], 1.0 / D))
            self.dop("v", lambda e: e.tensor_tensor(S[:, 4:5], S[:, 2:3], S[:, 2:3], ALU.mult))
            self.dop("v", lambda e: e.tensor_tensor(S[:, 4:5], S[:, 3:4], S[:, 4:5], ALU.subtract))
            self.dop("v", lambda e: e.tensor_scalar_add(S[:, 5:6], S[:, 4:5], EPS))
            vrec = self.dop("v", lambda e: e.reciprocal(S[:, 6:7], S[:, 5:6]))
            self.wait("a", self.sV, vrec)
            self.op("a", lambda e: e.activation(S[:, 7:8], S[:, 5:6], AF.Ln))
            r1 = self.dop("a", lambda e: e.activation(S[:, 7:8], S[:, 7:8], AF.Exp, scale=-0.5))
            self.wait("v", self.sA, r1)
            m2n = self.dop("v", lambda e: e.tensor_tensor(S[:, 8:9], S[:, 4:5], S[:, 6:7], ALU.mult))
            self.wait("a", self.sV, m2n)
            self.op("a", lambda e: e.activation(S[:, 9:10], S[:, 8:9], AF.Ln, bias=self.EPSC[:, 0:1]))
            r2 = self.dop("a", lambda e: e.activation(S[:, 9:10], S[:, 9:10], AF.Exp, scale=-0.5))
            self.wait("v", self.sA, r2)
            self.dop("v", lambda e: e.tensor_tensor(S[:, 10:11], S[:, 7:8], S[:, 9:10], ALU.mult))
            if i >= 4:
                self.wait("v", self.XST[ob], self.xr_tile_done[i - 4])
            xr_t = self.dop("v", lambda e, X=X, ob=ob: e.tensor_scalar(
                self.XRO[:, ob, :], X, S[:, 2:3], S[:, 10:11], ALU.subtract, ALU.mult))
            self.xti_used = getattr(self, "xti_used", {})
            self.xti_used[i] = xr_t
            self.wait("y", self.sV, xr_t)
            self.xr_tile_done[i] = self.dma("y", xrw[i], self.XRO[:, ob, :], ctr=self.XST[ob])
            if getattr(self, "p1_serial", False):
                self.barrier()
        self.ev_p1_done = [(c, c.n) for c in self.XST]

    def p2a(self):
        for k in range(16):
            self.dma("g", self.WIN16[:, k, :], self.w_in_s[k * 128:(k + 1) * 128, :], ctr=self.WLD)
        self.cast_dn1 = self.dma("g", self.wdownTbf.ap(), self.wdownTI, ctr=self.WCST)
        self.cast_dn2 = self.dma("g", self.downAbf.ap(), self.down_w, ctr=self.WCST)
        self.wait("v", self.WLD, self.WLD.n)
        for k in range(16):
            self.op("v", lambda e, k=k: e.tensor_scalar_mul(
                self.WIN16[:, k, :], self.WIN16[:, k, :], self.LN1W[:, k:k + 1]))
        for c, n in self.ev_p1_done:
            self.wait("v", c, n)
        cw = self.CWPARK
        for j in range(10):
            for k in range(4):
                self.op("v", lambda e, j=j, k=k: e.tensor_scalar_mul(
                    self.CONVD[:, j * 4 + k, :], self.IDB[:, :], cw[:, j, k:k + 1]))
        self.ev_win = self.sV.n
        self.wait("t", self.sV, self.ev_win)

        slot_ap = [self.pA[:, 0:512], self.pA[:, 512:1024], self.pA[:, 1024:1536],
                   self.pA[:, 1536:2048], self.pY[:, 0:512], self.pY[:, 512:1024]]
        slot_busy = {}
        xrv = self.xrscr.ap().rearrange("(c t) d -> c t d", t=512)
        zv = self.zscr.ap()
        dt_ev = []
        self.conv_done_v = 0

        for tc in range(8):
            rhs_t = []
            if tc == 0:
                for c, n in self.ev_p1_done:
                    self.wait("y", c, n)
            if tc >= 1:
                self.wait("y", self.sP, self.rhs_used)
            for k in range(16):
                t = self.dma("y", self.XRS[:, k, :],
                             self.xrscr.ap()[tc * 512:(tc + 1) * 512, k * 128:(k + 1) * 128],
                             transpose=True, ctr=self.RHT)
                rhs_t.append(t)
            self.wait("t", self.RHT, rhs_t[-1])
            if tc == 0:
                hz = self.op("v", lambda e: e.memset(self.CS[:, :, 0:3], 0.0))
            else:
                self.wait("v", self.sA, self.cs_written_a)
                self.wait("v", self.sV, self.cs_written_v)
                hz = self.op("v", lambda e: e.tensor_copy(self.CS[:, :, 0:3], self.CS[:, :, 512:515]))
            self.wait("t", self.sV, hz)

            for m in range(19):
                sl = m % 6
                if sl in slot_busy:
                    eng_c, tgt = slot_busy[sl]
                    self.wait("t", eng_c, tgt)
                ncols = 16 if m == 18 else 128
                dst = slot_ap[sl][0:ncols, :]
                for k in range(16):
                    mm = lambda e, k=k, m=m, dst=dst, ncols=ncols: e.matmul(
                        dst, self.WIN16[:, k, m * 128:m * 128 + ncols],
                        self.XRS[:, k, :], start=(k == 0), stop=(k == 15))
                    if k < 15:
                        self.em("t", mm)
                    else:
                        mm_t = self.op("t", mm)
                if m < 8:
                    zb = m % 4
                    if (tc, m) not in [(0, 0), (0, 1), (0, 2), (0, 3)]:
                        self.wait("a", self.ZSTc[zb], self.zst_free.get(zb, 0))
                    self.wait("a", self.sP, mm_t)
                    ev = self.op("a", lambda e, sl=sl, zb=zb: e.activation(
                        self.SST[:, zb, :], slot_ap[sl][:, :], AF.Silu))
                    slot_busy[sl] = (self.sA, ev)
                    self.wait("y", self.sA, ev)
                    st = self.dma("y", zv[:, m, tc * 512:(tc + 1) * 512], self.SST[:, zb, :], ctr=self.ZSTc[zb])
                    self.zst_free = getattr(self, "zst_free", {})
                    self.zst_free[zb] = st
                elif m < 18:
                    self.wait("v", self.sP, mm_t)
                    ev = self.op("v", lambda e, sl=sl, j=m - 8: e.tensor_copy(
                        self.CS[:, j, 3:515], slot_ap[sl][:, :]))
                    slot_busy[sl] = (self.sV, ev)
                else:
                    if tc >= 1:
                        self.wait("v", self.DTT, self.dtst_free)
                    self.wait("v", self.sP, mm_t)
                    ev = self.op("v", lambda e, sl=sl: e.tensor_copy(
                        self.DTST[:, :], slot_ap[sl][0:16, :]))
                    slot_busy[sl] = (self.sV, ev)
                    self.wait("y", self.sV, ev)
                    self.dtst_free = self.dma("y", self.dtdram.ap()[:, tc * 512:(tc + 1) * 512], self.DTST[:, :], ctr=self.DTT)
                    dt_ev.append(ev)
            self.rhs_used = self.sP.n
            ev_all_v = self.sV.n
            ev_all_a = self.sA.n
            self.wait("t", self.sV, ev_all_v)
            self.wait("t", self.sA, ev_all_a)
            conv_ev_v, conv_ev_a = [], []
            for j in range(10):
                sl = 2 + (j % 2)
                eng_c, tgt = slot_busy.get(sl, (None, 0))
                if eng_c is not None:
                    self.wait("t", eng_c, tgt)
                for k in range(4):
                    mm = lambda e, j=j, k=k, sl=sl: e.matmul(
                        slot_ap[sl][:, :], self.CONVD[:, j * 4 + k, :],
                        self.CS[:, j, k:k + 512], start=(k == 0), stop=(k == 3))
                    if k < 3:
                        self.em("t", mm)
                    else:
                        mm_t = self.op("t", mm)
                self.wait("a", self.sP, mm_t)
                if j < 8:
                    dst = self.XS[:, j, tc * 512:(tc + 1) * 512]
                elif j == 8:
                    dst = self.Bt[:, tc * 512:(tc + 1) * 512]
                else:
                    dst = self.Ct[:, tc * 512:(tc + 1) * 512]
                ev = self.op("a", lambda e, sl=sl, j=j, dst=dst: e.activation(
                    dst, slot_ap[sl][:, :], AF.Silu, bias=self.CONVB[:, j:j + 1]))
                slot_busy[sl] = (self.sA, ev)
                conv_ev_a.append(ev)
            self.cs_written_v = ev_all_v
            self.cs_written_a = self.sA.n
        self.ev_p2a_mm = self.sP.n
        self.ev_p2a_conv = self.sA.n

        if self.dbg:
            self.barrier()
            self.dbg_copy("xs", self.XS[:, :, :], [128, 8, T], BF16)
            self.dbg_copy("B", self.Bt[:, :], [128, T], BF16)
            self.dbg_copy("C", self.Ct[:, :], [128, T], BF16)
            self.dbg_copy("dtraw", self.dtdram.ap(), [16, T], F32)
            self.dbg_copy("sz", self.zscr.ap(), [128, 8, T], BF16)
            self.dbg_copy("xr", self.xrscr.ap(), [T, D], BF16)
            self.dbg_copy("stats", self.statdram.ap(), [32, 128, 16], F32)

    def p2prep(self):
        M = self.MID
        self.ZB = M[:, 0:1024]
        self.GBC2 = M[:, 1024:5120].rearrange("p (b n) -> p b n", b=2)
        self.SELCB = M[:, 5120:5248]
        self.ATB = M[0:32, 5248:5264]
        self.ABUF = M[0:16, 5264:5296]
        self.ALPH32 = M[:, 5296:6320].bitcast(F32)
        self.BTMALL = M[:, 6320:10416]
        self.ONES32B = M[:, 10416:10544]
        self.pAL32 = self.pS[:, 0:512]

        DT, LN_, CU, NB = self.DT16, self.LNDT, self.CUM, self.NB
        self.wait("v", self.sP, self.ev_p2a_mm)
        self.wait("a", self.sP, self.ev_p2a_mm)
        self.wait("v", self.sA, self.ev_p2a_conv)
        self.wait("y", self.DTT, self.DTT.n)
        t_dt = self.dma("y", DT[:, :], self.dtdram.ap(), ctr=self.PRP)
        self.wait("y", self.sP, self.ev_p2a_mm)
        self.dma("y", self.RHS48[32:48, :], self.hsel128I, ctr=self.PRP)
        self.dma("y", self.GNW[:, :], self.gnwI, ctr=self.PRP)
        self.dma("y", self.SUD[:, :], self.su_dpI, ctr=self.PRP)
        self.dma("y", self.SU1[:, :], self.su1I, ctr=self.PRP)
        t_xtra = self.dma("y", self.SU2[:, :], self.su2I, ctr=self.PRP)
        self.wait("v", self.PRP, t_dt)
        self.wait("a", self.PRP, t_dt)
        self.dop("v", lambda e: e.tensor_scalar_add(DT[:, :], DT[:, :], self.DTB[:, 0:1]))
        self.dop("v", lambda e: e.tensor_scalar_max(CU[:, :], DT[:, :], 0.0))
        A_ = self.ARE
        LNI = A_[0:16, 8192:16384].bitcast(I32)
        DTI = A_[0:16, 0:8192].bitcast(I32)
        ab = self.op("v", lambda e: e.tensor_single_scalar(LNI, DTI, 0x7fffffff, ALU.bitwise_and))
        self.wait("a", self.sV, ab)
        self.op("a", lambda e: e.activation(LN_[:, :], LN_[:, :], AF.Exp, scale=-1.0))
        l1 = self.dop("a", lambda e: e.activation(LN_[:, :], LN_[:, :], AF.Ln, bias=self.ONE1C[0:16, 0:1]))
        self.wait("v", self.sA, l1)
        self.dop("v", lambda e: e.tensor_tensor(DT[:, :], CU[:, :], LN_[:, :], ALU.add))
        dt_done = self.sV.n
        self.wait("a", self.sV, dt_done)
        lndt0 = self.op("a", lambda e: e.activation(LN_[:, :], DT[:, :], AF.Ln))
        self.wait("v", self.sA, lndt0)
        lndt = self.dop("v", lambda e: e.tensor_scalar_max(LN_[:, :], LN_[:, :], -60.0))
        self.dop("v", lambda e: e.tensor_scalar_mul(CU[:, :], DT[:, :], self.ACOL[:, 0:1]))
        self.op("v", lambda e: e.memset(NB[:, :], 1.0))
        self.dop("v", lambda e: e.tensor_tensor_scan(DT[:, :], NB[:, :], CU[:, :], 0.0, ALU.mult, ALU.add))
        for c in range(NCH):
            col = self.ZCOL[:, 0:1] if c == 0 else DT[:, c * QL - 1:c * QL]
            fn = lambda e, c=c, col=col: e.tensor_scalar(
                CU[:, c * QL:(c + 1) * QL], DT[:, c * QL:(c + 1) * QL], col, None,
                ALU.subtract, ALU.bypass)
            if c == 0:
                self.dop("v", fn)
            else:
                self.op("v", fn)
        nb_t = self.dop("v", lambda e: e.tensor_tensor(NB[:, :], LN_[:, :], CU[:, :], ALU.subtract))
        self.wait("y", self.sV, nb_t)
        st_c = self.dma("y", self.cumdram.ap(), CU[:, :], ctr=self.PRP)
        self.wait("y", self.PRP, st_c)
        cf_t = self.dma("y", self.RHS48[0:32, :].rearrange("p (h t) -> p h t", h=16),
                        self.cumdram.ap().rearrange("h (c t) -> c h t", c=32), ctr=self.PRP)
        self.wait("a", self.PRP, st_c)
        self.wait("a", self.sV, nb_t)
        g_a = self.dop("a", lambda e: e.activation(CU[:, :], CU[:, :], AF.Exp))
        self.wait("y", self.sA, g_a)
        st_g = self.dma("y", self.gamdram.ap(), CU[:, :], ctr=self.PRP)
        self.wait("v", self.sA, g_a)
        self.op("v", lambda e: e.memset(self.ONES32B[:, :], 1.0))
        abf = self.dop("v", lambda e: e.tensor_copy(
            self.ABUF[:, :], CU[:, QL - 1::QL]))
        self.wait("t", self.sV, abf)
        tr = self.op("t", lambda e: e.transpose(self.pBT[0:32, 0:16], self.ABUF[:, :], self.IDB[0:16, 0:16]))
        self.wait("v", self.sP, tr)
        atb = self.dop("v", lambda e: e.tensor_copy(self.ATB[:, :], self.pBT[0:32, 0:16]))
        sel_ts = []
        for c in range(NCH):
            s_t = self.dop("v", lambda e, c=c: e.tensor_scalar_mul(
                self.SELCB[0:32, :], self.ONES32B[0:32, :], self.ICOL[:, c:c + 1]))
            self.wait("t", self.sV, s_t)
            mm = self.op("t", lambda e, c=c: e.matmul(
                self.pAL32[:, c * 16:(c + 1) * 16],
                self.SELCB[0:32, :], self.ATB[:, :], start=True, stop=True))
            self.wait("v", self.sP, mm)
        alph = self.dop("v", lambda e: e.tensor_copy(self.ALPH32[:, 0:512], self.pAL32[:, :]))
        for g8 in range(4):
            if g8 > 0:
                self.wait("t", self.sV, self.sV.n)
            for j in range(8):
                c = g8 * 8 + j
                mm = lambda e, c=c, j=j: e.transpose(
                    self.pXT[:, j * QL:(j + 1) * QL], self.Bt[:, c * QL:(c + 1) * QL], self.IDB[:, :])
                if j < 7:
                    self.em("t", mm)
                else:
                    tr8 = self.op("t", mm)
            self.wait("v", self.sP, tr8)
            self.op("v", lambda e, g8=g8: e.tensor_copy(
                self.BTMALL[:, g8 * 1024:(g8 + 1) * 1024], self.pXT[:, :]))
        z0 = self.op("v", lambda e: e.memset(self.Zst[:, :], 0.0))
        self.ev_prep_v = self.sV.n
        self.ev_prep_cf = cf_t
        self.ev_prep_gam = st_g

    def p2b(self):
        NEGB = self.NEG[:, :].unsqueeze(1).broadcast_to([128, 16, QL])
        zv = self.zscr.ap()
        for eng in "tva":
            self.wait(eng, self.sV, self.ev_prep_v)
        gb_free = {}
        sz_free = {}
        sz_t = {}
        gb_t = {}
        self.xtm_used = {}
        self.y_used = {}
        self.g_used = {}
        self.d2_used = {}
        for c in range(getattr(self, 'p2b_chunks', NCH)):
            b = c % 2
            if c % 2 == 0:
                sb = (c // 2) % 2
                if c == 0:
                    for zc in self.ZSTc:
                        self.wait("y", zc, zc.n)
                if c >= 4:
                    self.wait("y", self.sV, sz_free.get(sb, 0))
                sz_t[c // 2] = self.dma(
                    "y", self.SZ[:, sb, :, :],
                    zv[:, :, c * QL:(c + 2) * QL], ctr=self.SZL[sb])
            if c >= 2:
                self.wait("g", self.sV, gb_free.get(b, 0))
            if c == 0:
                self.wait("g", self.PRP, self.ev_prep_gam)
            gb_t[c] = self.dma(
                "g", self.GBC2[:, b, :].rearrange("p (h t) -> p h t", h=16),
                self.gamdram.ap().rearrange("h (c t) -> c h t", c=NCH)[c].unsqueeze(0).broadcast_to([128, 16, QL]),
                ctr=self.GBL[b])

            lh1 = self.dop("v", lambda e, c=c: e.tensor_scalar_mul(
                self.LHS48[0:32, :], self.ONES32[0:32, :], self.ICOL[:, c:c + 1]))
            lh2 = self.op("v", lambda e, c=c: e.tensor_copy(
                self.LHS48[32:48, :], self.NB[:, c * QL:(c + 1) * QL]))
            if c > 0:
                self.wait("v", self.sP, self.ymm_prev)
            zb_t = self.dop("v", lambda e: e.tensor_copy(self.ZB[:, :], self.Zst[:, :]))

            if c == 0:
                self.wait("t", self.PRP, self.ev_prep_cf)
            else:
                self.wait("t", self.sV, self.d2_used[c - 1])
                self.wait("t", self.sV, self.g_used[c - 1])
            self.wait("t", self.sV, lh2)
            for i in range(4):
                mm = lambda e, i=i: e.matmul(
                    self.pA[:, i * 512:(i + 1) * 512], self.LHS48[:, :],
                    self.RHS48[:, i * 512:(i + 1) * 512], start=True, stop=True)
                if i < 3:
                    self.em("t", mm)
                else:
                    dmm = self.op("t", mm)
            if c > 0:
                self.wait("t", self.sV, self.cbtm_ev)
            cbt = self.op("t", lambda e, c=c: e.matmul(
                self.pB[:, :], self.Bt[:, c * QL:(c + 1) * QL],
                self.Ct[:, c * QL:(c + 1) * QL], start=True, stop=True))
            if c > 0:
                self.wait("t", self.sV, self.xtm_ev)
            for j in range(8):
                mm = lambda e, c=c, j=j: e.transpose(
                    self.pXT[:, j * QL:(j + 1) * QL], self.XS[:, j, c * QL:(c + 1) * QL], self.IDB[:, :])
                if j < 7:
                    self.em("t", mm)
                else:
                    xtr = self.op("t", mm)

            self.wait("v", self.sP, dmm)
            d2t = self.dop("v", lambda e: e.tensor_tensor(
                self.D2[:, :].rearrange("p (a b) -> p a b", a=16), self.pA[:, :].rearrange("p (a b) -> p a b", a=16),
                NEGB, ALU.add))
            self.d2_used[c] = d2t
            self.wait("v", self.sP, cbt)
            self.cbtm_ev = self.op("v", lambda e: e.tensor_copy(self.CBTM[:, :], self.pB[:, :]))
            self.wait("v", self.sP, xtr)
            self.xtm_ev = self.op("v", lambda e, b=b: e.tensor_copy(self.XTM[:, b, :], self.pXT[:, :]))
            self.wait("a", self.sV, d2t)
            e2t = self.op("a", lambda e: e.activation(self.E2[:, :], self.D2[:, :], AF.Exp))
            self.wait("v", self.sA, e2t)
            mt_t = self.dop("v", lambda e: e.tensor_tensor(
                self.MT[:, :].rearrange("p (a b) -> p a b", a=16),
                self.E2[:, :].rearrange("p (a b) -> p a b", a=16),
                self.CBTM.unsqueeze(1).broadcast_to([128, 16, QL]), ALU.mult))
            self.wait("v", self.GBL[b], gb_t[c])
            wt_t = self.op("v", lambda e, c=c, b=b: e.tensor_tensor(
                self.WTl[:, :].rearrange("p (a b) -> p a b", a=16),
                self.GBC2[:, b, :].rearrange("p (a b) -> p a b", a=16),
                self.Ct[:, c * QL:(c + 1) * QL].unsqueeze(1).broadcast_to([128, 16, QL]), ALU.mult))
            gb_free[b] = wt_t
            self.op("v", lambda e: e.tensor_copy(self.STAT[:, :], self.E2[:, QL - 1::QL]))
            self.wait("v", self.sV, self.sV.n)
            for h in range(16):
                self.op("v", lambda e, h=h, b=b: e.tensor_scalar_mul(
                    self.XDD[:, h * 64:(h + 1) * 64], self.XTM[:, b, h * 64:(h + 1) * 64],
                    self.STAT[:, h:h + 1]))
            xdd_t = self.sV.n

            if c > 0:
                self.wait("t", self.sV, self.y_used[c - 1])
            self.wait("t", self.sV, mt_t)
            self.wait("t", self.sV, wt_t)
            self.wait("t", self.sV, zb_t)
            self.wait("t", self.sV, self.xtm_ev)
            for h in range(16):
                u, jj = h % 2, h // 2
                out = self.pY[u * 64:u * 64 + 64, jj * QL:(jj + 1) * QL]
                self.em("t", lambda e, h=h, b=b, out=out: e.matmul(
                    out, self.XTM[:, b, h * 64:(h + 1) * 64],
                    self.MT[:, h * QL:(h + 1) * QL], start=True, stop=False))
                mm = lambda e, h=h, out=out: e.matmul(
                    out, self.ZB[:, h * 64:(h + 1) * 64],
                    self.WTl[:, h * QL:(h + 1) * QL], start=False, stop=True)
                if h < 15:
                    self.em("t", mm)
                else:
                    ymm = self.op("t", mm)
                    self.ymm_prev = ymm
            self.wait("t", self.sV, d2t)
            self.wait("t", self.sV, xdd_t)
            for i in range(2):
                mm = lambda e, c=c, i=i: e.matmul(
                    self.pA[:, i * 512:(i + 1) * 512], self.BTMALL[:, c * QL:(c + 1) * QL],
                    self.XDD[:, i * 512:(i + 1) * 512], start=True, stop=True)
                if i < 1:
                    self.em("t", mm)
                else:
                    gmm = self.op("t", mm)

            sb = (c // 2) % 2
            self.wait("v", self.SZL[sb], sz_t[c // 2])
            self.wait("v", self.sP, ymm)
            for jj in range(8):
                self.op("v", lambda e, jj=jj, c=c: e.scalar_tensor_tensor(
                    self.O1[:, jj * QL:(jj + 1) * QL], self.XS[:, jj, c * QL:(c + 1) * QL],
                    self.DBIG[:, jj:jj + 1], self.pY[:, jj * QL:(jj + 1) * QL],
                    ALU.mult, ALU.add))
            self.wait("v", self.sV, self.sV.n)
            for jj in range(8):
                self.op("v", lambda e, jj=jj, c=c, sb=sb: e.tensor_tensor(
                    self.XS[:, jj, c * QL:(c + 1) * QL], self.O1[:, jj * QL:(jj + 1) * QL],
                    self.SZ[:, sb, jj, (c % 2) * QL:(c % 2) * QL + QL], ALU.mult))
            self.y_used[c] = self.sV.n
            if c % 2 == 1:
                sz_free[sb] = self.sV.n
            self.wait("v", self.sP, gmm)
            self.wait("v", self.sV, self.sV.n)
            for h in range(16):
                self.op("v", lambda e, h=h, c=c: e.scalar_tensor_tensor(
                    self.Zst[:, h * 64:(h + 1) * 64], self.Zst[:, h * 64:(h + 1) * 64],
                    self.ALPH32[:, c * 16 + h:c * 16 + h + 1], self.pA[:, h * 64:(h + 1) * 64],
                    ALU.mult, ALU.add))
            self.g_used[c] = self.sV.n
        self.ev_ssm_v = self.sV.n

        if self.dbg:
            self.barrier()
            self.dbg_copy("y2", self.XS[:, :, :], [128, 8, T], BF16)
            self.dbg_copy("dt", self.dtdram.ap(), [16, T], F32)
            self.dbg_copy("cum", self.cumdram.ap(), [16, T], F32)

    def p2c(self):
        A = self.ARE
        SQS = A[:, 12288:20480].bitcast(F32)
        TMP = A[:, 0:8192].bitcast(F32)
        ROWA = A[0:1, 24576:32768].bitcast(F32)
        RGBC = A[:, 8192:12288]
        self.wait("y", self.sP, self.ev_p2a_mm)
        self.dma("y", self.GNW[:, :], self.gnwI, ctr=self.PR2)
        self.dma("y", self.SUD[:, :], self.su_dpI, ctr=self.PR2)
        self.dma("y", self.SU1[:, :], self.su1I, ctr=self.PR2)
        g_ld = self.dma("y", self.SU2[:, :], self.su2I, ctr=self.PR2)
        self.ev_pr2 = g_ld
        for eng in "va":
            self.wait(eng, self.sV, self.ev_ssm_v)
            for zc in self.SZL:
                self.wait(eng, zc, zc.n)
        sq0 = self.op("a", lambda e: e.activation(SQS[:, :], self.XS[:, 0, :], AF.Square))
        prev_add = 0
        for jj in range(1, 8):
            if prev_add:
                self.wait("a", self.sV, prev_add)
            t = self.dop("a", lambda e, jj=jj: e.activation(TMP[:, :], self.XS[:, jj, :], AF.Square))
            self.wait("v", self.sA, t)
            prev_add = self.dop("v", lambda e: e.tensor_tensor(SQS[:, :], SQS[:, :], TMP[:, :], ALU.add))
        sq_v = self.sV.n
        if "sqs" in self.dbg:
            self.barrier()
            self.dbg_copy("sqs", SQS[:, :], [128, T], F32)
        self.wait("t", self.sV, sq_v)
        self.wait("t", self.sV, self.g_used[NCH - 1])
        for half in range(2):
            for i in range(4):
                mm = lambda e, half=half, i=i: e.matmul(
                    self.pA[0:1, i * 512:(i + 1) * 512], self.ONESC[:, 0:1],
                    SQS[:, half * 2048 + i * 512:half * 2048 + (i + 1) * 512],
                    start=True, stop=True)
                mmt = self.op("t", mm)
            self.wait("v", self.sP, mmt)
            self.dop("v", lambda e, half=half: e.tensor_copy(ROWA[:, half * 2048:(half + 1) * 2048], self.pA[0:1, 0:2048]))
            if half == 0:
                self.wait("t", self.sV, self.sV.n)
        row_v = self.sV.n
        sq_a_done = self.sA.n
        self.wait("y", self.sV, row_v)
        ar_st = self.dma("y", self.ar_in.ap(), ROWA[:, :], ctr=self.PR2)
        self.wait("g", self.PR2, ar_st)
        self.em("g", lambda e: e.collective_compute(
            "ReduceScatter", ALU.add, replica_groups=[[0, 1, 2, 3], [4, 5, 6, 7]],
            ins=[self.ar_in.ap().opt()], outs=[self.ar2_out.ap().opt()]).then_inc(self.cc.sem))
        self.cc.inc(1)
        self.wait("v", self.PR2, g_ld)
        self.wait("v", self.sA, sq_a_done)
        for jj in range(8):
            self.dop("v", lambda e, jj=jj: e.tensor_scalar_mul(
                self.XS[:, jj, :], self.XS[:, jj, :], self.GNW[:, jj:jj + 1]))
        y2n_v = self.sV.n
        WOBv = A[:, 12288:20480].rearrange("p (b k m) -> p b k m", b=2, k=8)
        EV2 = A[:, 0:8192].bitcast(F32).rearrange("p (b n) -> p b n", b=2)
        wob_t = {}
        self.wait("t", self.sV, y2n_v)
        ev_done = {}
        st_done = {}
        wv = self.wout_s.rearrange("(k p) o -> p k o", p=128)
        mm_done_n = {}
        for n in range(4):
            if n >= 2:
                self.wait("g", self.sP, mm_done_n[n - 2])
            wob_t[n] = self.dma("g", WOBv[:, n % 2, :, :], wv[:, :, n * 512:(n + 1) * 512], ctr=self.WOBc[n % 2])
            self.wait("t", self.WOBc[n % 2], wob_t[n])
            for mt in range(32):
                sl = mt % 4
                if (n, mt) not in [(0, 0), (0, 1), (0, 2), (0, 3)]:
                    eng_c, tgt = ev_done.get(sl, (None, 0))
                    if eng_c is not None:
                        self.wait("t", eng_c, tgt)
                for kk in range(8):
                    mm = lambda e, n=n, mt=mt, kk=kk, sl=sl: e.matmul(
                        self.pA[:, sl * 512:(sl + 1) * 512],
                        self.XS[:, kk, mt * 128:(mt + 1) * 128],
                        WOBv[:, n % 2, kk, :], start=(kk == 0), stop=(kk == 7))
                    if kk < 7:
                        self.em("t", mm)
                    else:
                        mmt = self.op("t", mm)
                eb = mt % 2
                self.wait("v", self.RSQ[eb], st_done.get(eb, 0))
                self.wait("v", self.sP, mmt)
                ev = self.op("v", lambda e, sl=sl, eb=eb: e.tensor_copy(
                    EV2[:, eb, 0:512], self.pA[:, sl * 512:(sl + 1) * 512]))
                ev_done[sl] = (self.sV, ev)
                self.wait("y", self.sV, ev)
                st_done[eb] = self.dma("y", self.rs_in.ap()[mt * 128:(mt + 1) * 128, n * 512:(n + 1) * 512],
                                       EV2[:, eb, 0:512], ctr=self.RSQ[eb])
            mm_done_n[n] = self.sP.n
        for eb in range(2):
            self.wait("g", self.RSQ[eb], self.RSQ[eb].n)
        self.em("g", lambda e: e.collective_compute(
            "ReduceScatter", ALU.add, replica_groups=[[0, 1, 2, 3], [4, 5, 6, 7]],
            ins=[self.rs_in.ap().opt()], outs=[self.rs_out.ap().opt()]).then_inc(self.cc.sem))
        self.cc.inc(1)
        self.ev_rs = self.cc.n
        if self.dbg:
            self.barrier()
            self.dbg_copy("rsi", self.rs_in.ap(), [T, D], F32)
            self.dbg_copy("rso", self.rs_out.ap(), [TOK, D], F32)
            self.dbg_copy("arr", self.ar_out.ap(), [1, T], F32)
            self.dbg_copy("ari", self.ar_in.ap(), [1, T], F32)

    def p3(self):
        A = self.ARE
        S = self.STAT
        HT = A[:, 0:8192].bitcast(F32).rearrange("p (b n) -> p b n", b=2)
        XOT = A[:, 8192:16384].bitcast(F32).rearrange("p (b n) -> p b n", b=2)
        H2S = A[:, 24576:28672].rearrange("p (b n) -> p b n", b=2)
        XNS = A[:, 28672:32768].rearrange("p (b n) -> p b n", b=2)
        self.wait("y", self.cc, self.ev_rs)
        RR = A[0:1, 16384:18432].bitcast(F32)
        RRB = A[0:1, 18432:19456]
        RGT8 = A[:, 20480:20496].bitcast(F32)
        rr_t = self.dma("y", RR[:, :], self.ar2_out.ap(), ctr=self.PR2)
        self.wait("a", self.PR2, rr_t)
        self.dop("a", lambda e: e.activation(RR[:, :], RR[:, :], AF.Ln, bias=self.EPSC[0:1, 0:1], scale=1.0 / 4096.0))
        rg_a = self.dop("a", lambda e: e.activation(RR[:, :], RR[:, :], AF.Exp, scale=-0.5))
        self.wait("t", self.sA, rg_a)
        for mt in range(8):
            tr = self.op("t", lambda e, mt=mt: e.transpose(
                self.pS[:, mt:mt + 1], RR[0:1, mt * 128:(mt + 1) * 128], self.ONESR[0:1, 0:1]))
        self.wait("v", self.sP, tr)
        self.dop("v", lambda e: e.tensor_copy(RGT8[:, :], self.pS[:, 0:8]))
        xov = self.x_own.rearrange("(n p) d -> n p d", p=128)
        rov = self.rs_out.ap().rearrange("(n p) d -> n p d", p=128)
        h2v = self.h2nscr.ap().rearrange("(n p) d -> n p d", p=128)
        xnv = self.xnownscr.ap().rearrange("(n p) d -> n p d", p=128)
        for mt in range(8):
            b = mt % 2
            if mt >= 2:
                self.wait("y", self.sV, self.p3_used[mt - 2])
            lx = self.dma("y", XOT[:, b, :], xov[mt], ctr=self.XOL[b])
            lr = self.dma("y", HT[:, b, :], rov[mt], ctr=self.RSL[b])
            self.wait("v", self.XOL[b], lx)
            self.wait("a", self.XOL[b], lx)
            X = XOT[:, b, :]
            self.op("v", lambda e, X=X: e.tensor_reduce(S[:, 0:1], X, AX.X, ALU.add))
            if getattr(self, "p3_sq_rd", 0):
                self.wait("a", self.sV, self.p3_sq_rd)
            sq = self.op("a", lambda e, X=X: e.activation(self.ARE[:, 33280:37376].bitcast(F32), X, AF.Square))
            self.wait("v", self.sA, sq)
            rd1 = self.op("v", lambda e: e.tensor_reduce(S[:, 1:2], self.ARE[:, 33280:37376].bitcast(F32), AX.X, ALU.add))
            self.dop("v", lambda e: e.tensor_scalar_mul(S[:, 2:3], S[:, 0:1], 1.0 / D))
            self.op("v", lambda e: e.tensor_scalar_mul(S[:, 3:4], S[:, 1:2], 1.0 / D))
            self.dop("v", lambda e: e.tensor_tensor(S[:, 4:5], S[:, 2:3], S[:, 2:3], ALU.mult))
            self.dop("v", lambda e: e.tensor_tensor(S[:, 4:5], S[:, 3:4], S[:, 4:5], ALU.subtract))
            va = self.dop("v", lambda e: e.tensor_scalar_add(S[:, 5:6], S[:, 4:5], EPS))
            self.wait("a", self.sV, va)
            self.op("a", lambda e: e.activation(S[:, 6:7], S[:, 5:6], AF.Ln))
            r1 = self.dop("a", lambda e: e.activation(S[:, 6:7], S[:, 6:7], AF.Exp, scale=-0.5))
            self.wait("v", self.sA, r1)
            xnt = self.dop("v", lambda e, X=X, b=b: e.tensor_scalar(
                XNS[:, b, :], X, S[:, 2:3], S[:, 6:7], ALU.subtract, ALU.mult))
            self.wait("v", self.RSL[b], lr)
            ht = self.dop("v", lambda e, b=b, mt=mt: e.scalar_tensor_tensor(
                HT[:, b, :], HT[:, b, :], RGT8[:, mt:mt + 1], XNS[:, b, :], ALU.mult, ALU.add))
            ot = self.dop("v", lambda e, X=X, b=b, mt=mt: e.tensor_tensor(
                self.OUT[:, mt, :], X, HT[:, b, :], ALU.add))
            self.wait("a", self.sV, ot)
            sq2 = self.op("a", lambda e, b=b: e.activation(self.ARE[:, 33280:37376].bitcast(F32), HT[:, b, :], AF.Square))
            self.wait("v", self.sA, sq2)
            rd2 = self.op("v", lambda e: e.tensor_reduce(S[:, 8:9], self.ARE[:, 33280:37376].bitcast(F32), AX.X, ALU.add))
            self.p3_sq_rd = rd2
            r2v = self.dop("v", lambda e: e.tensor_scalar(S[:, 9:10], S[:, 8:9], 1.0 / D, EPS, ALU.mult, ALU.add))
            self.wait("a", self.sV, r2v)
            self.op("a", lambda e: e.activation(S[:, 10:11], S[:, 9:10], AF.Ln))
            r2 = self.dop("a", lambda e: e.activation(S[:, 10:11], S[:, 10:11], AF.Exp, scale=-0.5))
            self.wait("v", self.sA, r2)
            h2t = self.dop("v", lambda e, b=b: e.tensor_scalar_mul(H2S[:, b, :], HT[:, b, :], S[:, 10:11]))
            self.p3_used = getattr(self, "p3_used", {})
            self.p3_used[mt] = h2t
            self.wait("y", self.sV, h2t)
            self.dma("y", h2v[mt], H2S[:, b, :], ctr=self.H2T[b])
            self.dma("y", xnv[mt], XNS[:, b, :], ctr=self.H2T[b])
        self.ev_p3_v = self.sV.n
        self.ev_p3_h2 = [(c, c.n) for c in self.H2T]
        if self.dbg:
            self.barrier()
            self.dbg_copy("h2n", self.h2nscr.ap(), [TOK, D], BF16)
            self.dbg_copy("xno", self.xnownscr.ap(), [TOK, D], BF16)
            t3 = self.nc.dram_tensor("dbg_out3", [128, 8, D], F32, kind="ExternalOutput").ap()
            self.dma("y", t3, self.OUT[:, :, :], ctr=self.STO)

    def p4a(self):
        A = self.ARE
        H2F = self.MID[:, 0:16384].rearrange("p (k m) -> p k m", k=16)
        X2Q = A[:, 0:16384].rearrange("p (k m) -> p k m", k=16)
        WBK = A[:, 16384:32768].rearrange("p (s w k m) -> p s w k m", s=2, w=2, k=16)
        SG = A[:, 36864:38912].rearrange("p (b n) -> p b n", b=2)
        self.dr_used = 0
        self.dq_mm_g = 0
        self.pA_ev = {}
        for c, nn_ in self.ev_p3_h2:
            self.wait("y", c, nn_)
        for gc in self.GBL:
            self.wait("y", gc, gc.n)
        h2l = 0
        for k in range(16):
            h2l = self.dma("y", H2F[:, k, :], self.h2nscr.ap()[:, k * 128:(k + 1) * 128],
                           transpose=True, ctr=self.H2L)
        self.wait("v", self.H2L, h2l)
        self.wait("v", self.sV, self.ev_p3_v)
        for k in range(16):
            self.op("v", lambda e, k=k: e.tensor_scalar_mul(H2F[:, k, :], H2F[:, k, :], self.LN2W[:, k:k + 1]))
        h2s_v = self.sV.n
        self.wait("t", self.sV, h2s_v)
        gv = self.gate_w.rearrange("(k p) i -> p k i", p=128)
        uv = self.up_w.rearrange("(k p) i -> p k i", p=128)
        dv = self.downAbf.ap().rearrange("(k p) o -> p k o", p=128)
        self.wait("g", self.WCST, self.cast_dn2)
        if "h2f" in self.dbg:
            self.barrier()
            th = self.nc.dram_tensor("dbg_h2f", [128, 16, TOK], BF16, kind="ExternalOutput").ap()
            self.dma("y", th, H2F[:, :, :], ctr=self.STO)
        self.wait("g", self.sV, h2s_v)
        self._swiglu_common(H2F, X2Q, WBK, SG, gv, uv, dv, su=False)
        if "x2q" in self.dbg:
            self.barrier()
            tx = self.nc.dram_tensor("dbg_x2q", [128, 16, TOK], BF16, kind="ExternalOutput").ap()
            self.dma("y", tx, X2Q[:, :, :], ctr=self.STO)
        self.ev_p4a = self.sV.n
        self.ev_p4a_mm = self.sP.n
        if "out4a" in self.dbg:
            self.barrier()
            t4 = self.nc.dram_tensor("dbg_out4a", [128, 8, D], F32, kind="ExternalOutput").ap()
            self.dma("y", t4, self.OUT[:, :, :], ctr=self.STO)

    def _swiglu_common(self, H2F, X2Q, WBK, SG, gv, uv, dv, su):
        A = self.ARE
        wb_used = {}
        ev_pA = 0
        dq_mm = 0
        for q in range(4):
            x2_done = 0
            for ib in range(8):
                i0 = q * 16 + ib * 2
                slot = ib % 2
                if wb_used.get(slot, 0):
                    self.wait("g", self.sP, wb_used[slot])
                if ib < 2 and dq_mm:
                    self.wait("g", self.sP, dq_mm)
                tg = self.dma("g", WBK[:, slot, 0, :, :], gv[:, :, i0 * 128:(i0 + 2) * 128], ctr=self.WBL[slot])
                tu = self.dma("g", WBK[:, slot, 1, :, :], uv[:, :, i0 * 128:(i0 + 2) * 128], ctr=self.WBL[slot])
                self.wait("t", self.WBL[slot], tu)
                for ii in range(2):
                    if ev_pA:
                        self.wait("t", self.sV, ev_pA)
                    for w in range(2):
                        for n2 in range(2):
                            for k in range(16):
                                mm = lambda e, w=w, ii=ii, n2=n2, k=k, slot=slot: e.matmul(
                                    self.pA[:, (w * 2 + n2) * 512:(w * 2 + n2 + 1) * 512],
                                    WBK[:, slot, w, k, ii * 128:(ii + 1) * 128],
                                    H2F[:, k, n2 * 512:(n2 + 1) * 512],
                                    start=(k == 0), stop=(k == 15))
                                if k < 15 or w == 0:
                                    self.em("t", mm)
                                else:
                                    mmt = self.op("t", mm)
                    self.wait("v", self.sP, mmt)
                    sg_t = self.dop("v", lambda e: e.tensor_copy(SG[:, 0, :], self.pA[:, 0:1024]))
                    self.wait("a", self.sV, sg_t)
                    sl_t = self.op("a", lambda e: e.activation(SG[:, 1, :], SG[:, 0, :], AF.Silu))
                    self.wait("v", self.sA, sl_t)
                    x2_done = self.dop("v", lambda e, i=ib * 2 + ii: e.tensor_tensor(
                        X2Q[:, i, :], SG[:, 1, :], self.pA[:, 1024:2048], ALU.mult))
                    ev_pA = x2_done
                wb_used[slot] = self.sP.n
            q_mm = self.sP.n
            self.wait("t", self.sV, x2_done)
            DR8v = self.ARE[:, 22528:38912].rearrange("p (k m) -> p k m", k=8)
            for hf in range(2):
                if hf == 0:
                    self.wait("g", self.sP, q_mm)
                    self.wait("g", self.sV, x2_done)
                if self.dr_used:
                    self.wait("g", self.sP, self.dq_mm_g)
                drt = self.dma("g", DR8v[:, :, :], dv[:, q * 16 + hf * 8:q * 16 + (hf + 1) * 8, :], ctr=self.DRL[0])
                self.wait("t", self.DRL[0], drt)
                for gg in range(8):
                    pend = []
                    for j in range(4):
                        pidx = gg * 4 + j
                        m, n = pidx // 4, pidx % 4
                        if self.pA_ev.get(j, 0):
                            self.wait("t", self.sV, self.pA_ev[j])
                        for k in range(8):
                            mm = lambda e, j=j, m=m, n=n, k=k, hf=hf: e.matmul(
                                self.pA[:, j * 512:(j + 1) * 512],
                                X2Q[:, hf * 8 + k, m * 128:(m + 1) * 128],
                                DR8v[:, k, n * 512:(n + 1) * 512],
                                start=(k == 0), stop=(k == 7))
                            if k < 7:
                                self.em("t", mm)
                            else:
                                mmt = self.op("t", mm)
                        pend.append((j, m, n, mmt))
                    for j, m, n, mmt in pend:
                        self.wait("v", self.sP, mmt)
                        ev = self.dop("v", lambda e, j=j, m=m, n=n: e.tensor_tensor(
                            self.OUT[:, m, n * 512:(n + 1) * 512], self.OUT[:, m, n * 512:(n + 1) * 512],
                            self.pA[:, j * 512:(j + 1) * 512], ALU.add))
                        self.pA_ev[j] = ev
                self.dr_used = self.sV.n
                self.dq_mm_g = self.sP.n
                dq_mm = self.sP.n

    def p4b(self):
        A = self.ARE
        XNF = self.MID[:, 0:16384].rearrange("p (k m) -> p k m", k=16)
        SG = A[:, 36864:38912].rearrange("p (b n) -> p b n", b=2)
        EVB = A[:, 32768:36864].rearrange("p (b n) -> p b n", b=2)
        for c, nn_ in self.ev_p3_h2:
            self.wait("y", c, nn_)
        self.wait("y", self.sV, self.ev_p4a)
        self.wait("y", self.sP, self.ev_p4a_mm)
        xl = 0
        for k in range(16):
            xl = self.dma("y", XNF[:, k, :], self.xnownscr.ap()[:, k * 128:(k + 1) * 128],
                          transpose=True, ctr=self.H2L)
        self.wait("v", self.H2L, xl)
        self.wait("v", self.sV, self.ev_p4a)
        for k in range(16):
            self.op("v", lambda e, k=k: e.tensor_scalar_mul(XNF[:, k, :], XNF[:, k, :], self.SUU[:, k:k + 1]))
        xh_v = self.sV.n
        self.wait("t", self.sV, xh_v)
        s1v = self.s1scr.ap().rearrange("(a o) t -> o a t", o=16)
        s2v = self.s2scr.ap().rearrange("(o j u) t -> j o u t", o=16, j=16, u=8)
        pyev = 0
        for k in range(16):
            if pyev:
                self.wait("t", self.sV, pyev)
            for n2 in range(2):
                mm = lambda e, k=k, n2=n2: e.matmul(
                    self.pY[:, n2 * 512:(n2 + 1) * 512], self.H128[:, :],
                    XNF[:, k, n2 * 512:(n2 + 1) * 512], start=True, stop=True)
                if n2 == 0:
                    self.em("t", mm)
                else:
                    mmt = self.op("t", mm)
            self.wait("v", self.sP, mmt)
            eb = k % 2
            if k >= 2:
                self.wait("v", self.SH1[eb], self.sh1_t.get(k - 2, 0))
            ev = self.dop("v", lambda e, eb=eb: e.tensor_copy(EVB[:, eb, 0:1024], self.pY[:, :]))
            pyev = ev
            self.wait("y", self.sV, ev)
            self.sh1_t = getattr(self, "sh1_t", {})
            self.sh1_t[k] = self.dma("y", s1v[k], EVB[:, eb, 0:1024], ctr=self.SH1[eb])
        ld_t = {}
        for k in range(16):
            eb = k % 2
            if k >= 2:
                self.wait("y", self.sP, self.sh2_mm.get(k - 2, 0))
            if k < 2:
                for kk in range(16):
                    self.wait("y", self.SH1[kk % 2], self.sh1_t[kk])
            if k >= 1:
                self.wait("y", self.SH1[(k - 1) % 2], self.sh3_t[k - 1])
            ld_t[k] = self.dma("y", EVB[:, eb, 0:1024], self.s1scr.ap()[k * 128:(k + 1) * 128, :], ctr=self.SH2[eb])
            self.wait("t", self.SH2[eb], ld_t[k])
            if pyev:
                self.wait("t", self.sV, pyev)
            for n2 in range(2):
                mm = lambda e, eb=eb, n2=n2: e.matmul(
                    self.pY[:, n2 * 512:(n2 + 1) * 512], self.BD16[:, :],
                    EVB[:, eb, n2 * 512:(n2 + 1) * 512], start=True, stop=True)
                if n2 == 0:
                    self.em("t", mm)
                else:
                    mmt = self.op("t", mm)
            self.sh2_mm = getattr(self, "sh2_mm", {})
            self.sh2_mm[k] = mmt
            self.wait("v", self.sP, mmt)
            ev = self.dop("v", lambda e, eb=eb: e.tensor_copy(EVB[:, 1 - eb, 0:1024], self.pY[:, :]))
            pyev = ev
            self.wait("y", self.sV, ev)
            self.sh3_t = getattr(self, "sh3_t", {})
            self.sh3_t[k] = self.dma("y", s2v[k], EVB[:, 1 - eb, 0:1024], ctr=self.SH1[eb])
        for kk in range(16):
            self.wait("y", self.SH1[kk % 2], self.sh3_t[kk])
        x3l = self.dma("y", XNF[:, :, :], self.s2scr.ap().rearrange("(k p) t -> p k t", p=128), ctr=self.H2L)
        self.wait("t", self.H2L, x3l)
        if "xh3" in self.dbg:
            self.barrier()
            self.dbg_copy("xh3", self.s2scr.ap(), [D, TOK], BF16)
        self.wait("g", self.sV, self.ev_p4a)
        self.wait("g", self.sP, self.ev_p4a_mm)
        self.wait("v", self.PR2, self.ev_pr2)
        self.op("v", lambda e: e.tensor_scalar_mul(self.SUD[:, :], self.SUD[:, :], 1.0 / np.sqrt(8192.0)))
        WB2 = A[:, 6144:22528].rearrange("p (s w k m) -> p s w k m", s=2, w=2, k=16)
        s3v = self.s3scr.ap().rearrange("(a o) t -> o a t", o=64)
        upv = self.wupTI.rearrange("(k p) i -> p k i", p=128)
        gvv = self.wgateTI.rearrange("(k p) i -> p k i", p=128)
        wb_used = {}
        pAev = 0
        sh4 = {}
        for it in range(64):
            ib, ii = it // 2, it % 2
            slot = ib % 2
            if ii == 0:
                if wb_used.get(slot, 0):
                    self.wait("g", self.sP, wb_used[slot])
                tg = self.dma("g", WB2[:, slot, 0, :, :], upv[:, :, ib * 256:(ib + 1) * 256], ctr=self.WBL[slot])
                tu = self.dma("g", WB2[:, slot, 1, :, :], gvv[:, :, ib * 256:(ib + 1) * 256], ctr=self.WBL[slot])
                self.wait("t", self.WBL[slot], tu)
            if pAev:
                self.wait("t", self.sV, pAev)
            for w in range(2):
                for n2 in range(2):
                    for k in range(16):
                        mm = lambda e, w=w, ii=ii, n2=n2, k=k, slot=slot: e.matmul(
                            self.pA[:, (w * 2 + n2) * 512:(w * 2 + n2 + 1) * 512],
                            WB2[:, slot, w, k, ii * 128:(ii + 1) * 128],
                            XNF[:, k, n2 * 512:(n2 + 1) * 512],
                            start=(k == 0), stop=(k == 15))
                        if k < 15 or w == 0:
                            self.em("t", mm)
                        else:
                            mmt = self.op("t", mm)
            if ii == 1:
                wb_used[slot] = self.sP.n
            self.wait("v", self.sP, mmt)
            ut = self.dop("v", lambda e, it=it: e.tensor_scalar_mul(SG[:, 0, :], self.pA[:, 0:1024], self.SU1[:, it:it + 1]))
            gt = self.op("v", lambda e, it=it: e.tensor_scalar_mul(SG[:, 1, :], self.pA[:, 1024:2048], self.SU2[:, it:it + 1]))
            pAev = gt
            self.wait("a", self.sV, gt)
            st = self.op("a", lambda e: e.activation(SG[:, 1, :], SG[:, 1, :], AF.Silu))
            self.wait("v", self.sA, st)
            x2t = self.dop("v", lambda e, it=it: e.scalar_tensor_tensor(
                SG[:, 0, :], SG[:, 1, :], self.SUD[:, it:it + 1], SG[:, 0, :], ALU.mult, ALU.mult))
            self.wait("t", self.sV, x2t)
            if sh4.get("pyev", 0):
                self.wait("t", self.sV, sh4["pyev"])
            for n2 in range(2):
                mm = lambda e, n2=n2: e.matmul(
                    self.pY[:, n2 * 512:(n2 + 1) * 512], self.H128[:, :],
                    SG[:, 0, n2 * 512:(n2 + 1) * 512], start=True, stop=True)
                if n2 == 0:
                    self.em("t", mm)
                else:
                    p1mm = self.op("t", mm)
            self.wait("v", self.sP, p1mm)
            eb = it % 2
            if it >= 2:
                self.wait("v", self.SH1[eb], sh4.get(it - 2, 0))
            ev = self.dop("v", lambda e, eb=eb: e.tensor_copy(EVB[:, eb, 0:1024], self.pY[:, :]))
            sh4["pyev"] = ev
            self.wait("y", self.sV, ev)
            sh4[it] = self.dma("y", s3v[it], EVB[:, eb, 0:1024], ctr=self.SH1[eb])
        s4v = self.s4scr.ap().rearrange("(o j u) t -> j o u t", o=64, j=64, u=2)
        pyev = sh4["pyev"]
        sh5 = {}
        for k in range(64):
            eb = k % 2
            if k < 2:
                for kk in range(64):
                    self.wait("y", self.SH1[kk % 2], sh4[kk])
            if k >= 2:
                self.wait("y", self.sP, sh5.get(("mm", k - 2), 0))
            if k >= 1:
                self.wait("y", self.SH1[(k - 1) % 2], sh5[k - 1])
            lt = self.dma("y", EVB[:, eb, 0:1024], self.s3scr.ap()[k * 128:(k + 1) * 128, :], ctr=self.SH2[eb])
            self.wait("t", self.SH2[eb], lt)
            if pyev:
                self.wait("t", self.sV, pyev)
            for n2 in range(2):
                mm = lambda e, eb=eb, n2=n2: e.matmul(
                    self.pY[:, n2 * 512:(n2 + 1) * 512], self.BD64[:, :],
                    EVB[:, eb, n2 * 512:(n2 + 1) * 512], start=True, stop=True)
                if n2 == 0:
                    self.em("t", mm)
                else:
                    mmt = self.op("t", mm)
            sh5[("mm", k)] = mmt
            self.wait("v", self.sP, mmt)
            ev = self.dop("v", lambda e, eb=eb: e.tensor_copy(EVB[:, 1 - eb, 0:1024], self.pY[:, :]))
            pyev = ev
            self.wait("y", self.sV, ev)
            sh5[k] = self.dma("y", s4v[k], EVB[:, 1 - eb, 0:1024], ctr=self.SH1[eb])
        if "x3f" in self.dbg:
            self.barrier()
            self.dbg_copy("x3f", self.s4scr.ap(), [8192, TOK], BF16)
        X3H = [self.MID[:, 0:8192].rearrange("p (k m) -> p k m", k=8),
               self.MID[:, 8192:16384].rearrange("p (k m) -> p k m", k=8)]
        DR8b = [A[:, 22528:38912].rearrange("p (k m) -> p k m", k=8),
                A[:, 6144:22528].rearrange("p (k m) -> p k m", k=8)]
        SDB = A[:, 0:4096].bitcast(F32)
        TSF = A[:, 4096:6144].bitcast(F32)
        sdl = self.dma("y", SDB[:, :], self.sdbigI, ctr=self.H2L)
        dnv = self.wdownTbf.ap().rearrange("(k p) o -> p k o", p=128)
        s4r = self.s4scr.ap().rearrange("(k p) t -> p k t", p=128)
        pre_sp = self.sP.n
        pre_sv = self.sV.n
        self.wait("g", self.WCST, self.cast_dn1)
        self.wait("g", self.sP, pre_sp)
        self.wait("g", self.sV, pre_sv)
        for c in self.SH1:
            self.wait("g", c, c.n)
        self.wait("y", self.sP, pre_sp)
        for kk in range(64):
            self.wait("y", self.SH1[kk % 2], sh5[kk])
        self.wait("v", self.H2L, sdl)
        ldd = {}
        ldx = {}
        mmd = {}
        ldd[0] = self.dma("g", DR8b[0][:, :, :], dnv[:, 0:8, :], ctr=self.DRL2)
        ldx[0] = self.dma("y", X3H[0][:, :, :], s4r[:, 0:8, :], ctr=self.H2L)
        for t in range(8):
            b = t % 2
            if t + 1 < 8:
                if t >= 1:
                    self.wait("g", self.sP, mmd[t - 1])
                    self.wait("y", self.sP, mmd[t - 1])
                ldd[t + 1] = self.dma("g", DR8b[1 - b][:, :, :], dnv[:, (t + 1) * 8:(t + 2) * 8, :], ctr=self.DRL2)
                ldx[t + 1] = self.dma("y", X3H[1 - b][:, :, :], s4r[:, (t + 1) * 8:(t + 2) * 8, :], ctr=self.H2L)
            self.wait("t", self.DRL2, ldd[t])
            self.wait("t", self.H2L, ldx[t])
            for gg in range(8):
                pend = []
                for j in range(4):
                    pidx = gg * 4 + j
                    m, n = pidx // 4, pidx % 4
                    if self.pA_ev.get(j, 0):
                        self.wait("t", self.sV, self.pA_ev[j])
                    for k in range(8):
                        mm = lambda e, j=j, m=m, n=n, k=k, b=b: e.matmul(
                            self.pA[:, j * 512:(j + 1) * 512],
                            X3H[b][:, k, m * 128:(m + 1) * 128],
                            DR8b[b][:, k, n * 512:(n + 1) * 512],
                            start=(k == 0), stop=(k == 7))
                        if k < 7:
                            self.em("t", mm)
                        else:
                            mmt = self.op("t", mm)
                    pend.append((j, m, n, mmt))
                for j, m, n, mmt in pend:
                    self.wait("v", self.sP, mmt)
                    t1 = self.dop("v", lambda e, j=j, n=n: e.tensor_tensor(
                        TSF[:, 0:512], self.pA[:, j * 512:(j + 1) * 512],
                        SDB[:, n * 512:(n + 1) * 512], ALU.mult))
                    ev = self.dop("v", lambda e, m=m, n=n: e.tensor_tensor(
                        self.OUT[:, m, n * 512:(n + 1) * 512], self.OUT[:, m, n * 512:(n + 1) * 512],
                        TSF[:, 0:512], ALU.add))
                    self.pA_ev[j] = ev
            mmd[t] = self.sP.n
        self.ev_p4b = self.sV.n

    def pfin(self):
        ov = self.outT.rearrange("(n p) d -> n p d", p=128)
        self.wait("y", self.sV, self.ev_p4b)
        for mt in range(8):
            self.dma("y", ov[mt], self.OUT[:, mt, :], ctr=self.FIN)
        self.barrier()

    def emit(self, ctx):
        block = ctx.enter_context(self.nc.Block())
        block.gpsimd(lambda e: [f(e) for f in self.ops["g"]])
        block.vector(lambda e: [f(e) for f in self.ops["v"]])
        block.scalar(lambda e: [f(e) for f in self.ops["a"]])
        block.tensor(lambda e: [f(e) for f in self.ops["t"]])
        block.sync(lambda e: [f(e) for f in self.ops["y"]])

    def finalize_dev(self):
        self.dma("g", self.outT, self.xrscr.ap()[0:TOK, :], ctr=self.FIN)
        self.barrier()


def make_consts():
    h128 = had(128)
    def permkron(O, U):
        hh = had(O)
        m = np.zeros((O * U, O * U), np.float32)
        for u in range(U):
            for o in range(O):
                for o2 in range(O):
                    m[u * O + o, o2 * U + u] = hh[o, o2]
        return m
    bd16 = permkron(16, 8)
    bd64 = permkron(64, 2)
    ident = np.eye(128, dtype=np.float32)
    ut = np.triu(np.ones((128, 128), np.float32))
    negm = np.where(ut > 0, 0.0, -1e30).astype(np.float32)
    icol32 = np.eye(32, dtype=np.float32)
    hsel = np.zeros((16, 2048), np.float32)
    for j in range(16):
        hsel[j, j * 128:(j + 1) * 128] = 1.0
    return h128, bd16, bd64, ident, ut, negm, icol32, hsel


def prep_inputs(inputs):
    x = inputs["L0_attn_hidden_states"]
    w_in = inputs["L0_attn_in_proj_w"]
    conv_w = inputs["L0_attn_conv_w"][:, 0, :]
    conv_b = inputs["L0_attn_conv_b"]
    dt_bias = inputs["L0_attn_dt_bias"]
    A_log = inputs["L0_attn_A_log"]
    Dv = inputs["L0_attn_D"]
    gnorm_w = inputs["L0_attn_gnorm_w"]
    out_proj = inputs["L0_attn_out_proj_w"]
    ln1 = inputs["L0_attn_ln1_w"]
    ln2 = inputs["L0_attn_ln2_w"]
    h128, bd16, bd64, ident, ut, negm, icol32, hsel = make_consts()
    f32 = np.float32
    com = {
        "gate_w": np.ascontiguousarray(inputs["L0_attn_gate_w"]),
        "up_w": np.ascontiguousarray(inputs["L0_attn_up_w"]),
        "down_w": np.ascontiguousarray(inputs["L0_attn_down_w"]),
        "wupTI": np.ascontiguousarray(inputs["L0_ffn_W_up"].T),
        "wgateTI": np.ascontiguousarray(inputs["L0_ffn_W_gate"].T),
        "wdownTI": np.ascontiguousarray(inputs["L0_ffn_W_down"].T),
        "su_ugI": np.ascontiguousarray(inputs["L0_ffn_SU_ug"].reshape(16, 128).T),
        "su_dpI": np.ascontiguousarray(inputs["L0_ffn_SU_dp"].reshape(64, 128).T),
        "su1I": np.ascontiguousarray(inputs["L0_ffn_Wscale_ug"][:8192].reshape(64, 128).T),
        "su2I": np.ascontiguousarray(inputs["L0_ffn_Wscale_ug"][8192:].reshape(64, 128).T),
        "sdbigI": np.ascontiguousarray(np.broadcast_to(inputs["L0_ffn_Wscale_dp"][None, :], (128, 2048))),
        "ln1wI": np.ascontiguousarray(ln1.reshape(16, 128).T),
        "ln2wI": np.ascontiguousarray(ln2.reshape(16, 128).T),

        "h128I": h128, "bd16I": bd16, "bd64I": bd64, "identI": ident,
        "utI": ut, "negmaskI": negm, "icol32I": icol32, "hsel128I": hsel,
    }
    com = {k: np.asarray(v, f32) for k, v in com.items()}
    maps = []
    for core in range(8):
        g, j = core // 4, core % 4
        w_in_slice = np.concatenate([
            w_in[:, j * 1024:(j + 1) * 1024],
            w_in[:, 4096 + j * 1024:4096 + (j + 1) * 1024],
            w_in[:, 8192:8448],
            w_in[:, 8448 + 16 * j:8448 + 16 * (j + 1)],
        ], axis=1)
        ch = np.concatenate([np.arange(j * 1024, (j + 1) * 1024),
                             np.arange(4096, 4352)])
        cw = conv_w[ch].reshape(10, 128, 4).transpose(1, 0, 2)
        cb = conv_b[ch].reshape(10, 128).T
        Ds = Dv[16 * j:16 * (j + 1)]
        dbig = np.empty((128, 8), f32)
        for jj in range(8):
            dbig[0:64, jj] = Ds[2 * jj]
            dbig[64:128, jj] = Ds[2 * jj + 1]
        m = dict(com)
        m.update({
            "xb": np.ascontiguousarray(x[g]),
            "x_own": np.ascontiguousarray(x[g, j * 1024:(j + 1) * 1024]),
            "w_in_s": np.ascontiguousarray(w_in_slice),
            "conv_w_s": np.ascontiguousarray(cw),
            "conv_b_s": np.ascontiguousarray(cb),
            "dtb": np.ascontiguousarray(dt_bias[16 * j:16 * (j + 1)].reshape(16, 1)),
            "alog": np.ascontiguousarray(A_log[16 * j:16 * (j + 1)].reshape(16, 1)),
            "dbigI": dbig,
            "wout_s": np.ascontiguousarray(out_proj[j * 1024:(j + 1) * 1024]),
            "gnwI": np.ascontiguousarray(gnorm_w[j * 1024:(j + 1) * 1024].reshape(8, 128).T),
        })
        maps.append({k: np.asarray(v, f32) for k, v in m.items()})
    return maps


def build(dbg=(), upto="full"):
    b = Build(dbg)
    b.declare()
    with ExitStack() as ctx:
        b.alloc(ctx)
        b.p0()
        b.p1()
        if upto in ("p2a", "p2b", "full"):
            b.p2a()
        if upto in ("p2b", "full"):
            b.p2prep()
            b.p2b()
        if upto == "full":
            b.p2c()
            b.p3()
            b.p4a()
            b.p4b()
            b.pfin()
        else:
            b.finalize_dev()
        b.emit(ctx)
    return b



def kernel(**inputs):
    maps = prep_inputs(inputs)
    b = build(dbg=())
    res = run_bass_kernel_spmd(b.nc, maps, core_ids=list(range(8)))
    outs = [np.asarray(res.results[c]["out"], np.float32) for c in range(8)]
    g0 = np.concatenate(outs[0:4], axis=0)
    g1 = np.concatenate(outs[4:8], axis=0)
    return np.stack([g0, g1], axis=0)
